# version 02
# baseline (speedup 1.0000x reference)
import numpy as np
import ml_dtypes
from contextlib import ExitStack

import concourse.bass as bass
import concourse.mybir as mybir
from concourse.bass_utils import run_bass_kernel_spmd
from concourse.bass import IndirectOffsetOnAxis

F32 = mybir.dt.float32
BF16 = mybir.dt.bfloat16
I32 = mybir.dt.int32
AF = mybir.ActivationFunctionType
ALU = mybir.AluOpType
AX = mybir.AxisListType

NCORES = 8
B = 2
S = 2048
C = 1024
NH = 16
HD = 64
T = B * S
E = 16
NL = 4
FF = 4096
CAP = 512
EXP_SHIFT = -32.0
SENT = 1 << 20



def _host_constants():
    pos = np.arange(S, dtype=np.float64)
    inv_freq = 1.0 / 10000.0 ** (np.arange(0, 2 * HD, 2, dtype=np.float64) / (2 * HD))
    freqs = np.outer(pos, inv_freq)
    cos0 = np.cos(freqs).astype(np.float32).T
    sin0 = np.sin(freqs).astype(np.float32).T
    cos2 = np.ascontiguousarray(np.concatenate([cos0, cos0], 0))
    sin2 = np.ascontiguousarray(np.concatenate([sin0, sin0], 0))
    ident = np.eye(128, dtype=ml_dtypes.bfloat16)
    u128 = np.triu(np.ones((128, 128), np.float32))
    iota16 = (1024.0 - np.broadcast_to(np.arange(16, dtype=np.float32), (128, 16))).copy()
    iota_tok = (np.arange(32)[None, :] * 128 + np.arange(128)[:, None]).astype(np.int32)
    sent = np.full((E, CAP), SENT, np.int32)
    zeros_tm = np.zeros((T, C), ml_dtypes.bfloat16)
    return dict(cos2=cos2, sin2=sin2, ident=ident, u128=u128, iota16=iota16,
                iota_tok=np.ascontiguousarray(iota_tok), sent=sent, zeros_tm=zeros_tm)


def _host_prep(inputs):
    x = np.asarray(inputs["L0_attn_x"], np.float32)
    Wqkv = np.asarray(inputs["L0_attn_W_qkv"], np.float32)
    Wproj = np.asarray(inputs["L0_attn_W_proj"], np.float32)
    WqkT = np.ascontiguousarray(Wqkv[:4 * C].T)
    j = np.arange(2 * C)
    hh = j // 128
    dv = j % 128
    src = (4 + dv // HD) * C + hh * HD + (dv % HD)
    Wv_rhs = np.ascontiguousarray(Wqkv[src].T)
    WprojT = np.ascontiguousarray(Wproj.T)

    shared = dict(WqkT=WqkT, Wv_rhs=Wv_rhs, WprojT=WprojT,
                  routers=np.asarray(inputs["L0_ffn_routers"], np.float32),
                  W1=np.asarray(inputs["L0_ffn_W1"], np.float32),
                  W2=np.asarray(inputs["L0_ffn_W2"], np.float32),
                  lq1=np.asarray(inputs["L0_attn_lambda_q1"], np.float32).reshape(1, HD),
                  lq2=np.asarray(inputs["L0_attn_lambda_q2"], np.float32).reshape(1, HD),
                  lk1=np.asarray(inputs["L0_attn_lambda_k1"], np.float32).reshape(1, HD),
                  lk2=np.asarray(inputs["L0_attn_lambda_k2"], np.float32).reshape(1, HD),
                  **_host_constants())
    in_maps = []
    for c in range(NCORES):
        m = dict(shared)
        m["x_fm"] = np.ascontiguousarray(x[2 * c:2 * c + 2].transpose(0, 2, 1))
        in_maps.append(m)
    return in_maps


class P:
    pass


P4_PARTS = "ab"


def build(debug_taps=(), phases=("p0", "p1", "p2", "p3", "p4", "p5")):
    p = P()
    nc = bass.Bass("TRN2", target_bir_lowering=False, detect_race_conditions=False)
    p.nc = nc
    p.taps = set(debug_taps)

    tmp_sems = []
    try:
        for _ in range(256):
            tmp_sems.append(nc.alloc_semaphore())
    except Exception:
        pass
    with nc.Block() as _blk0:
        @_blk0.gpsimd
        def _(g):
            g.engine_nop()
            nc.clear_and_free_semaphores(tmp_sems)

    def din(name, shape, dt):
        return nc.dram_tensor(name, shape, dt, kind="ExternalInput").ap()
    p.x_fm = din("x_fm", [B, C, S], F32)
    p.WqkT = din("WqkT", [C, 4 * C], F32)
    p.Wv_rhs = din("Wv_rhs", [C, 2 * C], F32)
    p.WprojT = din("WprojT", [2 * C, C], F32)
    p.routers_d = din("routers", [NL, C, E], F32)
    p.W1_d = din("W1", [E, C, FF], F32)
    p.W2_d = din("W2", [E, FF, C], F32)
    p.lq1 = din("lq1", [1, HD], F32)
    p.lq2 = din("lq2", [1, HD], F32)
    p.lk1 = din("lk1", [1, HD], F32)
    p.lk2 = din("lk2", [1, HD], F32)
    p.cos2 = din("cos2", [128, S], F32)
    p.sin2 = din("sin2", [128, S], F32)
    p.ident_in = din("ident", [128, 128], BF16)
    p.u128_in = din("u128", [128, 128], F32)
    p.iota16_in = din("iota16", [128, 16], F32)
    p.iotatok_in = din("iota_tok", [128, 32], I32)
    p.sent_in = din("sent", [E, CAP], I32)
    p.zeros_in = din("zeros_tm", [T, C], BF16)

    p.out_fm = nc.dram_tensor("out_fm", [B, C, S], F32, kind="ExternalOutput").ap()
    p.xn_tm = nc.dram_tensor("xn_tm", [T, C], BF16).ap()
    p.xt_tm = [nc.dram_tensor(f"xt{i}_tm", [T, C], BF16).ap() for i in range(NL)]
    p.attn_fm = nc.dram_tensor("attn_fm", [B, 2 * C, S], BF16).ap()
    p.attnproj = nc.dram_tensor("attnproj", [B, C, S], F32).ap()
    p.glist = nc.dram_tensor("glist", [E, CAP], I32).ap()
    p.wlist = nc.dram_tensor("wlist", [E, CAP], F32).ap()

    def tap(name, shape, dt):
        if name in p.taps:
            return nc.dram_tensor("tap_" + name, shape, dt, kind="ExternalOutput").ap()
        return None
    p.tap_xn = tap("xn", [T, C], BF16)
    p.tap_glist = tap("glist", [NL, E, CAP], I32)
    p.tap_wlist = tap("wlist" if "glist" not in p.taps else "glist_w", [NL, E, CAP], F32)
    if "glist" in p.taps and p.tap_wlist is None:
        p.tap_wlist = nc.dram_tensor("tap_wlist", [NL, E, CAP], F32, kind="ExternalOutput").ap()
    p.tap_xt = [tap(f"xt{i}", [T, C], BF16) for i in range(NL)]
    p.tap_attn = tap("attn", [B, 2 * C, S], BF16)
    p.tap_attnproj = tap("attnproj", [B, C, S], F32)
    p.tap_route = tap("route", [32, 128, 4], F32)
    p.tap_oh = tap("oh", [32, 128, 16], F32)
    p.tap_logits = tap("logits", [32, 128, 16], F32)
    p.tap_xfm = tap("xfm", [32, 128, C], BF16)
    p.tap_xstrip = tap("xstrip", [32, 128, C], BF16)
    p.tap_desti = tap("desti", [32, 128], I32)
    p.tap_stats = tap("stats", [6, 128, 16], F32)

    with ExitStack() as ctx:
        p.ctx = ctx
        p.sb = lambda name, shape, dt: ctx.enter_context(nc.sbuf_tensor(name, shape, dt))
        p.ps = lambda name, shape, dt=F32: ctx.enter_context(nc.psum_tensor(name, shape, dt))
        p.sem = lambda name: ctx.enter_context(nc.semaphore(name))

        p.s_vf = p.sem("dve_fence")
        p.vf_n = [0]

        def vfence(v, ins):
            p.vf_n[0] += 1
            ins.then_inc(p.s_vf, 1)
            v.wait_ge(p.s_vf, p.vf_n[0])
        p.vfence = vfence
        p.s_af = p.sem("act_fence")
        p.af_n = [0]

        def afence(sc, ins):
            p.af_n[0] += 1
            ins.then_inc(p.s_af, 1)
            sc.wait_ge(p.s_af, p.af_n[0])
        p.afence = afence

        sb = p.sb
        p.guard = sb("guardpad", [128, 3072], BF16)
        p.ident = sb("identsb", [128, 128], BF16)
        p.u128 = sb("u128sb", [128, 128], F32)
        p.iota16 = sb("iota16sb", [128, 16], F32)
        p.iotatok = sb("iotatoksb", [128, 32], I32)
        p.onesb = sb("onesbsb", [128, 1], BF16)
        p.onesf = sb("onesfsb", [128, 1], F32)
        p.onesrow = sb("onesrowsb", [1, 128], F32)
        p.biasC = sb("biasCsb", [128, 1], F32)
        p.lamcol = sb("lamcolsb", [128, 1], F32)
        p.lam11 = sb("lam11sb", [1, 1], F32)
        actx = ExitStack()
        p.xnf = [actx.enter_context(nc.sbuf_tensor(f"xnf{c}", [128, T], BF16))
                 for c in range(8)]
        p.cos = actx.enter_context(nc.sbuf_tensor("cossb", [128, S], F32))
        p.sin = actx.enter_context(nc.sbuf_tensor("sinsb", [128, S], F32))

        if "p0" in phases:
            build_p0(p)
        if "p1" in phases:
            build_p1(p)
        if "p2" in phases:
            build_p2(p)
        if "p3" in phases:
            build_p3(p)
        actx.close()
        if "p4" in phases:
            build_p4(p)
        if "p5" in phases:
            build_p5(p)
    return nc


def build_p0(p):
    nc, sem = p.nc, p.sem
    from contextlib import ExitStack
    pctx = ExitStack()
    sb = lambda name, shape, dt: pctx.enter_context(nc.sbuf_tensor(name, shape, dt))
    ps = lambda name, shape, dt=F32: pctx.enter_context(nc.psum_tensor(name, shape, dt))
    p._p0ctx = pctx
    lt = [sb(f"p0_lam{i}", [1, HD], F32) for i in range(4)]
    e12 = sb("p0_e12", [1, 2], F32)
    pb = ps("p0_bc", [128, 1], F32)
    s_ld = sem("p0_ld")
    s_c = sem("p0_c")

    with nc.Block() as blk:
        @blk.sync
        def _(sync):
            sync.dma_start(p.ident[:], p.ident_in).then_inc(s_ld, 16)
            sync.dma_start(p.u128[:], p.u128_in).then_inc(s_ld, 16)
            sync.dma_start(p.iota16[:], p.iota16_in).then_inc(s_ld, 16)
            sync.dma_start(p.iotatok[:], p.iotatok_in).then_inc(s_ld, 16)
            sync.dma_start(p.cos[:], p.cos2).then_inc(s_ld, 16)
            sync.dma_start(p.sin[:], p.sin2).then_inc(s_ld, 16)
            for i, src in enumerate([p.lq1, p.lk1, p.lq2, p.lk2]):
                sync.dma_start(lt[i][:], src).then_inc(s_ld, 16)

        @blk.vector
        def _(v):
            v.memset(p.onesb[:], 1.0)
            v.memset(p.onesf[:], 1.0)
            v.memset(p.onesrow[:], 1.0)
            v.memset(p.biasC[:], EXP_SHIFT)
            v.wait_ge(s_ld, 160)
            v.tensor_mul(lt[0][:], lt[0][:], lt[1][:])
            p.vfence(v, v.tensor_mul(lt[2][:], lt[2][:], lt[3][:]))
            v.reduce_sum(e12[:, 0:1], lt[0][:], AX.X)
            p.vfence(v, v.reduce_sum(e12[:, 1:2], lt[2][:], AX.X))
            v.engine_nop().then_inc(s_c, 1)

        @blk.scalar
        def _(s):
            s.wait_ge(s_c, 1)
            p.afence(s, s.activation(e12[:], e12[:], AF.Exp))
            s.nop().then_inc(s_c, 1)

        @blk.vector
        def _(v2):
            v2.wait_ge(s_c, 2)
            p.vfence(v2, v2.tensor_sub(p.lam11[:], e12[:, 0:1], e12[:, 1:2]))
            v2.tensor_scalar(p.lam11[:], p.lam11[:], 0.2, None, ALU.add).then_inc(s_c, 1)

        @blk.tensor
        def _(t):
            t.wait_ge(s_c, 3)
            t.matmul(pb[:], p.onesrow[:], p.lam11[:], start=True, stop=True).then_inc(s_c, 1)

        @blk.vector
        def _(v3):
            v3.wait_ge(s_c, 4)
            v3.tensor_copy(p.lamcol[:], pb[:])
    pctx.close()


def build_p1(p):
    nc, sem = p.nc, p.sem
    from contextlib import ExitStack
    pctx = ExitStack()
    sb = lambda name, shape, dt: pctx.enter_context(nc.sbuf_tensor(name, shape, dt))
    ps = lambda name, shape, dt=F32: pctx.enter_context(nc.psum_tensor(name, shape, dt))
    NG = 8
    xld = [[sb(f"p1_x{i}_{c}", [128, 512], F32) for c in range(8)] for i in range(2)]
    xbf = [sb(f"p1_xb{c}", [128, 512], BF16) for c in range(8)]
    sqb = [sb(f"p1_sq{c}", [128, 512], BF16) for c in range(8)]
    stat = sb("p1_stat", [1, 512 * 5], F32)
    mean, ex2, var, srt, rstd = (stat[:, i * 512:(i + 1) * 512] for i in range(5))
    xtmb = [sb(f"p1_xtm{i}", [128, C], BF16) for i in range(2)]
    pA = ps("p1_A", [1, 512])
    pB = ps("p1_B", [1, 512])
    pC = ps("p1_C", [128, 512])
    pD = ps("p1_D", [128, 512])
    pE = [ps(f"p1_E{i}", [128, 512], BF16) for i in range(4)]

    s_ld = sem("p1_ld")
    s_cast = sem("p1_cast")
    s_sq = sem("p1_sq")
    s_pe = sem("p1_pe")
    s_stat = sem("p1_stat")
    s_sc = sem("p1_sc")
    s_v2 = sem("p1_v2")
    s_bc = sem("p1_bc")
    s_apl = sem("p1_apl")
    s_ptr = sem("p1_ptr")
    s_tev = sem("p1_tev")
    s_xtm = sem("p1_xtm")
    XM = 32 if p.tap_xn is not None else 16

    def tcol_of(g):
        t0 = (g // 4) * 2048 + (g % 4) * 512
        return slice(t0, t0 + 512)

    with nc.Block() as blk:
        @blk.sync
        def _(sync):
            for g in range(NG):
                b, tg = g // 4, g % 4
                if g >= 2:
                    sync.wait_ge(s_apl, g - 1)
                for c in range(8):
                    sync.dma_start(xld[g % 2][c][:],
                                   p.x_fm[b, c * 128:(c + 1) * 128, tg * 512:(tg + 1) * 512]
                                   ).then_inc(s_ld, 16)

        @blk.vector
        def _(v):
            for g in range(NG):
                v.wait_ge(s_ld, 16 * 8 * (g + 1))
                if g >= 1:
                    v.wait_ge(s_pe, 2 * g - 1)
                for c in range(8):
                    ins = v.tensor_copy(xbf[c][:], xld[g % 2][c][:])
                p.vfence(v, ins)
                v.engine_nop().then_inc(s_cast, 1)
                v.wait_ge(s_pe, 2 * g + 2)
                v.tensor_scalar(mean, pA[:], 1.0 / C, None, ALU.mult)
                p.vfence(v, v.tensor_scalar(ex2, pB[:], 1.0 / C, None, ALU.mult))
                p.vfence(v, v.tensor_mul(var, mean, mean))
                p.vfence(v, v.tensor_sub(var, ex2, var))
                p.vfence(v, v.tensor_scalar(var, var, 1e-5, None, ALU.add))
                v.engine_nop().then_inc(s_stat, 1)
                v.wait_ge(s_sc, g + 1)
                p.vfence(v, v.reciprocal(rstd, srt))
                v.engine_nop().then_inc(s_v2, 1)
                v.wait_ge(s_bc, g + 1)
                tcol = tcol_of(g)
                for c in range(8):
                    v.tensor_sub(xld[g % 2][c][:], xld[g % 2][c][:], pC[:])
                    ins = v.tensor_mul(p.xnf[c][:, tcol], xld[g % 2][c][:], pD[:])
                p.vfence(v, ins)
                v.engine_nop().then_inc(s_apl, 1)
                for sub in range(4):
                    gs = 4 * g + sub
                    pr = 2 * (gs % 2)
                    v.wait_ge(s_ptr, gs + 1)
                    if gs >= 2:
                        v.wait_ge(s_xtm, XM * (gs - 1))
                    v.tensor_copy(xtmb[gs % 2][:, 0:512], pE[pr][:])
                    p.vfence(v, v.tensor_copy(xtmb[gs % 2][:, 512:1024], pE[pr + 1][:]))
                    v.engine_nop().then_inc(s_tev, 1)

        @blk.scalar
        def _(s):
            for g in range(NG):
                s.wait_ge(s_ld, 16 * 8 * (g + 1))
                if g >= 1:
                    s.wait_ge(s_pe, 2 * g)
                for c in range(8):
                    ins = s.activation(sqb[c][:], xld[g % 2][c][:], AF.Square)
                p.afence(s, ins)
                s.nop().then_inc(s_sq, 1)
                s.wait_ge(s_stat, g + 1)
                p.afence(s, s.activation(srt, var, AF.Sqrt))
                s.nop().then_inc(s_sc, 1)

        @blk.tensor
        def _(t):
            for g in range(NG):
                t.wait_ge(s_cast, g + 1)
                if g >= 1:
                    t.wait_ge(s_stat, g)
                for c in range(8):
                    ins = t.matmul(pA[:], p.onesb[:], xbf[c][:], start=(c == 0), stop=(c == 7))
                ins.then_inc(s_pe, 1)
                t.wait_ge(s_sq, g + 1)
                for c in range(8):
                    ins = t.matmul(pB[:], p.onesb[:], sqb[c][:], start=(c == 0), stop=(c == 7))
                ins.then_inc(s_pe, 1)
                t.wait_ge(s_v2, g + 1)
                if g >= 1:
                    t.wait_ge(s_apl, g)
                t.matmul(pC[:], p.onesrow[:], mean, start=True, stop=True)
                t.matmul(pD[:], p.onesrow[:], rstd, start=True, stop=True).then_inc(s_bc, 1)
                t.wait_ge(s_apl, g + 1)
                for sub in range(4):
                    gs = 4 * g + sub
                    pr = 2 * (gs % 2)
                    if gs >= 2:
                        t.wait_ge(s_tev, gs - 1)
                    t0 = (g // 4) * 2048 + (g % 4) * 512 + sub * 128
                    scol = slice(t0, t0 + 128)
                    for c in range(8):
                        ins = t.transpose(pE[pr + c // 4][:, (c % 4) * 128:(c % 4) * 128 + 128],
                                          p.xnf[c][:, scol], p.ident[:])
                    ins.then_inc(s_ptr, 1)

        @blk.gpsimd
        def _(gp):
            for gs in range(32):
                gp.wait_ge(s_tev, gs + 1)
                row0 = gs * 128
                gp.dma_start(p.xn_tm[row0:row0 + 128, :], xtmb[gs % 2][:]).then_inc(s_xtm, 16)
                if p.tap_xn is not None:
                    gp.dma_start(p.tap_xn[row0:row0 + 128, :], xtmb[gs % 2][:]).then_inc(s_xtm, 16)
            gp.wait_ge(s_xtm, 32 * XM)
    pctx.close()


def build_p2(p):
    nc, sem = p.nc, p.sem
    pctx = ExitStack()
    sb = lambda name, shape, dt: pctx.enter_context(nc.sbuf_tensor(name, shape, dt))
    ps = lambda name, shape, dt=F32: pctx.enter_context(nc.psum_tensor(name, shape, dt))

    vtm = sb("p2_vtm", [128, 16 * 512], BF16)
    qsb = [sb(f"p2_q{j}", [128, S], BF16) for j in range(4)]
    ksb = [sb(f"p2_k{j}", [128, S], BF16) for j in range(4)]
    wqt = [sb(f"p2_wq{i}", [128, 1024], BF16) for i in range(8)]
    wvr = sb("p2_wvr", [128, 8 * 512], BF16)
    e1sb = [sb(f"p2_e1_{i}", [128, 512], BF16) for i in range(2)]
    e2sb = [sb(f"p2_e2_{i}", [128, 512], BF16) for i in range(2)]
    tmpa = sb("p2_tmpa", [64, 512], F32)
    tmpb = sb("p2_tmpb", [64, 512], F32)
    tmpc = sb("p2_tmpc", [128, 512], F32)
    tmpd = sb("p2_tmpd", [128, 512], F32)
    zr1 = sb("p2_zr1", [1, 512], F32)
    zr2 = sb("p2_zr2", [1, 512], F32)
    zb1 = sb("p2_zb1", [128, 512], F32)
    zb2 = sb("p2_zb2", [128, 512], F32)
    outh = [sb(f"p2_outh{i}", [128, 512], BF16) for i in range(2)]

    bkh = [ps(f"p2_bk{i}", [128, 512], F32) for i in range(8)]
    bk = [h[:] for h in bkh]
    pZ1 = bkh[6][0:1, :]
    pZ2 = bkh[7][0:1, :]

    s_wq = sem("p2_wq")
    s_wv = sem("p2_wv")
    s_qk = sem("p2_qk")
    s_rot = sem("p2_rot")
    s_vmm = sem("p2_vmm")
    s_vev = sem("p2_vev")
    s_s = sem("p2_s")
    s_exp = sem("p2_exp")
    s_pv = sem("p2_pv")
    s_zr = sem("p2_zr")
    s_bc = sem("p2_bc")
    s_cmb = sem("p2_cmb")
    s_st = sem("p2_st")

    def dcol(g, hg, pr):
        return g * 1024 + hg * 256 + pr * 128

    nG = 8
    for G in range(nG):
        b, hg = G // 4, G % 4

        with nc.Block() as blk:
            @blk.sync
            def _(sync, b=b, hg=hg, G=G):
                for j in range(4):
                    h = hg * 4 + j
                    for qg in range(4):
                        jq = G * 16 + j * 4 + qg
                        sync.wait_ge(s_cmb, jq + 1)
                        sync.dma_start(
                            p.attn_fm[b, h * 128:(h + 1) * 128, qg * 512:(qg + 1) * 512],
                            outh[jq % 2][:]).then_inc(s_st, 16)
                sync.wait_ge(s_st, 16 * 16 * (G + 1))

            @blk.gpsimd
            def _(gp, b=b, hg=hg, G=G):
                for c in range(8):
                    gp.dma_start(wvr[:, c * 512:(c + 1) * 512],
                                 p.Wv_rhs[c * 128:(c + 1) * 128,
                                          hg * 512:(hg + 1) * 512]).then_inc(s_wv, 16)
                for pr in range(2):
                    for g in range(4):
                        i = pr * 4 + g
                        gp.dma_start(
                            wqt[i][:].rearrange("p (c f) -> p c f", f=128),
                            p.WqkT[:, dcol(g, hg, pr):dcol(g, hg, pr) + 128]
                            .rearrange("(c p) f -> p c f", p=128)).then_inc(s_wq, 16)

            @blk.tensor
            def _(t, b=b, hg=hg, G=G):
                for tc in range(16):
                    t.wait_ge(s_wv, 16 * 8 * (G + 1))
                    vt = G * 16 + tc
                    if vt >= 2:
                        t.wait_ge(s_vev, vt - 1)
                    if tc == 0 and G >= 1:
                        t.wait_ge(s_pv, 256 * G)
                    for c in range(8):
                        ins = t.matmul(bk[4 + tc % 2],
                                       p.xnf[c][:, b * S + tc * 128:b * S + tc * 128 + 128],
                                       wvr[:, c * 512:(c + 1) * 512],
                                       start=(c == 0), stop=(c == 7))
                    ins.then_inc(s_vmm, 1)
                t.wait_ge(s_wq, 16 * 8 * (G + 1))
                for pr in range(2):
                    for qk in range(2):
                        for tg in range(4):
                            ev = G * 16 + pr * 8 + qk * 4 + tg
                            if ev >= 2:
                                t.wait_ge(s_rot, ev - 1)
                            for gg in range(2):
                                g = 2 * qk + gg
                                i = pr * 4 + g
                                for c in range(8):
                                    ins = t.matmul(bk[(tg % 2) * 2 + gg],
                                                   wqt[i][:, c * 128:(c + 1) * 128],
                                                   p.xnf[c][:, b * S + tg * 512:b * S + tg * 512 + 512],
                                                   start=(c == 0), stop=(c == 7))
                            ins.then_inc(s_qk, 1)
                for j in range(4):
                    t.wait_ge(s_rot, G * 16 + 16)
                    for qg in range(4):
                        jq = G * 16 + j * 4 + qg
                        for kc in range(16):
                            ki = jq * 16 + kc
                            if ki >= 2:
                                t.wait_ge(s_exp, 2 * (ki - 1))
                            t.matmul(bk[kc % 2],
                                     ksb[j][0:64, kc * 128:(kc + 1) * 128],
                                     qsb[j][0:64, qg * 512:(qg + 1) * 512],
                                     start=True, stop=True)
                            t.matmul(bk[2 + kc % 2],
                                     ksb[j][64:128, kc * 128:(kc + 1) * 128],
                                     qsb[j][64:128, qg * 512:(qg + 1) * 512],
                                     start=True, stop=True,
                                     tile_position=(64, 0)).then_inc(s_s, 1)
                            t.wait_ge(s_exp, 2 * (ki + 1))
                            t.wait_ge(s_vev, G * 16 + kc + 1)
                            if kc == 0:
                                if jq >= 1:
                                    t.wait_ge(s_cmb, jq)
                            t.matmul(bk[4], vtm[:, kc * 512 + j * 128:kc * 512 + j * 128 + 128],
                                     e1sb[kc % 2][:], start=(kc == 0), stop=(kc == 15))
                            t.matmul(bk[5], vtm[:, kc * 512 + j * 128:kc * 512 + j * 128 + 128],
                                     e2sb[kc % 2][:], start=(kc == 0), stop=(kc == 15))
                            t.matmul(pZ1, p.onesb[:], e1sb[kc % 2][:],
                                     start=(kc == 0), stop=(kc == 15))
                            t.matmul(pZ2, p.onesb[:], e2sb[kc % 2][:],
                                     start=(kc == 0), stop=(kc == 15)).then_inc(s_pv, 1)
                        t.wait_ge(s_zr, jq + 1)
                        t.matmul(bk[6], p.onesrow[:], zr1[:], start=True, stop=True)
                        t.matmul(bk[7], p.onesrow[:], zr2[:], start=True, stop=True
                                 ).then_inc(s_bc, 1)

            @blk.vector
            def _(v, b=b, hg=hg, G=G):
                for pr in range(2):
                    for qk in range(2):
                        for tg in range(4):
                            need = G * 16 + pr * 8 + qk * 4 + tg + 1
                            v.wait_ge(s_qk, need)
                            tgc = slice(tg * 512, (tg + 1) * 512)
                            for jj in range(2):
                                j = pr * 2 + jj
                                off = jj * 64
                                dst = (qsb if qk == 0 else ksb)[j]
                                A = bkh[(tg % 2) * 2][off:off + 64, :]
                                Bv = bkh[(tg % 2) * 2 + 1][off:off + 64, :]
                                cosx = p.cos[0:64, tgc]
                                sinx = p.sin[0:64, tgc]
                                v.tensor_mul(tmpa[:], A, cosx)
                                v.tensor_mul(tmpb[:], Bv, sinx)
                                v.tensor_sub(dst[0:64, tgc], tmpa[:], tmpb[:])
                                v.tensor_mul(tmpa[:], Bv, cosx)
                                v.tensor_mul(tmpb[:], A, sinx)
                                ins = v.tensor_add(dst[64:128, tgc], tmpa[:], tmpb[:])
                            p.vfence(v, ins)
                            v.engine_nop().then_inc(s_rot, 1)
                for j in range(4):
                    for qg in range(4):
                        jq = G * 16 + j * 4 + qg
                        v.wait_ge(s_pv, jq * 16 + 16)
                        v.reciprocal(zr1[:], pZ1)
                        p.vfence(v, v.reciprocal(zr2[:], pZ2))
                        v.tensor_scalar(zr2[:], zr2[:], p.lam11[:], None, ALU.mult)
                        v.engine_nop().then_inc(s_zr, 1)
                        v.wait_ge(s_bc, jq + 1)
                        v.tensor_copy(zb1[:], bk[6])
                        p.vfence(v, v.tensor_copy(zb2[:], bk[7]))
                        v.tensor_mul(tmpc[:], bk[5], zb2[:])
                        p.vfence(v, v.tensor_mul(tmpd[:], bk[4], zb1[:]))
                        if jq >= 2:
                            v.wait_ge(s_st, 16 * (jq - 1))
                        p.vfence(v, v.tensor_sub(outh[jq % 2][:], tmpd[:], tmpc[:]))
                        v.engine_nop().then_inc(s_cmb, 1)

            @blk.scalar
            def _(s, b=b, hg=hg, G=G):
                for tc in range(16):
                    s.wait_ge(s_vmm, G * 16 + tc + 1)
                    p.afence(s, s.copy(vtm[:, tc * 512:(tc + 1) * 512], bk[4 + tc % 2]))
                    s.nop().then_inc(s_vev, 1)
                for j in range(4):
                    for qg in range(4):
                        jq = G * 16 + j * 4 + qg
                        for kc in range(16):
                            ki = jq * 16 + kc
                            s.wait_ge(s_s, ki + 1)
                            if ki >= 2:
                                s.wait_ge(s_pv, ki - 1)
                            s.activation(e1sb[kc % 2][:], bk[kc % 2], AF.Exp,
                                         bias=p.biasC[:], scale=0.125)
                            p.afence(s, s.activation(e2sb[kc % 2][:], bk[2 + kc % 2],
                                                     AF.Exp, bias=p.biasC[:], scale=0.125))
                            s.nop().then_inc(s_exp, 2)
    pctx.close()


def build_p3(p):
    nc, sem = p.nc, p.sem
    pctx = ExitStack()
    sb = lambda name, shape, dt: pctx.enter_context(nc.sbuf_tensor(name, shape, dt))
    ps = lambda name, shape, dt=F32: pctx.enter_context(nc.psum_tensor(name, shape, dt))

    wpj = [sb(f"p3_w{d}", [128, 1024], BF16) for d in range(16)]
    arb = [[sb(f"p3_a{i}_{d}", [128, 512], BF16) for d in range(16)] for i in range(2)]
    ob = [sb(f"p3_o{i}", [128, 512], F32) for i in range(2)]
    pk = [h[:] for h in [ps(f"p3_p{i}", [128, 512], F32) for i in range(2)]]

    s_w = sem("p3_w")
    s_a = sem("p3_a")
    s_mm = sem("p3_mm")
    s_ev = sem("p3_ev")
    s_st = sem("p3_st")

    with nc.Block() as blk:
        @blk.gpsimd
        def _(gp):
            for d in range(16):
                gp.dma_start(wpj[d][:], p.WprojT[d * 128:(d + 1) * 128, :]
                             ).then_inc(s_w, 16)

        @blk.sync
        def _(sync):
            for bq in range(8):
                b, qg = bq // 4, bq % 4
                if bq >= 2:
                    sync.wait_ge(s_mm, 8 * (bq - 1))
                for d in range(16):
                    sync.dma_start(arb[bq % 2][d][:],
                                   p.attn_fm[b, d * 128:(d + 1) * 128,
                                             qg * 512:(qg + 1) * 512]).then_inc(s_a, 16)

        @blk.tensor
        def _(t):
            t.wait_ge(s_w, 256)
            for bq in range(8):
                t.wait_ge(s_a, 16 * 16 * (bq + 1))
                for c in range(8):
                    m = bq * 8 + c
                    if m >= 2:
                        t.wait_ge(s_ev, m - 1)
                    for d in range(16):
                        ins = t.matmul(pk[m % 2], wpj[d][:, c * 128:(c + 1) * 128],
                                       arb[bq % 2][d][:], start=(d == 0), stop=(d == 15))
                    ins.then_inc(s_mm, 1)

        @blk.vector
        def _(v):
            for bq in range(8):
                for c in range(8):
                    m = bq * 8 + c
                    v.wait_ge(s_mm, m + 1)
                    if m >= 2:
                        v.wait_ge(s_st, 16 * (m - 1))
                    p.vfence(v, v.tensor_copy(ob[m % 2][:], pk[m % 2]))
                    v.engine_nop().then_inc(s_ev, 1)

        @blk.scalar
        def _(sc):
            for bq in range(8):
                b, qg = bq // 4, bq % 4
                for c in range(8):
                    m = bq * 8 + c
                    sc.wait_ge(s_ev, m + 1)
                    sc.dma_start(p.attnproj[b, c * 128:(c + 1) * 128,
                                            qg * 512:(qg + 1) * 512],
                                 ob[m % 2][:]).then_inc(s_st, 16)
            sc.wait_ge(s_st, 16 * 64)
            nst = 64
            if p.tap_attn is not None:
                for b2 in range(B):
                    sc.dma_start(p.tap_attn[b2], p.attn_fm[b2]).then_inc(s_st, 16)
                    nst += 1
            if p.tap_attnproj is not None:
                for b2 in range(B):
                    sc.dma_start(p.tap_attnproj[b2], p.attnproj[b2]).then_inc(s_st, 16)
                    nst += 1
            sc.wait_ge(s_st, 16 * nst)
    pctx.close()


def build_p4(p):
    nc, sem = p.nc, p.sem
    pctx = ExitStack()
    sb = lambda name, shape, dt: pctx.enter_context(nc.sbuf_tensor(name, shape, dt))
    ps = lambda name, shape, dt=F32: pctx.enter_context(nc.psum_tensor(name, shape, dt))

    rtr = sb("p4_rtr", [128, 8 * 16], BF16)
    xstrip = [sb(f"p4_xs{i}", [128, C], BF16) for i in range(2)]
    xfmc = [sb(f"p4_xf{i}", [128, C], BF16) for i in range(2)]
    negm = sb("p4_negm", [128, 1], F32)
    et = sb("p4_et", [128, 16], F32)
    zrow = sb("p4_z", [128, 1], F32)
    ind = sb("p4_ind", [128, 16], F32)
    val = sb("p4_val", [128, 16], F32)
    vmn = sb("p4_vmn", [128, 1], F32)
    tmp16 = sb("p4_tmp16", [128, 16], F32)
    slotp = sb("p4_slotp", [128, 1], F32)
    over = sb("p4_over", [128, 1], F32)
    destf = sb("p4_destf", [128, 1], F32)
    oh = [sb(f"p4_oh{i}", [128, 16], F32) for i in range(32)]
    idxv = [sb(f"p4_idx{i}", [128, 1], F32) for i in range(32)]
    wch = [sb(f"p4_w{i}", [128, 1], F32) for i in range(32)]
    desti = [sb(f"p4_di{i}", [128, 1], I32) for i in range(32)]
    routebuf = [sb(f"p4_rb{i}", [128, 4], F32) for i in range(2)]
    logbuf = [sb(f"p4_lb{i}", [128, 16], F32) for i in range(2)]
    carry = sb("p4_carry", [1, 16], F32)
    gi = [[sb(f"p4_gi{b}_{cs}", [128, 1], I32) for cs in range(4)] for b in range(2)]
    wv = [[sb(f"p4_wv{b}_{cs}", [128, 1], F32) for cs in range(4)] for b in range(2)]
    xetm = [sb(f"p4_xet{cs}", [128, C], BF16) for cs in range(4)]
    xefm = sb("p4_xefm", [128, 8 * CAP], BF16)
    w1s = [[sb(f"p4_w1_{b}_{c}", [128, FF // 2], BF16) for c in range(8)] for b in range(2)]
    w2s = [sb(f"p4_w2_{ff}", [128, C], BF16) for ff in range(32)]
    silu = [sb(f"p4_sil{ff}", [128, CAP], BF16) for ff in range(32)]
    oute = [sb(f"p4_oute{cs}", [128, C], BF16) for cs in range(4)]

    bk = [ps(f"p4_bk{i}", [128, 512], F32) for i in range(6)]
    bb = [ps(f"p4_bb{i}", [128, 1024], BF16) for i in range(2)]
    pL = [bk[4][:, 0:16], bk[5][:, 0:16]]
    pcum = bk[2][:, 0:16]
    ptot = bk[3][0:1, 0:16]
    pH = [bk[0][:], bk[1][:]]
    pO = [bk[2][:], bk[3][:], bk[4][:], bk[5][:]]

    s_init = sem("p4_init"); s_rtr = sem("p4_rtr")
    s_ld = sem("p4_ld"); s_tr = sem("p4_tr"); s_tev = sem("p4_tev")
    s_lg = sem("p4_lg"); s_nm = sem("p4_nm"); s_et = sem("p4_et"); s_oh = sem("p4_oh")
    s_cum = sem("p4_cum"); s_tot = sem("p4_tot"); s_cw = sem("p4_cw"); s_di = sem("p4_di")
    s_sc = sem("p4_sc"); s_gl = sem("p4_gl"); s_ga = sem("p4_ga")
    s_xtr = sem("p4_xtr"); s_xev = sem("p4_xev")
    s_w1p = [sem("p4_w1a"), sem("p4_w1b")]; s_w2 = sem("p4_w2")
    s_gac = [sem(f"p4_gac{i}") for i in range(4)]
    s_sctp = [sem("p4_scta"), sem("p4_sctb")]
    s_h1 = sem("p4_h1"); s_sil = sem("p4_sil"); s_h2 = sem("p4_h2")
    s_oev = sem("p4_oev"); s_sct = sem("p4_sct"); s_tap = sem("p4_tap")
    s_rt = sem("p4_rtp"); s_rt2 = sem("p4_rt2")

    glist_flat = p.glist.rearrange("e (c one) -> (e c) one", one=1)
    wlist_flat = p.wlist.rearrange("e (c one) -> (e c) one", one=1)
    xefm3 = xefm[:].rearrange("p (c f) -> p c f", f=CAP)

    for l in range(NL):
        xsrc = p.xn_tm if l == 0 else p.xt_tm[l - 1]
        xdst = p.xt_tm[l]

        with nc.Block() as blk:
            @blk.gpsimd
            def _(gp, l=l, xdst=xdst):
                gp.dma_start(p.glist[:, :], p.sent_in).then_inc(s_init, 16)
                gp.dma_start(xdst, p.zeros_in).then_inc(s_init, 16)
                gp.dma_start(p.wlist[:, :], p.zeros_in[0:E, 0:CAP]).then_inc(s_init, 16)
                gp.dma_start(rtr[:].rearrange("p (c e) -> p c e", e=16),
                             p.routers_d[l].rearrange("(c p) e -> p c e", p=128)
                             ).then_inc(s_rtr, 16)
                IM = 80 if p.tap_glist is not None else 48
                bc_l = gp.to_reg(E * CAP - 1)
                gp.wait_ge(s_init, IM * l + 48)
                for i in range(32):
                    gp.wait_ge(s_di, 33 * l + i + 2)
                    gp.indirect_dma_start(
                        glist_flat, IndirectOffsetOnAxis(desti[i][:], 0),
                        p.iotatok[:, i:i + 1], None,
                        bounds_check=bc_l, oob_is_err=False).then_inc(s_sc, 16)
                    gp.indirect_dma_start(
                        wlist_flat, IndirectOffsetOnAxis(desti[i][:], 0),
                        wch[i][:], None,
                        bounds_check=bc_l, oob_is_err=False).then_inc(s_sc, 16)
                gp.wait_ge(s_sc, 32 * 32 * (l + 1))
                if p.tap_glist is not None:
                    gp.dma_start(p.tap_glist[l], p.glist[:, :]).then_inc(s_init, 16)
                    gp.dma_start(p.tap_wlist[l], p.wlist[:, :]).then_inc(s_init, 16)
                    gp.wait_ge(s_init, IM * (l + 1))

            @blk.sync
            def _(sync, l=l, xsrc=xsrc):
                for pos in range(33):
                    i = pos % 32
                    gc = 33 * l + pos
                    if gc >= 2:
                        sync.wait_ge(s_tr, gc - 1)
                    sync.dma_start(xstrip[pos % 2][:], xsrc[i * 128:(i + 1) * 128, :]
                                   ).then_inc(s_ld, 16)

            @blk.scalar
            def _(s, l=l):
                for pos in range(33):
                    i = pos % 32
                    gc = 33 * l + pos
                    if l == 0 and p.tap_logits is not None:
                        s.wait_ge(s_lg, gc + 1)
                        if pos >= 2:
                            s.wait_ge(s_rt2, 16 * (pos - 1))
                        s.copy(logbuf[pos % 2][:], pL[pos % 2])
                        s.dma_start(p.tap_logits[i], logbuf[pos % 2][:]).then_inc(s_rt2, 16)
                    s.wait_ge(s_nm, gc + 1)
                    if gc >= 1:
                        s.wait_ge(s_oh, gc)
                    p.afence(s, s.activation(et[:], pL[pos % 2], AF.Exp, bias=negm[:],
                                              accum_out=zrow[:]))
                    s.nop().then_inc(s_et, 1)

            @blk.tensor
            def _(t, l=l):
                def emit_transposes(pos, gc):
                    t.wait_ge(s_ld, 16 * (gc + 1))
                    if gc >= 2:
                        t.wait_ge(s_tev, gc - 1)
                    for c in range(8):
                        ins = t.transpose(bb[pos % 2][:, c * 128:(c + 1) * 128],
                                          xstrip[pos % 2][:, c * 128:(c + 1) * 128],
                                          p.ident[:])
                    ins.then_inc(s_tr, 1)

                def emit_logits(pos, gc):
                    t.wait_ge(s_tev, gc + 1)
                    if pos == 0:
                        t.wait_ge(s_rtr, 16 * (l + 1))
                    if gc >= 2:
                        t.wait_ge(s_oh, gc - 1)
                        t.wait_ge(s_et, gc - 1)
                    for c in range(8):
                        ins = t.matmul(pL[pos % 2], xfmc[pos % 2][:, c * 128:(c + 1) * 128],
                                       rtr[:, c * 16:(c + 1) * 16],
                                       start=(c == 0), stop=(c == 7))
                    ins.then_inc(s_lg, 1)
                for pos in range(33):
                    emit_transposes(pos, 33 * l + pos)
                    if pos >= 1:
                        emit_logits(pos - 1, 33 * l + pos - 1)
                emit_logits(32, 33 * l + 32)
                for pos in range(33):
                    i = max(0, pos - 1)
                    gc = 33 * l + pos
                    t.wait_ge(s_oh, 33 * (l + 1))
                    t.wait_ge(s_cw, gc + 1)
                    t.matmul(pcum, p.u128[:], oh[i][:], start=True, stop=False)
                    t.matmul(pcum, p.onesrow[:], carry[:], start=False, stop=True
                             ).then_inc(s_cum, 1)
                    t.matmul(ptot, p.onesf[:], oh[i][:], start=True,
                             stop=True).then_inc(s_tot, 1)

            @blk.vector
            def _(v, l=l):
                for pos in range(33):
                    i = pos % 32
                    gc = 33 * l + pos
                    v.wait_ge(s_tr, gc + 1)
                    if gc >= 2:
                        v.wait_ge(s_lg, gc - 1)
                    p.vfence(v, v.tensor_copy(xfmc[pos % 2][:], bb[pos % 2][:, 0:1024]))
                    v.engine_nop().then_inc(s_tev, 1)
                    v.wait_ge(s_lg, gc + 1)
                    p.vfence(v, v.reduce_max(negm[:], pL[pos % 2], AX.X, negate=True))
                    v.engine_nop().then_inc(s_nm, 1)
                    v.wait_ge(s_et, gc + 1)
                    p.vfence(v, v.reciprocal(wch[i][:], zrow[:]))
                    p.vfence(v, v.tensor_scalar(tmp16[:], pL[pos % 2], negm[:], 1e8,
                                                ALU.add, ALU.mult))
                    p.vfence(v, v.tensor_scalar(ind[:], tmp16[:], 1.0, 0.0,
                                                ALU.add, ALU.max))
                    p.vfence(v, v.tensor_mul(val[:], ind[:], p.iota16[:]))
                    p.vfence(v, v.reduce_max(vmn[:], val[:], AX.X))
                    p.vfence(v, v.tensor_scalar(over[:], vmn[:], -1.0, 1.0,
                                                ALU.mult, ALU.add))
                    p.vfence(v, v.tensor_scalar(oh[i][:], val[:], over[:], 0.0,
                                                ALU.add, ALU.max))
                    p.vfence(v, v.tensor_scalar(idxv[i][:], vmn[:], -1.0, 1024.0,
                                                ALU.mult, ALU.add))
                    v.engine_nop().then_inc(s_oh, 1)
                for pos in range(33):
                    i = max(0, pos - 1)
                    gc = 33 * l + pos
                    v.wait_ge(s_oh, 33 * (l + 1))
                    if pos <= 1:
                        p.vfence(v, v.memset(carry[:], 0.0))
                    else:
                        v.wait_ge(s_tot, gc)
                        p.vfence(v, v.tensor_add(carry[:], carry[:], ptot))
                    v.engine_nop().then_inc(s_cw, 1)
                    v.wait_ge(s_cum, gc + 1)
                    p.vfence(v, v.tensor_mul(tmp16[:], pcum, oh[i][:]))
                    p.vfence(v, v.reduce_sum(slotp[:], tmp16[:], AX.X))
                    p.vfence(v, v.tensor_scalar(destf[:], idxv[i][:], float(CAP), slotp[:],
                                                ALU.mult, ALU.add))
                    p.vfence(v, v.tensor_scalar(over[:], slotp[:], -(float(CAP) + 0.5), 0.0,
                                                ALU.add, ALU.max))
                    p.vfence(v, v.tensor_scalar(over[:], over[:], 4e4, -1.0,
                                                ALU.mult, ALU.add))
                    p.vfence(v, v.tensor_add(destf[:], destf[:], over[:]))
                    p.vfence(v, v.tensor_copy(desti[i][:], destf[:]))
                    v.engine_nop().then_inc(s_di, 1)

        if l == NL - 1 and p.tap_desti is not None:
            with nc.Block() as blkd:
                @blkd.sync
                def _(sync):
                    n = 0
                    for i2 in range(32):
                        sync.dma_start(p.tap_desti[i2].rearrange("(x y) -> x y", y=1),
                                       desti[i2][:]).then_inc(s_rt, 16)
                        n += 16
                        if p.tap_oh is not None:
                            sync.dma_start(p.tap_oh[i2], oh[i2][:]).then_inc(s_rt, 16)
                            n += 16
                        if p.tap_route is not None:
                            with nc.allow_non_contiguous_dma(reason="debug tap"):
                                sync.dma_start(p.tap_route[i2][:, 0:1], idxv[i2][:]
                                               ).then_inc(s_rt, 16)
                            n += 16
                    if p.tap_stats is not None:
                        slabs = [p.iota16[:], ind[:], val[:], tmp16[:], et[:]]
                        cols = [negm[:], vmn[:], over[:], zrow[:]]
                        for j2, t2 in enumerate(slabs):
                            sync.dma_start(p.tap_stats[j2], t2).then_inc(s_rt, 16)
                            n += 16
                        with nc.allow_non_contiguous_dma(reason="debug tap"):
                            for j2, t2 in enumerate(cols):
                                sync.dma_start(p.tap_stats[5][:, j2:j2 + 1], t2
                                               ).then_inc(s_rt, 16)
                                n += 16
                    sync.wait_ge(s_rt, n)
        if "b" not in P4_PARTS:
            continue
        with nc.Block() as blk:
            @blk.sync
            def _(sync, l=l):
                for e in range(E):
                    u = 16 * l + e
                    for cs in range(4):
                        if u >= 2:
                            sync.wait_ge(s_sctp[u % 2], 16 * 4 * ((u - 2) // 2 + 1))
                        sync.dma_start(gi[e % 2][cs][:],
                                       p.glist[e, cs * 128:(cs + 1) * 128]
                                       .rearrange("(x y) -> x y", y=1)).then_inc(s_gl, 16)
                        sync.dma_start(wv[e % 2][cs][:],
                                       p.wlist[e, cs * 128:(cs + 1) * 128]
                                       .rearrange("(x y) -> x y", y=1)).then_inc(s_gl, 16)

            @blk.gpsimd
            def _(gp, l=l, xsrc=xsrc, xdst=xdst):
                bc_t = gp.to_reg(T - 1)
                for e in range(E):
                    u = 16 * l + e
                    for hf in range(2):
                        hidx = 2 * u + hf
                        if hidx >= 2:
                            gp.wait_ge(s_h1, (hidx - 1) * 16)
                        for c in range(8):
                            gp.dma_start(w1s[hidx % 2][c][:],
                                         p.W1_d[e, c * 128:(c + 1) * 128,
                                                hf * 2048:(hf + 1) * 2048]
                                         ).then_inc(s_w1p[hidx % 2], 16)
                    for ff in range(32):
                        if u >= 1:
                            gp.wait_ge(s_h2, (u - 1) * 64 + 32 + ff + 1)
                        gp.dma_start(w2s[ff][:],
                                     p.W2_d[e, ff * 128:(ff + 1) * 128, :]
                                     ).then_inc(s_w2, 16)
                    for cs in range(4):
                        gp.wait_ge(s_gl, 16 * (8 * u + 2 * cs + 1))
                        if u >= 1:
                            gp.wait_ge(s_xtr, 4 * (u - 1) + cs + 1)
                        gp.indirect_dma_start(
                            xetm[cs][:], None, xsrc,
                            IndirectOffsetOnAxis(gi[e % 2][cs][:], 0),
                            bounds_check=bc_t, oob_is_err=False).then_inc(s_gac[cs], 16)
                    for cs in range(4):
                        gp.wait_ge(s_oev, 8 * u + 2 * (cs + 1))
                        gp.indirect_dma_start(
                            xdst, IndirectOffsetOnAxis(gi[e % 2][cs][:], 0),
                            oute[cs][:], None,
                            bounds_check=bc_t, oob_is_err=False).then_inc(s_sctp[u % 2], 16)
                gp.wait_ge(s_sctp[0], 16 * 4 * 8 * (l + 1))
                gp.wait_ge(s_sctp[1], 16 * 4 * 8 * (l + 1))
                if p.tap_xt[l] is not None:
                    gp.dma_start(p.tap_xt[l], xdst).then_inc(s_tap, 16)
                    gp.wait_ge(s_tap, 16 * (l + 1))

            @blk.tensor
            def _(t, l=l):
                for e in range(E):
                    u = 16 * l + e
                    for cs in range(4):
                        t.wait_ge(s_gac[cs], 16 * (u + 1))
                        if 4 * u + cs >= 2:
                            t.wait_ge(s_xev, 4 * u + cs - 1)
                        for c in range(8):
                            ins = t.transpose(bb[cs % 2][:, c * 128:(c + 1) * 128],
                                              xetm[cs][:, c * 128:(c + 1) * 128],
                                              p.ident[:])
                        ins.then_inc(s_xtr, 1)
                    t.wait_ge(s_xev, 4 * u + 4)
                    for ff in range(32):
                        fidx = 32 * u + ff
                        hidx = 2 * u + ff // 16
                        t.wait_ge(s_w1p[hidx % 2], 16 * 8 * (hidx // 2 + 1))
                        if fidx >= 2:
                            t.wait_ge(s_sil, fidx - 1)
                        for c in range(8):
                            ins = t.matmul(pH[ff % 2],
                                           w1s[hidx % 2][c][:, (ff % 16) * 128:(ff % 16) * 128 + 128],
                                           xefm[:, c * CAP:(c + 1) * CAP],
                                           start=(c == 0), stop=(c == 7))
                        ins.then_inc(s_h1, 1)
                    for ch in range(2):
                        for ff in range(32):
                            t.wait_ge(s_sil, 32 * u + ff + 1)
                            if ch == 0 and ff == 0:
                                t.wait_ge(s_w2, 16 * 32 * (u + 1))
                            if ch == 1 and ff == 0:
                                t.wait_ge(s_oev, 8 * u + 4)
                            for cs in range(2):
                                for dh in range(2):
                                    ins = t.matmul(
                                        pO[2 * cs + dh],
                                        silu[ff][:, (ch * 2 + cs) * 128:(ch * 2 + cs) * 128 + 128],
                                        w2s[ff][:, dh * 512:(dh + 1) * 512],
                                        start=(ff == 0), stop=(ff == 31))
                            ins.then_inc(s_h2, 1)

            @blk.scalar
            def _(s, l=l):
                for e in range(E):
                    u = 16 * l + e
                    for ff in range(32):
                        fidx = 32 * u + ff
                        s.wait_ge(s_h1, fidx + 1)
                        if u >= 1:
                            s.wait_ge(s_h2, (u - 1) * 64 + 32 + ff + 1)
                        p.afence(s, s.activation(silu[ff][:], pH[ff % 2], AF.Silu))
                        s.nop().then_inc(s_sil, 1)

            @blk.vector
            def _(v, l=l):
                for e in range(E):
                    u = 16 * l + e
                    for cs in range(4):
                        v.wait_ge(s_xtr, 4 * u + cs + 1)
                        if u >= 1 and cs == 0:
                            v.wait_ge(s_h1, u * 32)
                        p.vfence(v, v.tensor_copy(xefm3[:, :, cs * 128:cs * 128 + 128],
                                                   bb[cs % 2][:, 0:1024]
                                                   .rearrange("p (c f) -> p c f", f=128)))
                        v.engine_nop().then_inc(s_xev, 1)
                    for ch in range(2):
                        v.wait_ge(s_h2, 64 * u + 32 * (ch + 1))
                        for cs in range(2):
                            sc = ch * 2 + cs
                            if u >= 1:
                                v.wait_ge(s_sctp[(u - 1) % 2], 16 * 4 * ((u - 1) // 2 + 1))
                            for dh in range(2):
                                p.vfence(v, v.tensor_scalar(
                                    oute[sc][:, dh * 512:(dh + 1) * 512],
                                    pO[2 * cs + dh], wv[e % 2][sc][:], None, ALU.mult))
                                v.engine_nop().then_inc(s_oev, 1)
    pctx.close()


def build_p5(p):
    nc, sem = p.nc, p.sem
    pctx = ExitStack()
    sb = lambda name, shape, dt: pctx.enter_context(nc.sbuf_tensor(name, shape, dt))
    ps = lambda name, shape, dt=F32: pctx.enter_context(nc.psum_tensor(name, shape, dt))

    xld = [[sb(f"p5_x{i}_{c}", [128, 512], F32) for c in range(8)] for i in range(2)]
    ald = [[sb(f"p5_a{i}_{c}", [128, 512], F32) for c in range(8)] for i in range(2)]
    fst = [sb(f"p5_f{i}", [128, C], BF16) for i in range(2)]
    ob = [sb(f"p5_o{i}", [128, 512], F32) for i in range(4)]
    pF = [h[:] for h in [ps(f"p5_p{c}", [128, 512], BF16) for c in range(8)]]

    s_ld = sem("p5_ld")
    s_fsp = [sem("p5_fsa"), sem("p5_fsb")]
    s_tr = sem("p5_tr")
    s_add = sem("p5_add")
    s_st = sem("p5_st")

    with nc.Block() as blk:
        @blk.sync
        def _(sync):
            for bq in range(8):
                b, qg = bq // 4, bq % 4
                if bq >= 2:
                    sync.wait_ge(s_add, 8 * (bq - 1))
                for c in range(8):
                    sync.dma_start(xld[bq % 2][c][:],
                                   p.x_fm[b, c * 128:(c + 1) * 128,
                                          qg * 512:(qg + 1) * 512]).then_inc(s_ld, 16)
                    sync.dma_start(ald[bq % 2][c][:],
                                   p.attnproj[b, c * 128:(c + 1) * 128,
                                              qg * 512:(qg + 1) * 512]).then_inc(s_ld, 16)

        @blk.gpsimd
        def _(gp):
            for bq in range(8):
                for sub in range(4):
                    fi = bq * 4 + sub
                    if fi >= 2:
                        gp.wait_ge(s_tr, fi - 1)
                    row0 = (bq // 4) * S + (bq % 4) * 512 + sub * 128
                    gp.dma_start(fst[fi % 2][:], p.xt_tm[NL - 1][row0:row0 + 128, :]
                                 ).then_inc(s_fsp[fi % 2], 16)

        @blk.tensor
        def _(t):
            for bq in range(8):
                if bq >= 1:
                    t.wait_ge(s_add, 8 * bq)
                for sub in range(4):
                    fi = bq * 4 + sub
                    t.wait_ge(s_fsp[fi % 2], 16 * (fi // 2 + 1))
                    for c in range(8):
                        ins = t.transpose(pF[c][:, sub * 128:(sub + 1) * 128] if False else pF[c].tensor[0:128, sub * 128:(sub + 1) * 128],
                                          fst[fi % 2][:, c * 128:(c + 1) * 128],
                                          p.ident[:])
                    ins.then_inc(s_tr, 1)

        @blk.vector
        def _(v):
            for bq in range(8):
                v.wait_ge(s_ld, 16 * 16 * (bq + 1))
                v.wait_ge(s_tr, 4 * (bq + 1))
                for c in range(8):
                    m = bq * 8 + c
                    if m >= 4:
                        v.wait_ge(s_st, 16 * (m - 3))
                    v.tensor_add(ob[m % 4][:], xld[bq % 2][c][:], ald[bq % 2][c][:])
                    p.vfence(v, v.tensor_add(ob[m % 4][:], ob[m % 4][:], pF[c]))
                    v.engine_nop().then_inc(s_add, 1)

        @blk.scalar
        def _(sc):
            for bq in range(8):
                b, qg = bq // 4, bq % 4
                for c in range(8):
                    m = bq * 8 + c
                    sc.wait_ge(s_add, m + 1)
                    sc.dma_start(p.out_fm[b, c * 128:(c + 1) * 128,
                                          qg * 512:(qg + 1) * 512],
                                 ob[m % 4][:]).then_inc(s_st, 16)
            sc.wait_ge(s_st, 16 * 64)
    pctx.close()


def kernel(**inputs):
    in_maps = _host_prep(inputs)
    nc = build()
    res = run_bass_kernel_spmd(nc, in_maps, core_ids=list(range(NCORES)))
    outs = []
    for c in range(NCORES):
        o = np.asarray(res.results[c]["out_fm"], np.float32)
        outs.append(o.transpose(0, 2, 1))
    return np.ascontiguousarray(np.concatenate(outs, 0))


# version 03
# speedup vs baseline: 1.0140x; 1.0140x over previous
import numpy as np
import ml_dtypes
from contextlib import ExitStack

import concourse.bass as bass
import concourse.mybir as mybir
from concourse.bass_utils import run_bass_kernel_spmd
from concourse.bass import IndirectOffsetOnAxis

F32 = mybir.dt.float32
BF16 = mybir.dt.bfloat16
I32 = mybir.dt.int32
AF = mybir.ActivationFunctionType
ALU = mybir.AluOpType
AX = mybir.AxisListType

NCORES = 8
B = 2
S = 2048
C = 1024
NH = 16
HD = 64
T = B * S
E = 16
NL = 4
FF = 4096
CAP = 512
EXP_SHIFT = -32.0
SENT = 1 << 20



def _host_constants():
    pos = np.arange(S, dtype=np.float64)
    inv_freq = 1.0 / 10000.0 ** (np.arange(0, 2 * HD, 2, dtype=np.float64) / (2 * HD))
    freqs = np.outer(pos, inv_freq)
    cos0 = np.cos(freqs).astype(np.float32).T
    sin0 = np.sin(freqs).astype(np.float32).T
    cos2 = np.ascontiguousarray(np.concatenate([cos0, cos0], 0))
    sin2 = np.ascontiguousarray(np.concatenate([sin0, sin0], 0))
    ident = np.eye(128, dtype=ml_dtypes.bfloat16)
    u128 = np.triu(np.ones((128, 128), np.float32))
    iota16 = (1024.0 - np.broadcast_to(np.arange(16, dtype=np.float32), (128, 16))).copy()
    iota_tok = (np.arange(32)[None, :] * 128 + np.arange(128)[:, None]).astype(np.int32)
    sent = np.full((E, CAP), SENT, np.int32)
    zeros_tm = np.zeros((T, C), ml_dtypes.bfloat16)
    return dict(cos2=cos2, sin2=sin2, ident=ident, u128=u128, iota16=iota16,
                iota_tok=np.ascontiguousarray(iota_tok), sent=sent, zeros_tm=zeros_tm)


def _host_prep(inputs):
    x = np.asarray(inputs["L0_attn_x"], np.float32)
    Wqkv = np.asarray(inputs["L0_attn_W_qkv"], np.float32)
    Wproj = np.asarray(inputs["L0_attn_W_proj"], np.float32)
    WqkT = np.ascontiguousarray(Wqkv[:4 * C].T)
    j = np.arange(2 * C)
    hh = j // 128
    dv = j % 128
    src = (4 + dv // HD) * C + hh * HD + (dv % HD)
    Wv_rhs = np.ascontiguousarray(Wqkv[src].T)
    WprojT = np.ascontiguousarray(Wproj.T)

    shared = dict(WqkT=WqkT, Wv_rhs=Wv_rhs, WprojT=WprojT,
                  routers=np.asarray(inputs["L0_ffn_routers"], np.float32),
                  W1=np.asarray(inputs["L0_ffn_W1"], np.float32),
                  W2=np.asarray(inputs["L0_ffn_W2"], np.float32),
                  lq1=np.asarray(inputs["L0_attn_lambda_q1"], np.float32).reshape(1, HD),
                  lq2=np.asarray(inputs["L0_attn_lambda_q2"], np.float32).reshape(1, HD),
                  lk1=np.asarray(inputs["L0_attn_lambda_k1"], np.float32).reshape(1, HD),
                  lk2=np.asarray(inputs["L0_attn_lambda_k2"], np.float32).reshape(1, HD),
                  **_host_constants())
    in_maps = []
    for c in range(NCORES):
        m = dict(shared)
        m["x_fm"] = np.ascontiguousarray(x[2 * c:2 * c + 2].transpose(0, 2, 1))
        in_maps.append(m)
    return in_maps


class P:
    pass


P4_PARTS = "ab"


def build(debug_taps=(), phases=("p0", "p1", "p2", "p3", "p4", "p5")):
    p = P()
    nc = bass.Bass("TRN2", target_bir_lowering=False, detect_race_conditions=False)
    p.nc = nc
    p.taps = set(debug_taps)

    tmp_sems = []
    try:
        for _ in range(256):
            tmp_sems.append(nc.alloc_semaphore())
    except Exception:
        pass
    with nc.Block() as _blk0:
        @_blk0.gpsimd
        def _(g):
            g.engine_nop()
            nc.clear_and_free_semaphores(tmp_sems)

    def din(name, shape, dt):
        return nc.dram_tensor(name, shape, dt, kind="ExternalInput").ap()
    p.x_fm = din("x_fm", [B, C, S], F32)
    p.WqkT = din("WqkT", [C, 4 * C], F32)
    p.Wv_rhs = din("Wv_rhs", [C, 2 * C], F32)
    p.WprojT = din("WprojT", [2 * C, C], F32)
    p.routers_d = din("routers", [NL, C, E], F32)
    p.W1_d = din("W1", [E, C, FF], F32)
    p.W2_d = din("W2", [E, FF, C], F32)
    p.lq1 = din("lq1", [1, HD], F32)
    p.lq2 = din("lq2", [1, HD], F32)
    p.lk1 = din("lk1", [1, HD], F32)
    p.lk2 = din("lk2", [1, HD], F32)
    p.cos2 = din("cos2", [128, S], F32)
    p.sin2 = din("sin2", [128, S], F32)
    p.ident_in = din("ident", [128, 128], BF16)
    p.u128_in = din("u128", [128, 128], F32)
    p.iota16_in = din("iota16", [128, 16], F32)
    p.iotatok_in = din("iota_tok", [128, 32], I32)
    p.sent_in = din("sent", [E, CAP], I32)
    p.zeros_in = din("zeros_tm", [T, C], BF16)

    p.out_fm = nc.dram_tensor("out_fm", [B, C, S], F32, kind="ExternalOutput").ap()
    p.xn_tm = nc.dram_tensor("xn_tm", [T, C], BF16).ap()
    p.xt_tm = [nc.dram_tensor(f"xt{i}_tm", [T, C], BF16).ap() for i in range(NL)]
    p.attn_fm = nc.dram_tensor("attn_fm", [B, 2 * C, S], BF16).ap()
    p.attnproj = nc.dram_tensor("attnproj", [B, C, S], F32).ap()
    p.glist = nc.dram_tensor("glist", [E, CAP], I32).ap()
    p.wlist = nc.dram_tensor("wlist", [E, CAP], F32).ap()

    def tap(name, shape, dt):
        if name in p.taps:
            return nc.dram_tensor("tap_" + name, shape, dt, kind="ExternalOutput").ap()
        return None
    p.tap_xn = tap("xn", [T, C], BF16)
    p.tap_glist = tap("glist", [NL, E, CAP], I32)
    p.tap_wlist = tap("wlist" if "glist" not in p.taps else "glist_w", [NL, E, CAP], F32)
    if "glist" in p.taps and p.tap_wlist is None:
        p.tap_wlist = nc.dram_tensor("tap_wlist", [NL, E, CAP], F32, kind="ExternalOutput").ap()
    p.tap_xt = [tap(f"xt{i}", [T, C], BF16) for i in range(NL)]
    p.tap_attn = tap("attn", [B, 2 * C, S], BF16)
    p.tap_attnproj = tap("attnproj", [B, C, S], F32)
    p.tap_route = tap("route", [32, 128, 4], F32)
    p.tap_oh = tap("oh", [32, 128, 16], F32)
    p.tap_logits = tap("logits", [32, 128, 16], F32)
    p.tap_xfm = tap("xfm", [32, 128, C], BF16)
    p.tap_xstrip = tap("xstrip", [32, 128, C], BF16)
    p.tap_desti = tap("desti", [32, 128], I32)
    p.tap_stats = tap("stats", [6, 128, 16], F32)

    with ExitStack() as ctx:
        p.ctx = ctx
        p.sb = lambda name, shape, dt: ctx.enter_context(nc.sbuf_tensor(name, shape, dt))
        p.ps = lambda name, shape, dt=F32: ctx.enter_context(nc.psum_tensor(name, shape, dt))
        p.sem = lambda name: ctx.enter_context(nc.semaphore(name))

        p.s_vf = p.sem("dve_fence")
        p.vf_n = [0]

        def vfence(v, ins):
            p.vf_n[0] += 1
            ins.then_inc(p.s_vf, 1)
            v.wait_ge(p.s_vf, p.vf_n[0])
        p.vfence = vfence
        p.s_af = p.sem("act_fence")
        p.af_n = [0]

        def afence(sc, ins):
            p.af_n[0] += 1
            ins.then_inc(p.s_af, 1)
            sc.wait_ge(p.s_af, p.af_n[0])
        p.afence = afence

        sb = p.sb
        p.guard = sb("guardpad", [128, 3072], BF16)
        p.ident = sb("identsb", [128, 128], BF16)
        p.u128 = sb("u128sb", [128, 128], F32)
        p.iota16 = sb("iota16sb", [128, 16], F32)
        p.iotatok = sb("iotatoksb", [128, 32], I32)
        p.onesb = sb("onesbsb", [128, 1], BF16)
        p.onesf = sb("onesfsb", [128, 1], F32)
        p.onesrow = sb("onesrowsb", [1, 128], F32)
        p.biasC = sb("biasCsb", [128, 1], F32)
        p.lamcol = sb("lamcolsb", [128, 1], F32)
        p.lam11 = sb("lam11sb", [1, 1], F32)
        actx = ExitStack()
        p.xnf = [actx.enter_context(nc.sbuf_tensor(f"xnf{c}", [128, T], BF16))
                 for c in range(8)]
        p.cos = actx.enter_context(nc.sbuf_tensor("cossb", [128, S], F32))
        p.sin = actx.enter_context(nc.sbuf_tensor("sinsb", [128, S], F32))

        if "p0" in phases:
            build_p0(p)
        if "p1" in phases:
            build_p1(p)
        if "p2" in phases:
            build_p2(p)
        if "p3" in phases:
            build_p3(p)
        actx.close()
        if "p4" in phases:
            build_p4(p)
        if "p5" in phases:
            build_p5(p)
    return nc


def build_p0(p):
    nc, sem = p.nc, p.sem
    from contextlib import ExitStack
    pctx = ExitStack()
    sb = lambda name, shape, dt: pctx.enter_context(nc.sbuf_tensor(name, shape, dt))
    ps = lambda name, shape, dt=F32: pctx.enter_context(nc.psum_tensor(name, shape, dt))
    p._p0ctx = pctx
    lt = [sb(f"p0_lam{i}", [1, HD], F32) for i in range(4)]
    e12 = sb("p0_e12", [1, 2], F32)
    pb = ps("p0_bc", [128, 1], F32)
    s_ld = sem("p0_ld")
    s_c = sem("p0_c")

    with nc.Block() as blk:
        @blk.sync
        def _(sync):
            sync.dma_start(p.ident[:], p.ident_in).then_inc(s_ld, 16)
            sync.dma_start(p.u128[:], p.u128_in).then_inc(s_ld, 16)
            sync.dma_start(p.iota16[:], p.iota16_in).then_inc(s_ld, 16)
            sync.dma_start(p.iotatok[:], p.iotatok_in).then_inc(s_ld, 16)
            sync.dma_start(p.cos[:], p.cos2).then_inc(s_ld, 16)
            sync.dma_start(p.sin[:], p.sin2).then_inc(s_ld, 16)
            for i, src in enumerate([p.lq1, p.lk1, p.lq2, p.lk2]):
                sync.dma_start(lt[i][:], src).then_inc(s_ld, 16)

        @blk.vector
        def _(v):
            v.memset(p.onesb[:], 1.0)
            v.memset(p.onesf[:], 1.0)
            v.memset(p.onesrow[:], 1.0)
            v.memset(p.biasC[:], EXP_SHIFT)
            v.wait_ge(s_ld, 160)
            v.tensor_mul(lt[0][:], lt[0][:], lt[1][:])
            p.vfence(v, v.tensor_mul(lt[2][:], lt[2][:], lt[3][:]))
            v.reduce_sum(e12[:, 0:1], lt[0][:], AX.X)
            p.vfence(v, v.reduce_sum(e12[:, 1:2], lt[2][:], AX.X))
            v.engine_nop().then_inc(s_c, 1)

        @blk.scalar
        def _(s):
            s.wait_ge(s_c, 1)
            p.afence(s, s.activation(e12[:], e12[:], AF.Exp))
            s.nop().then_inc(s_c, 1)

        @blk.vector
        def _(v2):
            v2.wait_ge(s_c, 2)
            p.vfence(v2, v2.tensor_sub(p.lam11[:], e12[:, 0:1], e12[:, 1:2]))
            v2.tensor_scalar(p.lam11[:], p.lam11[:], 0.2, None, ALU.add).then_inc(s_c, 1)

        @blk.tensor
        def _(t):
            t.wait_ge(s_c, 3)
            t.matmul(pb[:], p.onesrow[:], p.lam11[:], start=True, stop=True).then_inc(s_c, 1)

        @blk.vector
        def _(v3):
            v3.wait_ge(s_c, 4)
            v3.tensor_copy(p.lamcol[:], pb[:])
    pctx.close()


def build_p1(p):
    nc, sem = p.nc, p.sem
    from contextlib import ExitStack
    pctx = ExitStack()
    sb = lambda name, shape, dt: pctx.enter_context(nc.sbuf_tensor(name, shape, dt))
    ps = lambda name, shape, dt=F32: pctx.enter_context(nc.psum_tensor(name, shape, dt))
    NG = 8
    xld = [[sb(f"p1_x{i}_{c}", [128, 512], F32) for c in range(8)] for i in range(2)]
    xbf = [sb(f"p1_xb{c}", [128, 512], BF16) for c in range(8)]
    sqb = [sb(f"p1_sq{c}", [128, 512], BF16) for c in range(8)]
    stat = sb("p1_stat", [1, 512 * 5], F32)
    mean, ex2, var, srt, rstd = (stat[:, i * 512:(i + 1) * 512] for i in range(5))
    xtmb = [sb(f"p1_xtm{i}", [128, C], BF16) for i in range(4)]
    pA = ps("p1_A", [1, 512])
    pB = ps("p1_B", [1, 512])
    pC = ps("p1_C", [128, 512])
    pD = ps("p1_D", [128, 512])
    pE = [ps(f"p1_E{i}", [128, 512], BF16) for i in range(4)]

    s_ld = sem("p1_ld")
    s_cast = sem("p1_cast")
    s_sq = sem("p1_sq")
    s_pe = sem("p1_pe")
    s_stat = sem("p1_stat")
    s_sc = sem("p1_sc")
    s_v2 = sem("p1_v2")
    s_bc = sem("p1_bc")
    s_apl = sem("p1_apl")
    s_ptr = sem("p1_ptr")
    s_tev = sem("p1_tev")
    s_xtm = sem("p1_xtm")
    XM = 32 if p.tap_xn is not None else 16

    def tcol_of(g):
        t0 = (g // 4) * 2048 + (g % 4) * 512
        return slice(t0, t0 + 512)

    with nc.Block() as blk:
        @blk.sync
        def _(sync):
            for g in range(NG):
                b, tg = g // 4, g % 4
                if g >= 2:
                    sync.wait_ge(s_apl, g - 1)
                for c in range(8):
                    sync.dma_start(xld[g % 2][c][:],
                                   p.x_fm[b, c * 128:(c + 1) * 128, tg * 512:(tg + 1) * 512]
                                   ).then_inc(s_ld, 16)

        @blk.vector
        def _(v):
            for g in range(NG):
                v.wait_ge(s_ld, 16 * 8 * (g + 1))
                if g >= 1:
                    v.wait_ge(s_pe, 2 * g - 1)
                for c in range(8):
                    ins = v.tensor_copy(xbf[c][:], xld[g % 2][c][:])
                p.vfence(v, ins)
                v.engine_nop().then_inc(s_cast, 1)
                v.wait_ge(s_pe, 2 * g + 2)
                v.tensor_scalar(mean, pA[:], 1.0 / C, None, ALU.mult)
                p.vfence(v, v.tensor_scalar(ex2, pB[:], 1.0 / C, None, ALU.mult))
                p.vfence(v, v.tensor_mul(var, mean, mean))
                p.vfence(v, v.tensor_sub(var, ex2, var))
                p.vfence(v, v.tensor_scalar(var, var, 1e-5, None, ALU.add))
                v.engine_nop().then_inc(s_stat, 1)
                v.wait_ge(s_sc, g + 1)
                p.vfence(v, v.reciprocal(rstd, srt))
                v.engine_nop().then_inc(s_v2, 1)
                v.wait_ge(s_bc, g + 1)
                tcol = tcol_of(g)
                for c in range(8):
                    v.tensor_sub(xld[g % 2][c][:], xld[g % 2][c][:], pC[:])
                    ins = v.tensor_mul(p.xnf[c][:, tcol], xld[g % 2][c][:], pD[:])
                p.vfence(v, ins)
                v.engine_nop().then_inc(s_apl, 1)
                for sub in range(4):
                    gs = 4 * g + sub
                    pr = 2 * (gs % 2)
                    v.wait_ge(s_ptr, gs + 1)
                    if gs >= 4:
                        v.wait_ge(s_xtm, XM * (gs - 3))
                    v.tensor_copy(xtmb[gs % 4][:, 0:512], pE[pr][:])
                    p.vfence(v, v.tensor_copy(xtmb[gs % 4][:, 512:1024], pE[pr + 1][:]))
                    v.engine_nop().then_inc(s_tev, 1)

        @blk.scalar
        def _(s):
            for g in range(NG):
                s.wait_ge(s_ld, 16 * 8 * (g + 1))
                if g >= 1:
                    s.wait_ge(s_pe, 2 * g)
                for c in range(8):
                    ins = s.activation(sqb[c][:], xld[g % 2][c][:], AF.Square)
                p.afence(s, ins)
                s.nop().then_inc(s_sq, 1)
                s.wait_ge(s_stat, g + 1)
                p.afence(s, s.activation(srt, var, AF.Sqrt))
                s.nop().then_inc(s_sc, 1)

        @blk.tensor
        def _(t):
            for g in range(NG):
                t.wait_ge(s_cast, g + 1)
                if g >= 1:
                    t.wait_ge(s_stat, g)
                for c in range(8):
                    ins = t.matmul(pA[:], p.onesb[:], xbf[c][:], start=(c == 0), stop=(c == 7))
                ins.then_inc(s_pe, 1)
                t.wait_ge(s_sq, g + 1)
                for c in range(8):
                    ins = t.matmul(pB[:], p.onesb[:], sqb[c][:], start=(c == 0), stop=(c == 7))
                ins.then_inc(s_pe, 1)
                t.wait_ge(s_v2, g + 1)
                if g >= 1:
                    t.wait_ge(s_apl, g)
                t.matmul(pC[:], p.onesrow[:], mean, start=True, stop=True)
                t.matmul(pD[:], p.onesrow[:], rstd, start=True, stop=True).then_inc(s_bc, 1)
                t.wait_ge(s_apl, g + 1)
                for sub in range(4):
                    gs = 4 * g + sub
                    pr = 2 * (gs % 2)
                    if gs >= 2:
                        t.wait_ge(s_tev, gs - 1)
                    t0 = (g // 4) * 2048 + (g % 4) * 512 + sub * 128
                    scol = slice(t0, t0 + 128)
                    for c in range(8):
                        ins = t.transpose(pE[pr + c // 4][:, (c % 4) * 128:(c % 4) * 128 + 128],
                                          p.xnf[c][:, scol], p.ident[:])
                    ins.then_inc(s_ptr, 1)

        @blk.gpsimd
        def _(gp):
            for gs in range(32):
                gp.wait_ge(s_tev, gs + 1)
                row0 = gs * 128
                gp.dma_start(p.xn_tm[row0:row0 + 128, :], xtmb[gs % 4][:]).then_inc(s_xtm, 16)
                if p.tap_xn is not None:
                    gp.dma_start(p.tap_xn[row0:row0 + 128, :], xtmb[gs % 4][:]).then_inc(s_xtm, 16)
            gp.wait_ge(s_xtm, 32 * XM)
    pctx.close()


def build_p2(p):
    nc, sem = p.nc, p.sem
    pctx = ExitStack()
    sb = lambda name, shape, dt: pctx.enter_context(nc.sbuf_tensor(name, shape, dt))
    ps = lambda name, shape, dt=F32: pctx.enter_context(nc.psum_tensor(name, shape, dt))

    vtm = sb("p2_vtm", [128, 16 * 512], BF16)
    qsb = [sb(f"p2_q{j}", [128, S], BF16) for j in range(4)]
    ksb = [sb(f"p2_k{j}", [128, S], BF16) for j in range(4)]
    wqt = [sb(f"p2_wq{i}", [128, 1024], BF16) for i in range(8)]
    wvr = sb("p2_wvr", [128, 8 * 512], BF16)
    e1sb = [sb(f"p2_e1_{i}", [128, 512], BF16) for i in range(2)]
    e2sb = [sb(f"p2_e2_{i}", [128, 512], BF16) for i in range(2)]
    tmpa = sb("p2_tmpa", [64, 512], F32)
    tmpb = sb("p2_tmpb", [64, 512], F32)
    tmpc = sb("p2_tmpc", [128, 512], F32)
    tmpd = sb("p2_tmpd", [128, 512], F32)
    zr1 = sb("p2_zr1", [1, 512], F32)
    zr2 = sb("p2_zr2", [1, 512], F32)
    zb1 = sb("p2_zb1", [128, 512], F32)
    zb2 = sb("p2_zb2", [128, 512], F32)
    outh = [sb(f"p2_outh{i}", [128, 512], BF16) for i in range(4)]

    bkh = [ps(f"p2_bk{i}", [128, 512], F32) for i in range(8)]
    bk = [h[:] for h in bkh]
    pZ1 = bkh[6][0:1, :]
    pZ2 = bkh[7][0:1, :]

    s_wq = sem("p2_wq")
    s_wv = sem("p2_wv")
    s_qk = sem("p2_qk")
    s_rot = sem("p2_rot")
    s_vmm = sem("p2_vmm")
    s_vev = sem("p2_vev")
    s_s = sem("p2_s")
    s_exp = sem("p2_exp")
    s_pv = sem("p2_pv")
    s_zr = sem("p2_zr")
    s_bc = sem("p2_bc")
    s_cmb = sem("p2_cmb")
    s_st = sem("p2_st")

    def dcol(g, hg, pr):
        return g * 1024 + hg * 256 + pr * 128

    nG = 8
    for G in range(nG):
        b, hg = G // 4, G % 4

        with nc.Block() as blk:
            @blk.sync
            def _(sync, b=b, hg=hg, G=G):
                for j in range(4):
                    h = hg * 4 + j
                    for qg in range(4):
                        jq = G * 16 + j * 4 + qg
                        sync.wait_ge(s_cmb, jq + 1)
                        sync.dma_start(
                            p.attn_fm[b, h * 128:(h + 1) * 128, qg * 512:(qg + 1) * 512],
                            outh[jq % 4][:]).then_inc(s_st, 16)
                sync.wait_ge(s_st, 16 * 16 * (G + 1))

            @blk.gpsimd
            def _(gp, b=b, hg=hg, G=G):
                for c in range(8):
                    gp.dma_start(wvr[:, c * 512:(c + 1) * 512],
                                 p.Wv_rhs[c * 128:(c + 1) * 128,
                                          hg * 512:(hg + 1) * 512]).then_inc(s_wv, 16)
                for pr in range(2):
                    for g in range(4):
                        i = pr * 4 + g
                        gp.dma_start(
                            wqt[i][:].rearrange("p (c f) -> p c f", f=128),
                            p.WqkT[:, dcol(g, hg, pr):dcol(g, hg, pr) + 128]
                            .rearrange("(c p) f -> p c f", p=128)).then_inc(s_wq, 16)

            @blk.tensor
            def _(t, b=b, hg=hg, G=G):
                for tc in range(16):
                    t.wait_ge(s_wv, 16 * 8 * (G + 1))
                    vt = G * 16 + tc
                    if vt >= 2:
                        t.wait_ge(s_vev, vt - 1)
                    if tc == 0 and G >= 1:
                        t.wait_ge(s_pv, 256 * G)
                    for c in range(8):
                        ins = t.matmul(bk[4 + tc % 2],
                                       p.xnf[c][:, b * S + tc * 128:b * S + tc * 128 + 128],
                                       wvr[:, c * 512:(c + 1) * 512],
                                       start=(c == 0), stop=(c == 7))
                    ins.then_inc(s_vmm, 1)
                t.wait_ge(s_wq, 16 * 8 * (G + 1))
                for pr in range(2):
                    for qk in range(2):
                        for tg in range(4):
                            ev = G * 16 + pr * 8 + qk * 4 + tg
                            if ev >= 2:
                                t.wait_ge(s_rot, ev - 1)
                            for gg in range(2):
                                g = 2 * qk + gg
                                i = pr * 4 + g
                                for c in range(8):
                                    ins = t.matmul(bk[(tg % 2) * 2 + gg],
                                                   wqt[i][:, c * 128:(c + 1) * 128],
                                                   p.xnf[c][:, b * S + tg * 512:b * S + tg * 512 + 512],
                                                   start=(c == 0), stop=(c == 7))
                            ins.then_inc(s_qk, 1)
                for j in range(4):
                    t.wait_ge(s_rot, G * 16 + 16)
                    for qg in range(4):
                        jq = G * 16 + j * 4 + qg
                        for kc in range(16):
                            ki = jq * 16 + kc
                            if ki >= 2:
                                t.wait_ge(s_exp, 2 * (ki - 1))
                            t.matmul(bk[kc % 2],
                                     ksb[j][0:64, kc * 128:(kc + 1) * 128],
                                     qsb[j][0:64, qg * 512:(qg + 1) * 512],
                                     start=True, stop=True)
                            t.matmul(bk[2 + kc % 2],
                                     ksb[j][64:128, kc * 128:(kc + 1) * 128],
                                     qsb[j][64:128, qg * 512:(qg + 1) * 512],
                                     start=True, stop=True,
                                     tile_position=(64, 0)).then_inc(s_s, 1)
                            t.wait_ge(s_exp, 2 * (ki + 1))
                            t.wait_ge(s_vev, G * 16 + kc + 1)
                            if kc == 0:
                                if jq >= 1:
                                    t.wait_ge(s_cmb, jq)
                            t.matmul(bk[4], vtm[:, kc * 512 + j * 128:kc * 512 + j * 128 + 128],
                                     e1sb[kc % 2][:], start=(kc == 0), stop=(kc == 15))
                            t.matmul(bk[5], vtm[:, kc * 512 + j * 128:kc * 512 + j * 128 + 128],
                                     e2sb[kc % 2][:], start=(kc == 0), stop=(kc == 15))
                            t.matmul(pZ1, p.onesb[:], e1sb[kc % 2][:],
                                     start=(kc == 0), stop=(kc == 15))
                            t.matmul(pZ2, p.onesb[:], e2sb[kc % 2][:],
                                     start=(kc == 0), stop=(kc == 15)).then_inc(s_pv, 1)
                        t.wait_ge(s_zr, jq + 1)
                        t.matmul(bk[6], p.onesrow[:], zr1[:], start=True, stop=True)
                        t.matmul(bk[7], p.onesrow[:], zr2[:], start=True, stop=True
                                 ).then_inc(s_bc, 1)

            @blk.vector
            def _(v, b=b, hg=hg, G=G):
                for pr in range(2):
                    for qk in range(2):
                        for tg in range(4):
                            need = G * 16 + pr * 8 + qk * 4 + tg + 1
                            v.wait_ge(s_qk, need)
                            tgc = slice(tg * 512, (tg + 1) * 512)
                            for jj in range(2):
                                j = pr * 2 + jj
                                off = jj * 64
                                dst = (qsb if qk == 0 else ksb)[j]
                                A = bkh[(tg % 2) * 2][off:off + 64, :]
                                Bv = bkh[(tg % 2) * 2 + 1][off:off + 64, :]
                                cosx = p.cos[0:64, tgc]
                                sinx = p.sin[0:64, tgc]
                                v.tensor_mul(tmpa[:], A, cosx)
                                v.tensor_mul(tmpb[:], Bv, sinx)
                                v.tensor_sub(dst[0:64, tgc], tmpa[:], tmpb[:])
                                v.tensor_mul(tmpa[:], Bv, cosx)
                                v.tensor_mul(tmpb[:], A, sinx)
                                ins = v.tensor_add(dst[64:128, tgc], tmpa[:], tmpb[:])
                            p.vfence(v, ins)
                            v.engine_nop().then_inc(s_rot, 1)
                for j in range(4):
                    for qg in range(4):
                        jq = G * 16 + j * 4 + qg
                        v.wait_ge(s_pv, jq * 16 + 16)
                        v.reciprocal(zr1[:], pZ1)
                        p.vfence(v, v.reciprocal(zr2[:], pZ2))
                        v.tensor_scalar(zr2[:], zr2[:], p.lam11[:], None, ALU.mult)
                        v.engine_nop().then_inc(s_zr, 1)
                        v.wait_ge(s_bc, jq + 1)
                        v.tensor_copy(zb1[:], bk[6])
                        p.vfence(v, v.tensor_copy(zb2[:], bk[7]))
                        v.tensor_mul(tmpc[:], bk[5], zb2[:])
                        p.vfence(v, v.tensor_mul(tmpd[:], bk[4], zb1[:]))
                        if jq >= 4:
                            v.wait_ge(s_st, 16 * (jq - 3))
                        p.vfence(v, v.tensor_sub(outh[jq % 4][:], tmpd[:], tmpc[:]))
                        v.engine_nop().then_inc(s_cmb, 1)

            @blk.scalar
            def _(s, b=b, hg=hg, G=G):
                for tc in range(16):
                    s.wait_ge(s_vmm, G * 16 + tc + 1)
                    p.afence(s, s.copy(vtm[:, tc * 512:(tc + 1) * 512], bk[4 + tc % 2]))
                    s.nop().then_inc(s_vev, 1)
                for j in range(4):
                    for qg in range(4):
                        jq = G * 16 + j * 4 + qg
                        for kc in range(16):
                            ki = jq * 16 + kc
                            s.wait_ge(s_s, ki + 1)
                            if ki >= 2:
                                s.wait_ge(s_pv, ki - 1)
                            s.activation(e1sb[kc % 2][:], bk[kc % 2], AF.Exp,
                                         bias=p.biasC[:], scale=0.125)
                            p.afence(s, s.activation(e2sb[kc % 2][:], bk[2 + kc % 2],
                                                     AF.Exp, bias=p.biasC[:], scale=0.125))
                            s.nop().then_inc(s_exp, 2)
    pctx.close()


def build_p3(p):
    nc, sem = p.nc, p.sem
    pctx = ExitStack()
    sb = lambda name, shape, dt: pctx.enter_context(nc.sbuf_tensor(name, shape, dt))
    ps = lambda name, shape, dt=F32: pctx.enter_context(nc.psum_tensor(name, shape, dt))

    wpj = [sb(f"p3_w{d}", [128, 1024], BF16) for d in range(16)]
    arb = [[sb(f"p3_a{i}_{d}", [128, 512], BF16) for d in range(16)] for i in range(2)]
    ob = [sb(f"p3_o{i}", [128, 512], F32) for i in range(4)]
    pk = [h[:] for h in [ps(f"p3_p{i}", [128, 512], F32) for i in range(2)]]

    s_w = sem("p3_w")
    s_a = sem("p3_a")
    s_mm = sem("p3_mm")
    s_ev = sem("p3_ev")
    s_st = sem("p3_st")

    with nc.Block() as blk:
        @blk.gpsimd
        def _(gp):
            for d in range(16):
                gp.dma_start(wpj[d][:], p.WprojT[d * 128:(d + 1) * 128, :]
                             ).then_inc(s_w, 16)

        @blk.sync
        def _(sync):
            for bq in range(8):
                b, qg = bq // 4, bq % 4
                if bq >= 2:
                    sync.wait_ge(s_mm, 8 * (bq - 1))
                for d in range(16):
                    sync.dma_start(arb[bq % 2][d][:],
                                   p.attn_fm[b, d * 128:(d + 1) * 128,
                                             qg * 512:(qg + 1) * 512]).then_inc(s_a, 16)

        @blk.tensor
        def _(t):
            t.wait_ge(s_w, 256)
            for bq in range(8):
                t.wait_ge(s_a, 16 * 16 * (bq + 1))
                for c in range(8):
                    m = bq * 8 + c
                    if m >= 2:
                        t.wait_ge(s_ev, m - 1)
                    for d in range(16):
                        ins = t.matmul(pk[m % 2], wpj[d][:, c * 128:(c + 1) * 128],
                                       arb[bq % 2][d][:], start=(d == 0), stop=(d == 15))
                    ins.then_inc(s_mm, 1)

        @blk.vector
        def _(v):
            for bq in range(8):
                for c in range(8):
                    m = bq * 8 + c
                    v.wait_ge(s_mm, m + 1)
                    if m >= 4:
                        v.wait_ge(s_st, 16 * (m - 3))
                    p.vfence(v, v.tensor_copy(ob[m % 4][:], pk[m % 2]))
                    v.engine_nop().then_inc(s_ev, 1)

        @blk.scalar
        def _(sc):
            for bq in range(8):
                b, qg = bq // 4, bq % 4
                for c in range(8):
                    m = bq * 8 + c
                    sc.wait_ge(s_ev, m + 1)
                    sc.dma_start(p.attnproj[b, c * 128:(c + 1) * 128,
                                            qg * 512:(qg + 1) * 512],
                                 ob[m % 4][:]).then_inc(s_st, 16)
            sc.wait_ge(s_st, 16 * 64)
            nst = 64
            if p.tap_attn is not None:
                for b2 in range(B):
                    sc.dma_start(p.tap_attn[b2], p.attn_fm[b2]).then_inc(s_st, 16)
                    nst += 1
            if p.tap_attnproj is not None:
                for b2 in range(B):
                    sc.dma_start(p.tap_attnproj[b2], p.attnproj[b2]).then_inc(s_st, 16)
                    nst += 1
            sc.wait_ge(s_st, 16 * nst)
    pctx.close()


def build_p4(p):
    nc, sem = p.nc, p.sem
    pctx = ExitStack()
    sb = lambda name, shape, dt: pctx.enter_context(nc.sbuf_tensor(name, shape, dt))
    ps = lambda name, shape, dt=F32: pctx.enter_context(nc.psum_tensor(name, shape, dt))

    rtr = sb("p4_rtr", [128, 8 * 16], BF16)
    xstrip = [sb(f"p4_xs{i}", [128, C], BF16) for i in range(2)]
    xfmc = [sb(f"p4_xf{i}", [128, C], BF16) for i in range(2)]
    negm = sb("p4_negm", [128, 1], F32)
    et = sb("p4_et", [128, 16], F32)
    zrow = sb("p4_z", [128, 1], F32)
    ind = sb("p4_ind", [128, 16], F32)
    val = sb("p4_val", [128, 16], F32)
    vmn = sb("p4_vmn", [128, 1], F32)
    tmp16 = sb("p4_tmp16", [128, 16], F32)
    slotp = sb("p4_slotp", [128, 1], F32)
    over = sb("p4_over", [128, 1], F32)
    destf = sb("p4_destf", [128, 1], F32)
    oh = [sb(f"p4_oh{i}", [128, 16], F32) for i in range(32)]
    idxv = [sb(f"p4_idx{i}", [128, 1], F32) for i in range(32)]
    wch = [sb(f"p4_w{i}", [128, 1], F32) for i in range(32)]
    desti = [sb(f"p4_di{i}", [128, 1], I32) for i in range(32)]
    routebuf = [sb(f"p4_rb{i}", [128, 4], F32) for i in range(2)]
    logbuf = [sb(f"p4_lb{i}", [128, 16], F32) for i in range(2)]
    carry = sb("p4_carry", [1, 16], F32)
    gi = [[sb(f"p4_gi{b}_{cs}", [128, 1], I32) for cs in range(4)] for b in range(2)]
    wv = [[sb(f"p4_wv{b}_{cs}", [128, 1], F32) for cs in range(4)] for b in range(2)]
    xetm = [sb(f"p4_xet{cs}", [128, C], BF16) for cs in range(4)]
    xefm = sb("p4_xefm", [128, 8 * CAP], BF16)
    w1s = [[sb(f"p4_w1_{b}_{c}", [128, FF // 2], BF16) for c in range(8)] for b in range(2)]
    w2s = [sb(f"p4_w2_{ff}", [128, C], BF16) for ff in range(32)]
    silu = [sb(f"p4_sil{ff}", [128, CAP], BF16) for ff in range(32)]
    oute = [sb(f"p4_oute{cs}", [128, C], BF16) for cs in range(4)]

    bk = [ps(f"p4_bk{i}", [128, 512], F32) for i in range(6)]
    bb = [ps(f"p4_bb{i}", [128, 1024], BF16) for i in range(2)]
    pL = [bk[4][:, 0:16], bk[5][:, 0:16]]
    pcum = bk[2][:, 0:16]
    ptot = bk[3][0:1, 0:16]
    pH = [bk[0][:], bk[1][:]]
    pO = [bk[2][:], bk[3][:], bk[4][:], bk[5][:]]

    s_init = sem("p4_init"); s_rtr = sem("p4_rtr")
    s_ld = sem("p4_ld"); s_tr = sem("p4_tr"); s_tev = sem("p4_tev")
    s_lg = sem("p4_lg"); s_nm = sem("p4_nm"); s_et = sem("p4_et"); s_oh = sem("p4_oh")
    s_cum = sem("p4_cum"); s_tot = sem("p4_tot"); s_cw = sem("p4_cw"); s_di = sem("p4_di")
    s_sc = sem("p4_sc"); s_gl = sem("p4_gl"); s_ga = sem("p4_ga")
    s_xtr = sem("p4_xtr"); s_xev = sem("p4_xev")
    s_w1p = [sem("p4_w1a"), sem("p4_w1b")]; s_w2 = sem("p4_w2")
    s_gac = [sem(f"p4_gac{i}") for i in range(4)]
    s_sctp = [sem("p4_scta"), sem("p4_sctb")]
    s_h1 = sem("p4_h1"); s_sil = sem("p4_sil"); s_h2 = sem("p4_h2")
    s_oev = sem("p4_oev"); s_sct = sem("p4_sct"); s_tap = sem("p4_tap")
    s_rt = sem("p4_rtp"); s_rt2 = sem("p4_rt2")

    glist_flat = p.glist.rearrange("e (c one) -> (e c) one", one=1)
    wlist_flat = p.wlist.rearrange("e (c one) -> (e c) one", one=1)
    xefm3 = xefm[:].rearrange("p (c f) -> p c f", f=CAP)

    for l in range(NL):
        xsrc = p.xn_tm if l == 0 else p.xt_tm[l - 1]
        xdst = p.xt_tm[l]

        with nc.Block() as blk:
            @blk.gpsimd
            def _(gp, l=l, xdst=xdst):
                gp.dma_start(p.glist[:, :], p.sent_in).then_inc(s_init, 16)
                gp.dma_start(xdst, p.zeros_in).then_inc(s_init, 16)
                gp.dma_start(p.wlist[:, :], p.zeros_in[0:E, 0:CAP]).then_inc(s_init, 16)
                gp.dma_start(rtr[:].rearrange("p (c e) -> p c e", e=16),
                             p.routers_d[l].rearrange("(c p) e -> p c e", p=128)
                             ).then_inc(s_rtr, 16)
                IM = 80 if p.tap_glist is not None else 48
                bc_l = gp.to_reg(E * CAP - 1)
                gp.wait_ge(s_init, IM * l + 48)
                for i in range(32):
                    gp.wait_ge(s_di, 33 * l + i + 2)
                    gp.indirect_dma_start(
                        glist_flat, IndirectOffsetOnAxis(desti[i][:], 0),
                        p.iotatok[:, i:i + 1], None,
                        bounds_check=bc_l, oob_is_err=False).then_inc(s_sc, 16)
                    gp.indirect_dma_start(
                        wlist_flat, IndirectOffsetOnAxis(desti[i][:], 0),
                        wch[i][:], None,
                        bounds_check=bc_l, oob_is_err=False).then_inc(s_sc, 16)
                gp.wait_ge(s_sc, 32 * 32 * (l + 1))
                if p.tap_glist is not None:
                    gp.dma_start(p.tap_glist[l], p.glist[:, :]).then_inc(s_init, 16)
                    gp.dma_start(p.tap_wlist[l], p.wlist[:, :]).then_inc(s_init, 16)
                    gp.wait_ge(s_init, IM * (l + 1))

            @blk.sync
            def _(sync, l=l, xsrc=xsrc):
                for pos in range(33):
                    i = pos % 32
                    gc = 33 * l + pos
                    if gc >= 2:
                        sync.wait_ge(s_tr, gc - 1)
                    sync.dma_start(xstrip[pos % 2][:], xsrc[i * 128:(i + 1) * 128, :]
                                   ).then_inc(s_ld, 16)

            @blk.scalar
            def _(s, l=l):
                for pos in range(33):
                    i = pos % 32
                    gc = 33 * l + pos
                    if l == 0 and p.tap_logits is not None:
                        s.wait_ge(s_lg, gc + 1)
                        if pos >= 2:
                            s.wait_ge(s_rt2, 16 * (pos - 1))
                        s.copy(logbuf[pos % 2][:], pL[pos % 2])
                        s.dma_start(p.tap_logits[i], logbuf[pos % 2][:]).then_inc(s_rt2, 16)
                    s.wait_ge(s_nm, gc + 1)
                    if gc >= 1:
                        s.wait_ge(s_oh, gc)
                    p.afence(s, s.activation(et[:], pL[pos % 2], AF.Exp, bias=negm[:],
                                              accum_out=zrow[:]))
                    s.nop().then_inc(s_et, 1)

            @blk.tensor
            def _(t, l=l):
                def emit_transposes(pos, gc):
                    t.wait_ge(s_ld, 16 * (gc + 1))
                    if gc >= 2:
                        t.wait_ge(s_tev, gc - 1)
                    for c in range(8):
                        ins = t.transpose(bb[pos % 2][:, c * 128:(c + 1) * 128],
                                          xstrip[pos % 2][:, c * 128:(c + 1) * 128],
                                          p.ident[:])
                    ins.then_inc(s_tr, 1)

                def emit_logits(pos, gc):
                    t.wait_ge(s_tev, gc + 1)
                    if pos == 0:
                        t.wait_ge(s_rtr, 16 * (l + 1))
                    if gc >= 2:
                        t.wait_ge(s_oh, gc - 1)
                        t.wait_ge(s_et, gc - 1)
                    for c in range(8):
                        ins = t.matmul(pL[pos % 2], xfmc[pos % 2][:, c * 128:(c + 1) * 128],
                                       rtr[:, c * 16:(c + 1) * 16],
                                       start=(c == 0), stop=(c == 7))
                    ins.then_inc(s_lg, 1)
                for pos in range(33):
                    emit_transposes(pos, 33 * l + pos)
                    if pos >= 1:
                        emit_logits(pos - 1, 33 * l + pos - 1)
                emit_logits(32, 33 * l + 32)
                for pos in range(33):
                    i = max(0, pos - 1)
                    gc = 33 * l + pos
                    t.wait_ge(s_oh, 33 * (l + 1))
                    t.wait_ge(s_cw, gc + 1)
                    t.matmul(pcum, p.u128[:], oh[i][:], start=True, stop=False)
                    t.matmul(pcum, p.onesrow[:], carry[:], start=False, stop=True
                             ).then_inc(s_cum, 1)
                    t.matmul(ptot, p.onesf[:], oh[i][:], start=True,
                             stop=True).then_inc(s_tot, 1)

            @blk.vector
            def _(v, l=l):
                for pos in range(33):
                    i = pos % 32
                    gc = 33 * l + pos
                    v.wait_ge(s_tr, gc + 1)
                    if gc >= 2:
                        v.wait_ge(s_lg, gc - 1)
                    p.vfence(v, v.tensor_copy(xfmc[pos % 2][:], bb[pos % 2][:, 0:1024]))
                    v.engine_nop().then_inc(s_tev, 1)
                    v.wait_ge(s_lg, gc + 1)
                    p.vfence(v, v.reduce_max(negm[:], pL[pos % 2], AX.X, negate=True))
                    v.engine_nop().then_inc(s_nm, 1)
                    v.wait_ge(s_et, gc + 1)
                    p.vfence(v, v.reciprocal(wch[i][:], zrow[:]))
                    p.vfence(v, v.tensor_scalar(tmp16[:], pL[pos % 2], negm[:], 1e8,
                                                ALU.add, ALU.mult))
                    p.vfence(v, v.tensor_scalar(ind[:], tmp16[:], 1.0, 0.0,
                                                ALU.add, ALU.max))
                    p.vfence(v, v.tensor_mul(val[:], ind[:], p.iota16[:]))
                    p.vfence(v, v.reduce_max(vmn[:], val[:], AX.X))
                    p.vfence(v, v.tensor_scalar(over[:], vmn[:], -1.0, 1.0,
                                                ALU.mult, ALU.add))
                    p.vfence(v, v.tensor_scalar(oh[i][:], val[:], over[:], 0.0,
                                                ALU.add, ALU.max))
                    p.vfence(v, v.tensor_scalar(idxv[i][:], vmn[:], -1.0, 1024.0,
                                                ALU.mult, ALU.add))
                    v.engine_nop().then_inc(s_oh, 1)
                for pos in range(33):
                    i = max(0, pos - 1)
                    gc = 33 * l + pos
                    v.wait_ge(s_oh, 33 * (l + 1))
                    if pos <= 1:
                        p.vfence(v, v.memset(carry[:], 0.0))
                    else:
                        v.wait_ge(s_tot, gc)
                        p.vfence(v, v.tensor_add(carry[:], carry[:], ptot))
                    v.engine_nop().then_inc(s_cw, 1)
                    v.wait_ge(s_cum, gc + 1)
                    p.vfence(v, v.tensor_mul(tmp16[:], pcum, oh[i][:]))
                    p.vfence(v, v.reduce_sum(slotp[:], tmp16[:], AX.X))
                    p.vfence(v, v.tensor_scalar(destf[:], idxv[i][:], float(CAP), slotp[:],
                                                ALU.mult, ALU.add))
                    p.vfence(v, v.tensor_scalar(over[:], slotp[:], -(float(CAP) + 0.5), 0.0,
                                                ALU.add, ALU.max))
                    p.vfence(v, v.tensor_scalar(over[:], over[:], 4e4, -1.0,
                                                ALU.mult, ALU.add))
                    p.vfence(v, v.tensor_add(destf[:], destf[:], over[:]))
                    p.vfence(v, v.tensor_copy(desti[i][:], destf[:]))
                    v.engine_nop().then_inc(s_di, 1)

        if l == NL - 1 and p.tap_desti is not None:
            with nc.Block() as blkd:
                @blkd.sync
                def _(sync):
                    n = 0
                    for i2 in range(32):
                        sync.dma_start(p.tap_desti[i2].rearrange("(x y) -> x y", y=1),
                                       desti[i2][:]).then_inc(s_rt, 16)
                        n += 16
                        if p.tap_oh is not None:
                            sync.dma_start(p.tap_oh[i2], oh[i2][:]).then_inc(s_rt, 16)
                            n += 16
                        if p.tap_route is not None:
                            with nc.allow_non_contiguous_dma(reason="debug tap"):
                                sync.dma_start(p.tap_route[i2][:, 0:1], idxv[i2][:]
                                               ).then_inc(s_rt, 16)
                            n += 16
                    if p.tap_stats is not None:
                        slabs = [p.iota16[:], ind[:], val[:], tmp16[:], et[:]]
                        cols = [negm[:], vmn[:], over[:], zrow[:]]
                        for j2, t2 in enumerate(slabs):
                            sync.dma_start(p.tap_stats[j2], t2).then_inc(s_rt, 16)
                            n += 16
                        with nc.allow_non_contiguous_dma(reason="debug tap"):
                            for j2, t2 in enumerate(cols):
                                sync.dma_start(p.tap_stats[5][:, j2:j2 + 1], t2
                                               ).then_inc(s_rt, 16)
                                n += 16
                    sync.wait_ge(s_rt, n)
        if "b" not in P4_PARTS:
            continue
        with nc.Block() as blk:
            @blk.sync
            def _(sync, l=l):
                for e in range(E):
                    u = 16 * l + e
                    for cs in range(4):
                        if u >= 2:
                            sync.wait_ge(s_sctp[u % 2], 16 * 4 * ((u - 2) // 2 + 1))
                        sync.dma_start(gi[e % 2][cs][:],
                                       p.glist[e, cs * 128:(cs + 1) * 128]
                                       .rearrange("(x y) -> x y", y=1)).then_inc(s_gl, 16)
                        sync.dma_start(wv[e % 2][cs][:],
                                       p.wlist[e, cs * 128:(cs + 1) * 128]
                                       .rearrange("(x y) -> x y", y=1)).then_inc(s_gl, 16)

            @blk.gpsimd
            def _(gp, l=l, xsrc=xsrc, xdst=xdst):
                bc_t = gp.to_reg(T - 1)
                for e in range(E):
                    u = 16 * l + e
                    for hf in range(2):
                        hidx = 2 * u + hf
                        if hidx >= 2:
                            gp.wait_ge(s_h1, (hidx - 1) * 16)
                        for c in range(8):
                            gp.dma_start(w1s[hidx % 2][c][:],
                                         p.W1_d[e, c * 128:(c + 1) * 128,
                                                hf * 2048:(hf + 1) * 2048]
                                         ).then_inc(s_w1p[hidx % 2], 16)
                    for ff in range(32):
                        if u >= 1:
                            gp.wait_ge(s_h2, (u - 1) * 64 + 32 + ff + 1)
                        gp.dma_start(w2s[ff][:],
                                     p.W2_d[e, ff * 128:(ff + 1) * 128, :]
                                     ).then_inc(s_w2, 16)
                    for cs in range(4):
                        gp.wait_ge(s_gl, 16 * (8 * u + 2 * cs + 1))
                        if u >= 1:
                            gp.wait_ge(s_xtr, 4 * (u - 1) + cs + 1)
                        gp.indirect_dma_start(
                            xetm[cs][:], None, xsrc,
                            IndirectOffsetOnAxis(gi[e % 2][cs][:], 0),
                            bounds_check=bc_t, oob_is_err=False).then_inc(s_gac[cs], 16)
                    for cs in range(4):
                        gp.wait_ge(s_oev, 8 * u + 2 * (cs + 1))
                        gp.indirect_dma_start(
                            xdst, IndirectOffsetOnAxis(gi[e % 2][cs][:], 0),
                            oute[cs][:], None,
                            bounds_check=bc_t, oob_is_err=False).then_inc(s_sctp[u % 2], 16)
                gp.wait_ge(s_sctp[0], 16 * 4 * 8 * (l + 1))
                gp.wait_ge(s_sctp[1], 16 * 4 * 8 * (l + 1))
                if p.tap_xt[l] is not None:
                    gp.dma_start(p.tap_xt[l], xdst).then_inc(s_tap, 16)
                    gp.wait_ge(s_tap, 16 * (l + 1))

            @blk.tensor
            def _(t, l=l):
                for e in range(E):
                    u = 16 * l + e
                    for cs in range(4):
                        t.wait_ge(s_gac[cs], 16 * (u + 1))
                        if 4 * u + cs >= 2:
                            t.wait_ge(s_xev, 4 * u + cs - 1)
                        for c in range(8):
                            ins = t.transpose(bb[cs % 2][:, c * 128:(c + 1) * 128],
                                              xetm[cs][:, c * 128:(c + 1) * 128],
                                              p.ident[:])
                        ins.then_inc(s_xtr, 1)
                    t.wait_ge(s_xev, 4 * u + 4)
                    for ff in range(32):
                        fidx = 32 * u + ff
                        hidx = 2 * u + ff // 16
                        t.wait_ge(s_w1p[hidx % 2], 16 * 8 * (hidx // 2 + 1))
                        if fidx >= 2:
                            t.wait_ge(s_sil, fidx - 1)
                        for c in range(8):
                            ins = t.matmul(pH[ff % 2],
                                           w1s[hidx % 2][c][:, (ff % 16) * 128:(ff % 16) * 128 + 128],
                                           xefm[:, c * CAP:(c + 1) * CAP],
                                           start=(c == 0), stop=(c == 7))
                        ins.then_inc(s_h1, 1)
                    for ch in range(2):
                        for ff in range(32):
                            t.wait_ge(s_sil, 32 * u + ff + 1)
                            if ch == 0 and ff == 0:
                                t.wait_ge(s_w2, 16 * 32 * (u + 1))
                            if ch == 1 and ff == 0:
                                t.wait_ge(s_oev, 8 * u + 4)
                            for cs in range(2):
                                for dh in range(2):
                                    ins = t.matmul(
                                        pO[2 * cs + dh],
                                        silu[ff][:, (ch * 2 + cs) * 128:(ch * 2 + cs) * 128 + 128],
                                        w2s[ff][:, dh * 512:(dh + 1) * 512],
                                        start=(ff == 0), stop=(ff == 31))
                            ins.then_inc(s_h2, 1)

            @blk.scalar
            def _(s, l=l):
                for e in range(E):
                    u = 16 * l + e
                    for ff in range(32):
                        fidx = 32 * u + ff
                        s.wait_ge(s_h1, fidx + 1)
                        if u >= 1:
                            s.wait_ge(s_h2, (u - 1) * 64 + 32 + ff + 1)
                        p.afence(s, s.activation(silu[ff][:], pH[ff % 2], AF.Silu))
                        s.nop().then_inc(s_sil, 1)

            @blk.vector
            def _(v, l=l):
                for e in range(E):
                    u = 16 * l + e
                    for cs in range(4):
                        v.wait_ge(s_xtr, 4 * u + cs + 1)
                        if u >= 1 and cs == 0:
                            v.wait_ge(s_h1, u * 32)
                        p.vfence(v, v.tensor_copy(xefm3[:, :, cs * 128:cs * 128 + 128],
                                                   bb[cs % 2][:, 0:1024]
                                                   .rearrange("p (c f) -> p c f", f=128)))
                        v.engine_nop().then_inc(s_xev, 1)
                    for ch in range(2):
                        v.wait_ge(s_h2, 64 * u + 32 * (ch + 1))
                        for cs in range(2):
                            sc = ch * 2 + cs
                            if u >= 1:
                                v.wait_ge(s_sctp[(u - 1) % 2], 16 * 4 * ((u - 1) // 2 + 1))
                            for dh in range(2):
                                p.vfence(v, v.tensor_scalar(
                                    oute[sc][:, dh * 512:(dh + 1) * 512],
                                    pO[2 * cs + dh], wv[e % 2][sc][:], None, ALU.mult))
                                v.engine_nop().then_inc(s_oev, 1)
    pctx.close()


def build_p5(p):
    nc, sem = p.nc, p.sem
    pctx = ExitStack()
    sb = lambda name, shape, dt: pctx.enter_context(nc.sbuf_tensor(name, shape, dt))
    ps = lambda name, shape, dt=F32: pctx.enter_context(nc.psum_tensor(name, shape, dt))

    xld = [[sb(f"p5_x{i}_{c}", [128, 512], F32) for c in range(8)] for i in range(2)]
    ald = [[sb(f"p5_a{i}_{c}", [128, 512], F32) for c in range(8)] for i in range(2)]
    fst = [sb(f"p5_f{i}", [128, C], BF16) for i in range(2)]
    ob = [sb(f"p5_o{i}", [128, 512], F32) for i in range(8)]
    pF = [h[:] for h in [ps(f"p5_p{c}", [128, 512], BF16) for c in range(8)]]

    s_ld = sem("p5_ld")
    s_fsp = [sem("p5_fsa"), sem("p5_fsb")]
    s_tr = sem("p5_tr")
    s_add = sem("p5_add")
    s_st = sem("p5_st")

    with nc.Block() as blk:
        @blk.sync
        def _(sync):
            for bq in range(8):
                b, qg = bq // 4, bq % 4
                if bq >= 2:
                    sync.wait_ge(s_add, 8 * (bq - 1))
                for c in range(8):
                    sync.dma_start(xld[bq % 2][c][:],
                                   p.x_fm[b, c * 128:(c + 1) * 128,
                                          qg * 512:(qg + 1) * 512]).then_inc(s_ld, 16)
                    sync.dma_start(ald[bq % 2][c][:],
                                   p.attnproj[b, c * 128:(c + 1) * 128,
                                              qg * 512:(qg + 1) * 512]).then_inc(s_ld, 16)

        @blk.gpsimd
        def _(gp):
            for bq in range(8):
                for sub in range(4):
                    fi = bq * 4 + sub
                    if fi >= 2:
                        gp.wait_ge(s_tr, fi - 1)
                    row0 = (bq // 4) * S + (bq % 4) * 512 + sub * 128
                    gp.dma_start(fst[fi % 2][:], p.xt_tm[NL - 1][row0:row0 + 128, :]
                                 ).then_inc(s_fsp[fi % 2], 16)

        @blk.tensor
        def _(t):
            for bq in range(8):
                if bq >= 1:
                    t.wait_ge(s_add, 8 * bq)
                for sub in range(4):
                    fi = bq * 4 + sub
                    t.wait_ge(s_fsp[fi % 2], 16 * (fi // 2 + 1))
                    for c in range(8):
                        ins = t.transpose(pF[c][:, sub * 128:(sub + 1) * 128] if False else pF[c].tensor[0:128, sub * 128:(sub + 1) * 128],
                                          fst[fi % 2][:, c * 128:(c + 1) * 128],
                                          p.ident[:])
                    ins.then_inc(s_tr, 1)

        @blk.vector
        def _(v):
            for bq in range(8):
                v.wait_ge(s_ld, 16 * 16 * (bq + 1))
                v.wait_ge(s_tr, 4 * (bq + 1))
                for c in range(8):
                    m = bq * 8 + c
                    if m >= 8:
                        v.wait_ge(s_st, 16 * (m - 7))
                    v.tensor_add(ob[m % 8][:], xld[bq % 2][c][:], ald[bq % 2][c][:])
                    p.vfence(v, v.tensor_add(ob[m % 8][:], ob[m % 8][:], pF[c]))
                    v.engine_nop().then_inc(s_add, 1)

        @blk.scalar
        def _(sc):
            for bq in range(8):
                b, qg = bq // 4, bq % 4
                for c in range(8):
                    m = bq * 8 + c
                    sc.wait_ge(s_add, m + 1)
                    sc.dma_start(p.out_fm[b, c * 128:(c + 1) * 128,
                                          qg * 512:(qg + 1) * 512],
                                 ob[m % 8][:]).then_inc(s_st, 16)
            sc.wait_ge(s_st, 16 * 64)
    pctx.close()


def kernel(**inputs):
    in_maps = _host_prep(inputs)
    nc = build()
    res = run_bass_kernel_spmd(nc, in_maps, core_ids=list(range(NCORES)))
    outs = []
    for c in range(NCORES):
        o = np.asarray(res.results[c]["out_fm"], np.float32)
        outs.append(o.transpose(0, 2, 1))
    return np.ascontiguousarray(np.concatenate(outs, 0))


# version 04
# speedup vs baseline: 1.2664x; 1.2490x over previous
import numpy as np
import ml_dtypes
from contextlib import ExitStack

import concourse.bass as bass
import concourse.mybir as mybir
from concourse.bass_utils import run_bass_kernel_spmd
from concourse.bass import IndirectOffsetOnAxis

F32 = mybir.dt.float32
BF16 = mybir.dt.bfloat16
I32 = mybir.dt.int32
AF = mybir.ActivationFunctionType
ALU = mybir.AluOpType
AX = mybir.AxisListType

NCORES = 8
B = 2
S = 2048
C = 1024
NH = 16
HD = 64
T = B * S
E = 16
NL = 4
FF = 4096
CAP = 512
EXP_SHIFT = -32.0
SENT = 1 << 20



def _host_constants():
    pos = np.arange(S, dtype=np.float64)
    inv_freq = 1.0 / 10000.0 ** (np.arange(0, 2 * HD, 2, dtype=np.float64) / (2 * HD))
    freqs = np.outer(pos, inv_freq)
    cos0 = np.cos(freqs).astype(np.float32).T
    sin0 = np.sin(freqs).astype(np.float32).T
    cos2 = np.ascontiguousarray(np.concatenate([cos0, cos0], 0))
    sin2 = np.ascontiguousarray(np.concatenate([sin0, sin0], 0))
    ident = np.eye(128, dtype=ml_dtypes.bfloat16)
    u128 = np.triu(np.ones((128, 128), np.float32))
    iota16 = (1024.0 - np.broadcast_to(np.arange(16, dtype=np.float32), (128, 16))).copy()
    iota_tok = (np.arange(32)[None, :] * 128 + np.arange(128)[:, None]).astype(np.int32)
    sent = np.full((E, CAP), SENT, np.int32)
    zeros_tm = np.zeros((T, C), ml_dtypes.bfloat16)
    return dict(cos2=cos2, sin2=sin2, ident=ident, u128=u128, iota16=iota16,
                iota_tok=np.ascontiguousarray(iota_tok), sent=sent, zeros_tm=zeros_tm)


def _host_prep(inputs):
    x = np.asarray(inputs["L0_attn_x"], np.float32)
    Wqkv = np.asarray(inputs["L0_attn_W_qkv"], np.float32)
    Wproj = np.asarray(inputs["L0_attn_W_proj"], np.float32)
    WqkT = np.ascontiguousarray(Wqkv[:4 * C].T)
    j = np.arange(2 * C)
    hh = j // 128
    dv = j % 128
    src = (4 + dv // HD) * C + hh * HD + (dv % HD)
    Wv_rhs = np.ascontiguousarray(Wqkv[src].T)
    WprojT = np.ascontiguousarray(Wproj.T)

    shared = dict(WqkT=WqkT, Wv_rhs=Wv_rhs, WprojT=WprojT,
                  routers=np.asarray(inputs["L0_ffn_routers"], np.float32),
                  W1=np.asarray(inputs["L0_ffn_W1"], np.float32),
                  W2=np.asarray(inputs["L0_ffn_W2"], np.float32),
                  lq1=np.asarray(inputs["L0_attn_lambda_q1"], np.float32).reshape(1, HD),
                  lq2=np.asarray(inputs["L0_attn_lambda_q2"], np.float32).reshape(1, HD),
                  lk1=np.asarray(inputs["L0_attn_lambda_k1"], np.float32).reshape(1, HD),
                  lk2=np.asarray(inputs["L0_attn_lambda_k2"], np.float32).reshape(1, HD),
                  **_host_constants())
    in_maps = []
    for c in range(NCORES):
        m = dict(shared)
        m["x_fm"] = np.ascontiguousarray(x[2 * c:2 * c + 2].transpose(0, 2, 1))
        in_maps.append(m)
    return in_maps


class P:
    pass


P4_PARTS = "ab"


def build(debug_taps=(), phases=("p0", "p1", "p2", "p3", "p4", "p5")):
    p = P()
    nc = bass.Bass("TRN2", target_bir_lowering=False, detect_race_conditions=False)
    p.nc = nc
    p.taps = set(debug_taps)

    tmp_sems = []
    try:
        for _ in range(256):
            tmp_sems.append(nc.alloc_semaphore())
    except Exception:
        pass
    with nc.Block() as _blk0:
        @_blk0.gpsimd
        def _(g):
            g.engine_nop()
            nc.clear_and_free_semaphores(tmp_sems)

    def din(name, shape, dt):
        return nc.dram_tensor(name, shape, dt, kind="ExternalInput").ap()
    p.x_fm = din("x_fm", [B, C, S], F32)
    p.WqkT = din("WqkT", [C, 4 * C], F32)
    p.Wv_rhs = din("Wv_rhs", [C, 2 * C], F32)
    p.WprojT = din("WprojT", [2 * C, C], F32)
    p.routers_d = din("routers", [NL, C, E], F32)
    p.W1_d = din("W1", [E, C, FF], F32)
    p.W2_d = din("W2", [E, FF, C], F32)
    p.lq1 = din("lq1", [1, HD], F32)
    p.lq2 = din("lq2", [1, HD], F32)
    p.lk1 = din("lk1", [1, HD], F32)
    p.lk2 = din("lk2", [1, HD], F32)
    p.cos2 = din("cos2", [128, S], F32)
    p.sin2 = din("sin2", [128, S], F32)
    p.ident_in = din("ident", [128, 128], BF16)
    p.u128_in = din("u128", [128, 128], F32)
    p.iota16_in = din("iota16", [128, 16], F32)
    p.iotatok_in = din("iota_tok", [128, 32], I32)
    p.sent_in = din("sent", [E, CAP], I32)
    p.zeros_in = din("zeros_tm", [T, C], BF16)

    p.out_fm = nc.dram_tensor("out_fm", [B, C, S], F32, kind="ExternalOutput").ap()
    p.xn_tm = nc.dram_tensor("xn_tm", [T, C], BF16).ap()
    p.xt_tm = [nc.dram_tensor(f"xt{i}_tm", [T, C], BF16).ap() for i in range(NL)]
    p.attn_fm = nc.dram_tensor("attn_fm", [B, 2 * C, S], BF16).ap()
    p.attnproj = nc.dram_tensor("attnproj", [B, C, S], F32).ap()
    p.glist = nc.dram_tensor("glist", [E, CAP], I32).ap()
    p.w1b = nc.dram_tensor("w1b", [E, 2, C, FF // 2], BF16).ap()
    p.w2b = nc.dram_tensor("w2b", [E, FF, C], BF16).ap()
    p.wlist = nc.dram_tensor("wlist", [E, CAP], F32).ap()

    def tap(name, shape, dt):
        if name in p.taps:
            return nc.dram_tensor("tap_" + name, shape, dt, kind="ExternalOutput").ap()
        return None
    p.tap_xn = tap("xn", [T, C], BF16)
    p.tap_glist = tap("glist", [NL, E, CAP], I32)
    p.tap_wlist = tap("wlist" if "glist" not in p.taps else "glist_w", [NL, E, CAP], F32)
    if "glist" in p.taps and p.tap_wlist is None:
        p.tap_wlist = nc.dram_tensor("tap_wlist", [NL, E, CAP], F32, kind="ExternalOutput").ap()
    p.tap_xt = [tap(f"xt{i}", [T, C], BF16) for i in range(NL)]
    p.tap_attn = tap("attn", [B, 2 * C, S], BF16)
    p.tap_attnproj = tap("attnproj", [B, C, S], F32)
    p.tap_route = tap("route", [32, 128, 4], F32)
    p.tap_oh = tap("oh", [32, 128, 16], F32)
    p.tap_logits = tap("logits", [32, 128, 16], F32)
    p.tap_xfm = tap("xfm", [32, 128, C], BF16)
    p.tap_xstrip = tap("xstrip", [32, 128, C], BF16)
    p.tap_desti = tap("desti", [32, 128], I32)
    p.tap_stats = tap("stats", [6, 128, 16], F32)

    with ExitStack() as ctx:
        p.ctx = ctx
        p.sb = lambda name, shape, dt: ctx.enter_context(nc.sbuf_tensor(name, shape, dt))
        p.ps = lambda name, shape, dt=F32: ctx.enter_context(nc.psum_tensor(name, shape, dt))
        p.sem = lambda name: ctx.enter_context(nc.semaphore(name))
        p.s_wcast = p.sem("wcast")

        p.s_vf = p.sem("dve_fence")
        p.vf_n = [0]

        def vfence(v, ins):
            p.vf_n[0] += 1
            ins.then_inc(p.s_vf, 1)
            v.wait_ge(p.s_vf, p.vf_n[0])
        p.vfence = vfence
        p.s_af = p.sem("act_fence")
        p.af_n = [0]

        def afence(sc, ins):
            p.af_n[0] += 1
            ins.then_inc(p.s_af, 1)
            sc.wait_ge(p.s_af, p.af_n[0])
        p.afence = afence

        sb = p.sb
        p.guard = sb("guardpad", [128, 3072], BF16)
        p.ident = sb("identsb", [128, 128], BF16)
        p.u128 = sb("u128sb", [128, 128], F32)
        p.iota16 = sb("iota16sb", [128, 16], F32)
        p.iotatok = sb("iotatoksb", [128, 32], I32)
        p.onesb = sb("onesbsb", [128, 1], BF16)
        p.onesf = sb("onesfsb", [128, 1], F32)
        p.onesrow = sb("onesrowsb", [1, 128], F32)
        p.biasC = sb("biasCsb", [128, 1], F32)
        p.lamcol = sb("lamcolsb", [128, 1], F32)
        p.lam11 = sb("lam11sb", [1, 1], F32)
        actx = ExitStack()
        p.xnf = [actx.enter_context(nc.sbuf_tensor(f"xnf{c}", [128, T], BF16))
                 for c in range(8)]
        p.cos = actx.enter_context(nc.sbuf_tensor("cossb", [128, S], F32))
        p.sin = actx.enter_context(nc.sbuf_tensor("sinsb", [128, S], F32))

        p.phases = phases
        if "p0" in phases:
            build_p0(p)
        if "p1" in phases:
            build_p1(p)
        if "p2" in phases:
            build_p2(p)
        if "p3" in phases:
            build_p3(p)
        actx.close()
        if "p4" in phases:
            build_p4(p)
        if "p5" in phases:
            build_p5(p)
    return nc


def build_p0(p):
    nc, sem = p.nc, p.sem
    from contextlib import ExitStack
    pctx = ExitStack()
    sb = lambda name, shape, dt: pctx.enter_context(nc.sbuf_tensor(name, shape, dt))
    ps = lambda name, shape, dt=F32: pctx.enter_context(nc.psum_tensor(name, shape, dt))
    p._p0ctx = pctx
    lt = [sb(f"p0_lam{i}", [1, HD], F32) for i in range(4)]
    e12 = sb("p0_e12", [1, 2], F32)
    pb = ps("p0_bc", [128, 1], F32)
    s_ld = sem("p0_ld")
    s_c = sem("p0_c")

    with nc.Block() as blk:
        @blk.sync
        def _(sync):
            sync.dma_start(p.ident[:], p.ident_in).then_inc(s_ld, 16)
            sync.dma_start(p.u128[:], p.u128_in).then_inc(s_ld, 16)
            sync.dma_start(p.iota16[:], p.iota16_in).then_inc(s_ld, 16)
            sync.dma_start(p.iotatok[:], p.iotatok_in).then_inc(s_ld, 16)
            sync.dma_start(p.cos[:], p.cos2).then_inc(s_ld, 16)
            sync.dma_start(p.sin[:], p.sin2).then_inc(s_ld, 16)
            for i, src in enumerate([p.lq1, p.lk1, p.lq2, p.lk2]):
                sync.dma_start(lt[i][:], src).then_inc(s_ld, 16)

        @blk.vector
        def _(v):
            v.memset(p.onesb[:], 1.0)
            v.memset(p.onesf[:], 1.0)
            v.memset(p.onesrow[:], 1.0)
            v.memset(p.biasC[:], EXP_SHIFT)
            v.wait_ge(s_ld, 160)
            v.tensor_mul(lt[0][:], lt[0][:], lt[1][:])
            p.vfence(v, v.tensor_mul(lt[2][:], lt[2][:], lt[3][:]))
            v.reduce_sum(e12[:, 0:1], lt[0][:], AX.X)
            p.vfence(v, v.reduce_sum(e12[:, 1:2], lt[2][:], AX.X))
            v.engine_nop().then_inc(s_c, 1)

        @blk.scalar
        def _(s):
            s.wait_ge(s_c, 1)
            p.afence(s, s.activation(e12[:], e12[:], AF.Exp))
            s.nop().then_inc(s_c, 1)

        @blk.vector
        def _(v2):
            v2.wait_ge(s_c, 2)
            p.vfence(v2, v2.tensor_sub(p.lam11[:], e12[:, 0:1], e12[:, 1:2]))
            v2.tensor_scalar(p.lam11[:], p.lam11[:], 0.2, None, ALU.add).then_inc(s_c, 1)

        @blk.tensor
        def _(t):
            t.wait_ge(s_c, 3)
            t.matmul(pb[:], p.onesrow[:], p.lam11[:], start=True, stop=True).then_inc(s_c, 1)

        @blk.vector
        def _(v3):
            v3.wait_ge(s_c, 4)
            v3.tensor_copy(p.lamcol[:], pb[:])
    pctx.close()


def build_p1(p):
    nc, sem = p.nc, p.sem
    from contextlib import ExitStack
    pctx = ExitStack()
    sb = lambda name, shape, dt: pctx.enter_context(nc.sbuf_tensor(name, shape, dt))
    ps = lambda name, shape, dt=F32: pctx.enter_context(nc.psum_tensor(name, shape, dt))
    NG = 8
    xld = [[sb(f"p1_x{i}_{c}", [128, 512], F32) for c in range(8)] for i in range(2)]
    xbf = [sb(f"p1_xb{c}", [128, 512], BF16) for c in range(8)]
    sqb = [sb(f"p1_sq{c}", [128, 512], BF16) for c in range(8)]
    stat = sb("p1_stat", [1, 512 * 5], F32)
    mean, ex2, var, srt, rstd = (stat[:, i * 512:(i + 1) * 512] for i in range(5))
    xtmb = [sb(f"p1_xtm{i}", [128, C], BF16) for i in range(4)]
    pA = ps("p1_A", [1, 512])
    pB = ps("p1_B", [1, 512])
    pC = ps("p1_C", [128, 512])
    pD = ps("p1_D", [128, 512])
    pE = [ps(f"p1_E{i}", [128, 512], BF16) for i in range(4)]

    s_ld = sem("p1_ld")
    s_cast = sem("p1_cast")
    s_sq = sem("p1_sq")
    s_pe = sem("p1_pe")
    s_stat = sem("p1_stat")
    s_sc = sem("p1_sc")
    s_v2 = sem("p1_v2")
    s_bc = sem("p1_bc")
    s_apl = sem("p1_apl")
    s_ptr = sem("p1_ptr")
    s_tev = sem("p1_tev")
    s_xtm = sem("p1_xtm")
    XM = 32 if p.tap_xn is not None else 16

    def tcol_of(g):
        t0 = (g // 4) * 2048 + (g % 4) * 512
        return slice(t0, t0 + 512)

    with nc.Block() as blk:
        @blk.sync
        def _(sync):
            for g in range(NG):
                b, tg = g // 4, g % 4
                if g >= 2:
                    sync.wait_ge(s_apl, g - 1)
                for c in range(8):
                    sync.dma_start(xld[g % 2][c][:],
                                   p.x_fm[b, c * 128:(c + 1) * 128, tg * 512:(tg + 1) * 512]
                                   ).then_inc(s_ld, 16)

        @blk.vector
        def _(v):
            for g in range(NG):
                v.wait_ge(s_ld, 16 * 8 * (g + 1))
                if g >= 1:
                    v.wait_ge(s_pe, 2 * g - 1)
                for c in range(8):
                    ins = v.tensor_copy(xbf[c][:], xld[g % 2][c][:])
                p.vfence(v, ins)
                v.engine_nop().then_inc(s_cast, 1)
                v.wait_ge(s_pe, 2 * g + 2)
                v.tensor_scalar(mean, pA[:], 1.0 / C, None, ALU.mult)
                p.vfence(v, v.tensor_scalar(ex2, pB[:], 1.0 / C, None, ALU.mult))
                p.vfence(v, v.tensor_mul(var, mean, mean))
                p.vfence(v, v.tensor_sub(var, ex2, var))
                p.vfence(v, v.tensor_scalar(var, var, 1e-5, None, ALU.add))
                v.engine_nop().then_inc(s_stat, 1)
                v.wait_ge(s_sc, g + 1)
                p.vfence(v, v.reciprocal(rstd, srt))
                v.engine_nop().then_inc(s_v2, 1)
                v.wait_ge(s_bc, g + 1)
                tcol = tcol_of(g)
                for c in range(8):
                    v.tensor_sub(xld[g % 2][c][:], xld[g % 2][c][:], pC[:])
                    ins = v.tensor_mul(p.xnf[c][:, tcol], xld[g % 2][c][:], pD[:])
                p.vfence(v, ins)
                v.engine_nop().then_inc(s_apl, 1)
                for sub in range(4):
                    gs = 4 * g + sub
                    pr = 2 * (gs % 2)
                    v.wait_ge(s_ptr, gs + 1)
                    if gs >= 4:
                        v.wait_ge(s_xtm, XM * (gs - 3))
                    v.tensor_copy(xtmb[gs % 4][:, 0:512], pE[pr][:])
                    p.vfence(v, v.tensor_copy(xtmb[gs % 4][:, 512:1024], pE[pr + 1][:]))
                    v.engine_nop().then_inc(s_tev, 1)

        @blk.scalar
        def _(s):
            for g in range(NG):
                s.wait_ge(s_ld, 16 * 8 * (g + 1))
                if g >= 1:
                    s.wait_ge(s_pe, 2 * g)
                for c in range(8):
                    ins = s.activation(sqb[c][:], xld[g % 2][c][:], AF.Square)
                p.afence(s, ins)
                s.nop().then_inc(s_sq, 1)
                s.wait_ge(s_stat, g + 1)
                p.afence(s, s.activation(srt, var, AF.Sqrt))
                s.nop().then_inc(s_sc, 1)

        @blk.tensor
        def _(t):
            for g in range(NG):
                t.wait_ge(s_cast, g + 1)
                if g >= 1:
                    t.wait_ge(s_stat, g)
                for c in range(8):
                    ins = t.matmul(pA[:], p.onesb[:], xbf[c][:], start=(c == 0), stop=(c == 7))
                ins.then_inc(s_pe, 1)
                t.wait_ge(s_sq, g + 1)
                for c in range(8):
                    ins = t.matmul(pB[:], p.onesb[:], sqb[c][:], start=(c == 0), stop=(c == 7))
                ins.then_inc(s_pe, 1)
                t.wait_ge(s_v2, g + 1)
                if g >= 1:
                    t.wait_ge(s_apl, g)
                t.matmul(pC[:], p.onesrow[:], mean, start=True, stop=True)
                t.matmul(pD[:], p.onesrow[:], rstd, start=True, stop=True).then_inc(s_bc, 1)
                t.wait_ge(s_apl, g + 1)
                for sub in range(4):
                    gs = 4 * g + sub
                    pr = 2 * (gs % 2)
                    if gs >= 2:
                        t.wait_ge(s_tev, gs - 1)
                    t0 = (g // 4) * 2048 + (g % 4) * 512 + sub * 128
                    scol = slice(t0, t0 + 128)
                    for c in range(8):
                        ins = t.transpose(pE[pr + c // 4][:, (c % 4) * 128:(c % 4) * 128 + 128],
                                          p.xnf[c][:, scol], p.ident[:])
                    ins.then_inc(s_ptr, 1)

        @blk.gpsimd
        def _(gp):
            for gs in range(32):
                gp.wait_ge(s_tev, gs + 1)
                row0 = gs * 128
                gp.dma_start(p.xn_tm[row0:row0 + 128, :], xtmb[gs % 4][:]).then_inc(s_xtm, 16)
                if p.tap_xn is not None:
                    gp.dma_start(p.tap_xn[row0:row0 + 128, :], xtmb[gs % 4][:]).then_inc(s_xtm, 16)
            gp.wait_ge(s_xtm, 32 * XM)
    pctx.close()


def build_p2(p):
    nc, sem = p.nc, p.sem
    pctx = ExitStack()
    sb = lambda name, shape, dt: pctx.enter_context(nc.sbuf_tensor(name, shape, dt))
    ps = lambda name, shape, dt=F32: pctx.enter_context(nc.psum_tensor(name, shape, dt))

    vtm = sb("p2_vtm", [128, 16 * 512], BF16)
    qsb = [sb(f"p2_q{j}", [128, S], BF16) for j in range(4)]
    ksb = [sb(f"p2_k{j}", [128, S], BF16) for j in range(4)]
    wqt = [sb(f"p2_wq{i}", [128, 1024], BF16) for i in range(8)]
    wvr = sb("p2_wvr", [128, 8 * 512], BF16)
    e1sb = [sb(f"p2_e1_{i}", [128, 512], BF16) for i in range(2)]
    e2sb = [sb(f"p2_e2_{i}", [128, 512], BF16) for i in range(2)]
    tmpa = sb("p2_tmpa", [64, 512], F32)
    tmpb = sb("p2_tmpb", [64, 512], F32)
    tmpc = sb("p2_tmpc", [128, 512], F32)
    tmpd = sb("p2_tmpd", [128, 512], F32)
    zr1 = sb("p2_zr1", [1, 512], F32)
    zr2 = sb("p2_zr2", [1, 512], F32)
    zb1 = sb("p2_zb1", [128, 512], F32)
    zb2 = sb("p2_zb2", [128, 512], F32)
    outh = [sb(f"p2_outh{i}", [128, 512], BF16) for i in range(4)]

    bkh = [ps(f"p2_bk{i}", [128, 512], F32) for i in range(8)]
    bk = [h[:] for h in bkh]
    pZ1 = bkh[6][0:1, :]
    pZ2 = bkh[7][0:1, :]

    s_wq = sem("p2_wq")
    s_wv = sem("p2_wv")
    s_qk = sem("p2_qk")
    s_rot = sem("p2_rot")
    s_vmm = sem("p2_vmm")
    s_vev = sem("p2_vev")
    s_s = sem("p2_s")
    s_exp = sem("p2_exp")
    s_pv = sem("p2_pv")
    s_zr = sem("p2_zr")
    s_bc = sem("p2_bc")
    s_cmb = sem("p2_cmb")
    s_st = sem("p2_st")

    def dcol(g, hg, pr):
        return g * 1024 + hg * 256 + pr * 128

    nG = 8
    for G in range(nG):
        b, hg = G // 4, G % 4

        with nc.Block() as blk:
            @blk.sync
            def _(sync, b=b, hg=hg, G=G):
                for j in range(4):
                    h = hg * 4 + j
                    for qg in range(4):
                        jq = G * 16 + j * 4 + qg
                        sync.wait_ge(s_cmb, jq + 1)
                        sync.dma_start(
                            p.attn_fm[b, h * 128:(h + 1) * 128, qg * 512:(qg + 1) * 512],
                            outh[jq % 4][:]).then_inc(s_st, 16)
                sync.wait_ge(s_st, 16 * 16 * (G + 1))

            @blk.gpsimd
            def _(gp, b=b, hg=hg, G=G):
                for c in range(8):
                    gp.dma_start(wvr[:, c * 512:(c + 1) * 512],
                                 p.Wv_rhs[c * 128:(c + 1) * 128,
                                          hg * 512:(hg + 1) * 512]).then_inc(s_wv, 16)
                for pr in range(2):
                    for g in range(4):
                        i = pr * 4 + g
                        gp.dma_start(
                            wqt[i][:].rearrange("p (c f) -> p c f", f=128),
                            p.WqkT[:, dcol(g, hg, pr):dcol(g, hg, pr) + 128]
                            .rearrange("(c p) f -> p c f", p=128)).then_inc(s_wq, 16)
                for e2 in (2 * G, 2 * G + 1):
                    for hf2 in range(2):
                        gp.dma_start(p.w1b[e2, hf2],
                                     p.W1_d[e2][:, hf2 * 2048:(hf2 + 1) * 2048]
                                     ).then_inc(p.s_wcast, 16)
                    gp.dma_start(p.w2b[e2], p.W2_d[e2]).then_inc(p.s_wcast, 16)

            @blk.tensor
            def _(t, b=b, hg=hg, G=G):
                for tc in range(16):
                    t.wait_ge(s_wv, 16 * 8 * (G + 1))
                    vt = G * 16 + tc
                    if vt >= 2:
                        t.wait_ge(s_vev, vt - 1)
                    if tc == 0 and G >= 1:
                        t.wait_ge(s_pv, 256 * G)
                    for c in range(8):
                        ins = t.matmul(bk[4 + tc % 2],
                                       p.xnf[c][:, b * S + tc * 128:b * S + tc * 128 + 128],
                                       wvr[:, c * 512:(c + 1) * 512],
                                       start=(c == 0), stop=(c == 7))
                    ins.then_inc(s_vmm, 1)
                t.wait_ge(s_wq, 16 * 8 * (G + 1))
                for pr in range(2):
                    for qk in range(2):
                        for tg in range(4):
                            ev = G * 16 + pr * 8 + qk * 4 + tg
                            if ev >= 2:
                                t.wait_ge(s_rot, ev - 1)
                            for gg in range(2):
                                g = 2 * qk + gg
                                i = pr * 4 + g
                                for c in range(8):
                                    ins = t.matmul(bk[(tg % 2) * 2 + gg],
                                                   wqt[i][:, c * 128:(c + 1) * 128],
                                                   p.xnf[c][:, b * S + tg * 512:b * S + tg * 512 + 512],
                                                   start=(c == 0), stop=(c == 7))
                            ins.then_inc(s_qk, 1)
                for j in range(4):
                    t.wait_ge(s_rot, G * 16 + 16)
                    for qg in range(4):
                        jq = G * 16 + j * 4 + qg
                        for kc in range(16):
                            ki = jq * 16 + kc
                            if ki >= 2:
                                t.wait_ge(s_exp, 2 * (ki - 1))
                            t.matmul(bk[kc % 2],
                                     ksb[j][0:64, kc * 128:(kc + 1) * 128],
                                     qsb[j][0:64, qg * 512:(qg + 1) * 512],
                                     start=True, stop=True)
                            t.matmul(bk[2 + kc % 2],
                                     ksb[j][64:128, kc * 128:(kc + 1) * 128],
                                     qsb[j][64:128, qg * 512:(qg + 1) * 512],
                                     start=True, stop=True,
                                     tile_position=(64, 0)).then_inc(s_s, 1)
                            t.wait_ge(s_exp, 2 * (ki + 1))
                            t.wait_ge(s_vev, G * 16 + kc + 1)
                            if kc == 0:
                                if jq >= 1:
                                    t.wait_ge(s_cmb, jq)
                            t.matmul(bk[4], vtm[:, kc * 512 + j * 128:kc * 512 + j * 128 + 128],
                                     e1sb[kc % 2][:], start=(kc == 0), stop=(kc == 15))
                            t.matmul(bk[5], vtm[:, kc * 512 + j * 128:kc * 512 + j * 128 + 128],
                                     e2sb[kc % 2][:], start=(kc == 0), stop=(kc == 15))
                            t.matmul(pZ1, p.onesb[:], e1sb[kc % 2][:],
                                     start=(kc == 0), stop=(kc == 15))
                            t.matmul(pZ2, p.onesb[:], e2sb[kc % 2][:],
                                     start=(kc == 0), stop=(kc == 15)).then_inc(s_pv, 1)
                        t.wait_ge(s_zr, jq + 1)
                        t.matmul(bk[6], p.onesrow[:], zr1[:], start=True, stop=True)
                        t.matmul(bk[7], p.onesrow[:], zr2[:], start=True, stop=True
                                 ).then_inc(s_bc, 1)

            @blk.vector
            def _(v, b=b, hg=hg, G=G):
                for pr in range(2):
                    for qk in range(2):
                        for tg in range(4):
                            need = G * 16 + pr * 8 + qk * 4 + tg + 1
                            v.wait_ge(s_qk, need)
                            tgc = slice(tg * 512, (tg + 1) * 512)
                            for jj in range(2):
                                j = pr * 2 + jj
                                off = jj * 64
                                dst = (qsb if qk == 0 else ksb)[j]
                                A = bkh[(tg % 2) * 2][off:off + 64, :]
                                Bv = bkh[(tg % 2) * 2 + 1][off:off + 64, :]
                                cosx = p.cos[0:64, tgc]
                                sinx = p.sin[0:64, tgc]
                                v.tensor_mul(tmpa[:], A, cosx)
                                v.tensor_mul(tmpb[:], Bv, sinx)
                                v.tensor_sub(dst[0:64, tgc], tmpa[:], tmpb[:])
                                v.tensor_mul(tmpa[:], Bv, cosx)
                                v.tensor_mul(tmpb[:], A, sinx)
                                ins = v.tensor_add(dst[64:128, tgc], tmpa[:], tmpb[:])
                            p.vfence(v, ins)
                            v.engine_nop().then_inc(s_rot, 1)
                for j in range(4):
                    for qg in range(4):
                        jq = G * 16 + j * 4 + qg
                        v.wait_ge(s_pv, jq * 16 + 16)
                        v.reciprocal(zr1[:], pZ1)
                        p.vfence(v, v.reciprocal(zr2[:], pZ2))
                        v.tensor_scalar(zr2[:], zr2[:], p.lam11[:], None, ALU.mult)
                        v.engine_nop().then_inc(s_zr, 1)
                        v.wait_ge(s_bc, jq + 1)
                        v.tensor_copy(zb1[:], bk[6])
                        p.vfence(v, v.tensor_copy(zb2[:], bk[7]))
                        v.tensor_mul(tmpc[:], bk[5], zb2[:])
                        p.vfence(v, v.tensor_mul(tmpd[:], bk[4], zb1[:]))
                        if jq >= 4:
                            v.wait_ge(s_st, 16 * (jq - 3))
                        p.vfence(v, v.tensor_sub(outh[jq % 4][:], tmpd[:], tmpc[:]))
                        v.engine_nop().then_inc(s_cmb, 1)

            @blk.scalar
            def _(s, b=b, hg=hg, G=G):
                for tc in range(16):
                    s.wait_ge(s_vmm, G * 16 + tc + 1)
                    p.afence(s, s.copy(vtm[:, tc * 512:(tc + 1) * 512], bk[4 + tc % 2]))
                    s.nop().then_inc(s_vev, 1)
                for j in range(4):
                    for qg in range(4):
                        jq = G * 16 + j * 4 + qg
                        for kc in range(16):
                            ki = jq * 16 + kc
                            s.wait_ge(s_s, ki + 1)
                            if ki >= 2:
                                s.wait_ge(s_pv, ki - 1)
                            s.activation(e1sb[kc % 2][:], bk[kc % 2], AF.Exp,
                                         bias=p.biasC[:], scale=0.125)
                            p.afence(s, s.activation(e2sb[kc % 2][:], bk[2 + kc % 2],
                                                     AF.Exp, bias=p.biasC[:], scale=0.125))
                            s.nop().then_inc(s_exp, 2)
    pctx.close()


def build_p3(p):
    nc, sem = p.nc, p.sem
    pctx = ExitStack()
    sb = lambda name, shape, dt: pctx.enter_context(nc.sbuf_tensor(name, shape, dt))
    ps = lambda name, shape, dt=F32: pctx.enter_context(nc.psum_tensor(name, shape, dt))

    wpj = [sb(f"p3_w{d}", [128, 1024], BF16) for d in range(16)]
    arb = [[sb(f"p3_a{i}_{d}", [128, 512], BF16) for d in range(16)] for i in range(2)]
    ob = [sb(f"p3_o{i}", [128, 512], F32) for i in range(4)]
    pk = [h[:] for h in [ps(f"p3_p{i}", [128, 512], F32) for i in range(2)]]

    s_w = sem("p3_w")
    s_a = sem("p3_a")
    s_mm = sem("p3_mm")
    s_ev = sem("p3_ev")
    s_st = sem("p3_st")

    with nc.Block() as blk:
        @blk.gpsimd
        def _(gp):
            for d in range(16):
                gp.dma_start(wpj[d][:], p.WprojT[d * 128:(d + 1) * 128, :]
                             ).then_inc(s_w, 16)

        @blk.sync
        def _(sync):
            for bq in range(8):
                b, qg = bq // 4, bq % 4
                if bq >= 2:
                    sync.wait_ge(s_mm, 8 * (bq - 1))
                for d in range(16):
                    sync.dma_start(arb[bq % 2][d][:],
                                   p.attn_fm[b, d * 128:(d + 1) * 128,
                                             qg * 512:(qg + 1) * 512]).then_inc(s_a, 16)

        @blk.tensor
        def _(t):
            t.wait_ge(s_w, 256)
            for bq in range(8):
                t.wait_ge(s_a, 16 * 16 * (bq + 1))
                for c in range(8):
                    m = bq * 8 + c
                    if m >= 2:
                        t.wait_ge(s_ev, m - 1)
                    for d in range(16):
                        ins = t.matmul(pk[m % 2], wpj[d][:, c * 128:(c + 1) * 128],
                                       arb[bq % 2][d][:], start=(d == 0), stop=(d == 15))
                    ins.then_inc(s_mm, 1)

        @blk.vector
        def _(v):
            for bq in range(8):
                for c in range(8):
                    m = bq * 8 + c
                    v.wait_ge(s_mm, m + 1)
                    if m >= 4:
                        v.wait_ge(s_st, 16 * (m - 3))
                    p.vfence(v, v.tensor_copy(ob[m % 4][:], pk[m % 2]))
                    v.engine_nop().then_inc(s_ev, 1)

        @blk.scalar
        def _(sc):
            for bq in range(8):
                b, qg = bq // 4, bq % 4
                for c in range(8):
                    m = bq * 8 + c
                    sc.wait_ge(s_ev, m + 1)
                    sc.dma_start(p.attnproj[b, c * 128:(c + 1) * 128,
                                            qg * 512:(qg + 1) * 512],
                                 ob[m % 4][:]).then_inc(s_st, 16)
            sc.wait_ge(s_st, 16 * 64)
            nst = 64
            if p.tap_attn is not None:
                for b2 in range(B):
                    sc.dma_start(p.tap_attn[b2], p.attn_fm[b2]).then_inc(s_st, 16)
                    nst += 1
            if p.tap_attnproj is not None:
                for b2 in range(B):
                    sc.dma_start(p.tap_attnproj[b2], p.attnproj[b2]).then_inc(s_st, 16)
                    nst += 1
            sc.wait_ge(s_st, 16 * nst)
    pctx.close()


def build_p4(p):
    nc, sem = p.nc, p.sem
    pctx = ExitStack()
    sb = lambda name, shape, dt: pctx.enter_context(nc.sbuf_tensor(name, shape, dt))
    ps = lambda name, shape, dt=F32: pctx.enter_context(nc.psum_tensor(name, shape, dt))

    rtr = sb("p4_rtr", [128, 8 * 16], BF16)
    xstrip = [sb(f"p4_xs{i}", [128, C], BF16) for i in range(2)]
    xfmc = [sb(f"p4_xf{i}", [128, C], BF16) for i in range(2)]
    negm = sb("p4_negm", [128, 1], F32)
    et = sb("p4_et", [128, 16], F32)
    zrow = sb("p4_z", [128, 1], F32)
    ind = sb("p4_ind", [128, 16], F32)
    val = sb("p4_val", [128, 16], F32)
    vmn = sb("p4_vmn", [128, 1], F32)
    tmp16 = sb("p4_tmp16", [128, 16], F32)
    slotp = sb("p4_slotp", [128, 1], F32)
    over = sb("p4_over", [128, 1], F32)
    destf = sb("p4_destf", [128, 1], F32)
    oh = [sb(f"p4_oh{i}", [128, 16], F32) for i in range(32)]
    idxv = [sb(f"p4_idx{i}", [128, 1], F32) for i in range(32)]
    wch = [sb(f"p4_w{i}", [128, 1], F32) for i in range(32)]
    desti = [sb(f"p4_di{i}", [128, 1], I32) for i in range(32)]
    routebuf = [sb(f"p4_rb{i}", [128, 4], F32) for i in range(2)]
    logbuf = [sb(f"p4_lb{i}", [128, 16], F32) for i in range(2)]
    carry = sb("p4_carry", [1, 16], F32)
    gi = [[sb(f"p4_gi{b}_{cs}", [128, 1], I32) for cs in range(4)] for b in range(2)]
    wv = [[sb(f"p4_wv{b}_{cs}", [128, 1], F32) for cs in range(4)] for b in range(2)]
    xetm = [sb(f"p4_xet{cs}", [128, C], BF16) for cs in range(4)]
    xefm = sb("p4_xefm", [128, 8 * CAP], BF16)
    w1s = [[sb(f"p4_w1_{b}_{c}", [128, FF // 2], BF16) for c in range(8)] for b in range(2)]
    w2s = [sb(f"p4_w2_{ff}", [128, C], BF16) for ff in range(32)]
    silu = [sb(f"p4_sil{ff}", [128, CAP], BF16) for ff in range(32)]
    oute = [sb(f"p4_oute{cs}", [128, C], BF16) for cs in range(4)]

    bk = [ps(f"p4_bk{i}", [128, 512], F32) for i in range(6)]
    bb = [ps(f"p4_bb{i}", [128, 1024], BF16) for i in range(2)]
    pL = [bk[4][:, 0:16], bk[5][:, 0:16]]
    pcum = bk[2][:, 0:16]
    ptot = bk[3][0:1, 0:16]
    pH = [bk[0][:], bk[1][:]]
    pO = [bk[2][:], bk[3][:], bk[4][:], bk[5][:]]

    s_init = sem("p4_init"); s_rtr = sem("p4_rtr")
    s_ld = sem("p4_ld"); s_tr = sem("p4_tr"); s_tev = sem("p4_tev")
    s_lg = sem("p4_lg"); s_nm = sem("p4_nm"); s_et = sem("p4_et"); s_oh = sem("p4_oh")
    s_cum = sem("p4_cum"); s_tot = sem("p4_tot"); s_cw = sem("p4_cw"); s_di = sem("p4_di")
    s_sc = sem("p4_sc"); s_gl = sem("p4_gl"); s_ga = sem("p4_ga")
    s_xtr = sem("p4_xtr"); s_xev = sem("p4_xev")
    s_w1p = [sem("p4_w1a"), sem("p4_w1b")]; s_w2 = sem("p4_w2")
    s_gac = [sem(f"p4_gac{i}") for i in range(4)]
    s_sctp = [sem("p4_scta"), sem("p4_sctb")]
    s_h1 = sem("p4_h1"); s_sil = sem("p4_sil"); s_h2 = sem("p4_h2")
    s_oev = sem("p4_oev"); s_sct = sem("p4_sct"); s_tap = sem("p4_tap")
    s_rt = sem("p4_rtp"); s_rt2 = sem("p4_rt2")

    glist_flat = p.glist.rearrange("e (c one) -> (e c) one", one=1)
    wlist_flat = p.wlist.rearrange("e (c one) -> (e c) one", one=1)
    xefm3 = xefm[:].rearrange("p (c f) -> p c f", f=CAP)

    for l in range(NL):
        xsrc = p.xn_tm if l == 0 else p.xt_tm[l - 1]
        xdst = p.xt_tm[l]

        with nc.Block() as blk:
            @blk.gpsimd
            def _(gp, l=l, xdst=xdst):
                gp.dma_start(p.glist[:, :], p.sent_in).then_inc(s_init, 16)
                gp.dma_start(xdst, p.zeros_in).then_inc(s_init, 16)
                gp.dma_start(p.wlist[:, :], p.zeros_in[0:E, 0:CAP]).then_inc(s_init, 16)
                gp.dma_start(rtr[:].rearrange("p (c e) -> p c e", e=16),
                             p.routers_d[l].rearrange("(c p) e -> p c e", p=128)
                             ).then_inc(s_rtr, 16)
                if l == 0 and "p2" not in p.phases:
                    for e2 in range(E):
                        for hf2 in range(2):
                            gp.dma_start(p.w1b[e2, hf2],
                                         p.W1_d[e2][:, hf2 * 2048:(hf2 + 1) * 2048]
                                         ).then_inc(p.s_wcast, 16)
                        gp.dma_start(p.w2b[e2], p.W2_d[e2]).then_inc(p.s_wcast, 16)
                IM = 80 if p.tap_glist is not None else 48
                bc_l = gp.to_reg(E * CAP - 1)
                gp.wait_ge(s_init, IM * l + 48)
                for i in range(32):
                    gp.wait_ge(s_di, 33 * l + i + 2)
                    gp.indirect_dma_start(
                        glist_flat, IndirectOffsetOnAxis(desti[i][:], 0),
                        p.iotatok[:, i:i + 1], None,
                        bounds_check=bc_l, oob_is_err=False).then_inc(s_sc, 16)
                    gp.indirect_dma_start(
                        wlist_flat, IndirectOffsetOnAxis(desti[i][:], 0),
                        wch[i][:], None,
                        bounds_check=bc_l, oob_is_err=False).then_inc(s_sc, 16)
                gp.wait_ge(s_sc, 32 * 32 * (l + 1))
                if p.tap_glist is not None:
                    gp.dma_start(p.tap_glist[l], p.glist[:, :]).then_inc(s_init, 16)
                    gp.dma_start(p.tap_wlist[l], p.wlist[:, :]).then_inc(s_init, 16)
                    gp.wait_ge(s_init, IM * (l + 1))

            @blk.sync
            def _(sync, l=l, xsrc=xsrc):
                for pos in range(33):
                    i = pos % 32
                    gc = 33 * l + pos
                    if gc >= 2:
                        sync.wait_ge(s_tr, gc - 1)
                    sync.dma_start(xstrip[pos % 2][:], xsrc[i * 128:(i + 1) * 128, :]
                                   ).then_inc(s_ld, 16)

            @blk.scalar
            def _(s, l=l):
                for pos in range(33):
                    i = pos % 32
                    gc = 33 * l + pos
                    if l == 0 and p.tap_logits is not None:
                        s.wait_ge(s_lg, gc + 1)
                        if pos >= 2:
                            s.wait_ge(s_rt2, 16 * (pos - 1))
                        s.copy(logbuf[pos % 2][:], pL[pos % 2])
                        s.dma_start(p.tap_logits[i], logbuf[pos % 2][:]).then_inc(s_rt2, 16)
                    s.wait_ge(s_nm, gc + 1)
                    if gc >= 1:
                        s.wait_ge(s_oh, gc)
                    p.afence(s, s.activation(et[:], pL[pos % 2], AF.Exp, bias=negm[:],
                                              accum_out=zrow[:]))
                    s.nop().then_inc(s_et, 1)

            @blk.tensor
            def _(t, l=l):
                def emit_transposes(pos, gc):
                    t.wait_ge(s_ld, 16 * (gc + 1))
                    if gc >= 2:
                        t.wait_ge(s_tev, gc - 1)
                    for c in range(8):
                        ins = t.transpose(bb[pos % 2][:, c * 128:(c + 1) * 128],
                                          xstrip[pos % 2][:, c * 128:(c + 1) * 128],
                                          p.ident[:])
                    ins.then_inc(s_tr, 1)

                def emit_logits(pos, gc):
                    t.wait_ge(s_tev, gc + 1)
                    if pos == 0:
                        t.wait_ge(s_rtr, 16 * (l + 1))
                    if gc >= 2:
                        t.wait_ge(s_oh, gc - 1)
                        t.wait_ge(s_et, gc - 1)
                    for c in range(8):
                        ins = t.matmul(pL[pos % 2], xfmc[pos % 2][:, c * 128:(c + 1) * 128],
                                       rtr[:, c * 16:(c + 1) * 16],
                                       start=(c == 0), stop=(c == 7))
                    ins.then_inc(s_lg, 1)
                for pos in range(33):
                    emit_transposes(pos, 33 * l + pos)
                    if pos >= 1:
                        emit_logits(pos - 1, 33 * l + pos - 1)
                emit_logits(32, 33 * l + 32)
                for pos in range(33):
                    i = max(0, pos - 1)
                    gc = 33 * l + pos
                    t.wait_ge(s_oh, 33 * (l + 1))
                    t.wait_ge(s_cw, gc + 1)
                    t.matmul(pcum, p.u128[:], oh[i][:], start=True, stop=False)
                    t.matmul(pcum, p.onesrow[:], carry[:], start=False, stop=True
                             ).then_inc(s_cum, 1)
                    t.matmul(ptot, p.onesf[:], oh[i][:], start=True,
                             stop=True).then_inc(s_tot, 1)

            @blk.vector
            def _(v, l=l):
                for pos in range(33):
                    i = pos % 32
                    gc = 33 * l + pos
                    v.wait_ge(s_tr, gc + 1)
                    if gc >= 2:
                        v.wait_ge(s_lg, gc - 1)
                    p.vfence(v, v.tensor_copy(xfmc[pos % 2][:], bb[pos % 2][:, 0:1024]))
                    v.engine_nop().then_inc(s_tev, 1)
                    v.wait_ge(s_lg, gc + 1)
                    p.vfence(v, v.reduce_max(negm[:], pL[pos % 2], AX.X, negate=True))
                    v.engine_nop().then_inc(s_nm, 1)
                    v.wait_ge(s_et, gc + 1)
                    p.vfence(v, v.reciprocal(wch[i][:], zrow[:]))
                    p.vfence(v, v.tensor_scalar(tmp16[:], pL[pos % 2], negm[:], 1e8,
                                                ALU.add, ALU.mult))
                    p.vfence(v, v.tensor_scalar(ind[:], tmp16[:], 1.0, 0.0,
                                                ALU.add, ALU.max))
                    p.vfence(v, v.tensor_mul(val[:], ind[:], p.iota16[:]))
                    p.vfence(v, v.reduce_max(vmn[:], val[:], AX.X))
                    p.vfence(v, v.tensor_scalar(over[:], vmn[:], -1.0, 1.0,
                                                ALU.mult, ALU.add))
                    p.vfence(v, v.tensor_scalar(oh[i][:], val[:], over[:], 0.0,
                                                ALU.add, ALU.max))
                    p.vfence(v, v.tensor_scalar(idxv[i][:], vmn[:], -1.0, 1024.0,
                                                ALU.mult, ALU.add))
                    v.engine_nop().then_inc(s_oh, 1)
                for pos in range(33):
                    i = max(0, pos - 1)
                    gc = 33 * l + pos
                    v.wait_ge(s_oh, 33 * (l + 1))
                    if pos <= 1:
                        p.vfence(v, v.memset(carry[:], 0.0))
                    else:
                        v.wait_ge(s_tot, gc)
                        p.vfence(v, v.tensor_add(carry[:], carry[:], ptot))
                    v.engine_nop().then_inc(s_cw, 1)
                    v.wait_ge(s_cum, gc + 1)
                    p.vfence(v, v.tensor_mul(tmp16[:], pcum, oh[i][:]))
                    p.vfence(v, v.reduce_sum(slotp[:], tmp16[:], AX.X))
                    p.vfence(v, v.tensor_scalar(destf[:], idxv[i][:], float(CAP), slotp[:],
                                                ALU.mult, ALU.add))
                    p.vfence(v, v.tensor_scalar(over[:], slotp[:], -(float(CAP) + 0.5), 0.0,
                                                ALU.add, ALU.max))
                    p.vfence(v, v.tensor_scalar(over[:], over[:], 4e4, -1.0,
                                                ALU.mult, ALU.add))
                    p.vfence(v, v.tensor_add(destf[:], destf[:], over[:]))
                    p.vfence(v, v.tensor_copy(desti[i][:], destf[:]))
                    v.engine_nop().then_inc(s_di, 1)

        if l == NL - 1 and p.tap_desti is not None:
            with nc.Block() as blkd:
                @blkd.sync
                def _(sync):
                    n = 0
                    for i2 in range(32):
                        sync.dma_start(p.tap_desti[i2].rearrange("(x y) -> x y", y=1),
                                       desti[i2][:]).then_inc(s_rt, 16)
                        n += 16
                        if p.tap_oh is not None:
                            sync.dma_start(p.tap_oh[i2], oh[i2][:]).then_inc(s_rt, 16)
                            n += 16
                        if p.tap_route is not None:
                            with nc.allow_non_contiguous_dma(reason="debug tap"):
                                sync.dma_start(p.tap_route[i2][:, 0:1], idxv[i2][:]
                                               ).then_inc(s_rt, 16)
                            n += 16
                    if p.tap_stats is not None:
                        slabs = [p.iota16[:], ind[:], val[:], tmp16[:], et[:]]
                        cols = [negm[:], vmn[:], over[:], zrow[:]]
                        for j2, t2 in enumerate(slabs):
                            sync.dma_start(p.tap_stats[j2], t2).then_inc(s_rt, 16)
                            n += 16
                        with nc.allow_non_contiguous_dma(reason="debug tap"):
                            for j2, t2 in enumerate(cols):
                                sync.dma_start(p.tap_stats[5][:, j2:j2 + 1], t2
                                               ).then_inc(s_rt, 16)
                                n += 16
                    sync.wait_ge(s_rt, n)
        if "b" not in P4_PARTS:
            continue
        with nc.Block() as blk:
            @blk.sync
            def _(sync, l=l):
                for e in range(E):
                    u = 16 * l + e
                    for cs in range(4):
                        if u >= 2:
                            sync.wait_ge(s_sctp[u % 2], 16 * 4 * ((u - 2) // 2 + 1))
                        sync.dma_start(gi[e % 2][cs][:],
                                       p.glist[e, cs * 128:(cs + 1) * 128]
                                       .rearrange("(x y) -> x y", y=1)).then_inc(s_gl, 16)
                        sync.dma_start(wv[e % 2][cs][:],
                                       p.wlist[e, cs * 128:(cs + 1) * 128]
                                       .rearrange("(x y) -> x y", y=1)).then_inc(s_gl, 16)
                    if l == 0:
                        sync.wait_ge(p.s_wcast, 16 * 3 * (e + 1))
                    for hf in range(2):
                        hidx = 2 * u + hf
                        if hidx >= 2:
                            sync.wait_ge(s_h1, (hidx - 1) * 16)
                        for c in range(8):
                            sync.dma_start(w1s[hidx % 2][c][:],
                                           p.w1b[e, hf, c * 128:(c + 1) * 128, :]
                                           ).then_inc(s_w1p[hidx % 2], 16)
                    for ff in range(32):
                        if u >= 1:
                            sync.wait_ge(s_h2, (u - 1) * 64 + 32 + ff + 1)
                        sync.dma_start(w2s[ff][:],
                                       p.w2b[e, ff * 128:(ff + 1) * 128, :]
                                       ).then_inc(s_w2, 16)

            @blk.gpsimd
            def _(gp, l=l, xsrc=xsrc, xdst=xdst):
                bc_t = gp.to_reg(T - 1)
                for e in range(E):
                    u = 16 * l + e
                    for cs in range(4):
                        gp.wait_ge(s_gl, 16 * (8 * u + 2 * cs + 1))
                        if u >= 1:
                            gp.wait_ge(s_xtr, 4 * (u - 1) + cs + 1)
                        gp.indirect_dma_start(
                            xetm[cs][:], None, xsrc,
                            IndirectOffsetOnAxis(gi[e % 2][cs][:], 0),
                            bounds_check=bc_t, oob_is_err=False).then_inc(s_gac[cs], 16)
                    for cs in range(4):
                        gp.wait_ge(s_oev, 8 * u + 2 * (cs + 1))
                        gp.indirect_dma_start(
                            xdst, IndirectOffsetOnAxis(gi[e % 2][cs][:], 0),
                            oute[cs][:], None,
                            bounds_check=bc_t, oob_is_err=False).then_inc(s_sctp[u % 2], 16)
                gp.wait_ge(s_sctp[0], 16 * 4 * 8 * (l + 1))
                gp.wait_ge(s_sctp[1], 16 * 4 * 8 * (l + 1))
                if p.tap_xt[l] is not None:
                    gp.dma_start(p.tap_xt[l], xdst).then_inc(s_tap, 16)
                    gp.wait_ge(s_tap, 16 * (l + 1))

            @blk.tensor
            def _(t, l=l):
                for e in range(E):
                    u = 16 * l + e
                    for cs in range(4):
                        t.wait_ge(s_gac[cs], 16 * (u + 1))
                        if 4 * u + cs >= 2:
                            t.wait_ge(s_xev, 4 * u + cs - 1)
                        for c in range(8):
                            ins = t.transpose(bb[cs % 2][:, c * 128:(c + 1) * 128],
                                              xetm[cs][:, c * 128:(c + 1) * 128],
                                              p.ident[:])
                        ins.then_inc(s_xtr, 1)
                    t.wait_ge(s_xev, 4 * u + 4)
                    for ff in range(32):
                        fidx = 32 * u + ff
                        hidx = 2 * u + ff // 16
                        t.wait_ge(s_w1p[hidx % 2], 16 * 8 * (hidx // 2 + 1))
                        if fidx >= 2:
                            t.wait_ge(s_sil, fidx - 1)
                        for c in range(8):
                            ins = t.matmul(pH[ff % 2],
                                           w1s[hidx % 2][c][:, (ff % 16) * 128:(ff % 16) * 128 + 128],
                                           xefm[:, c * CAP:(c + 1) * CAP],
                                           start=(c == 0), stop=(c == 7))
                        ins.then_inc(s_h1, 1)
                    for ch in range(2):
                        for ff in range(32):
                            t.wait_ge(s_sil, 32 * u + ff + 1)
                            if ch == 0 and ff == 0:
                                t.wait_ge(s_w2, 16 * 32 * (u + 1))
                            if ch == 1 and ff == 0:
                                t.wait_ge(s_oev, 8 * u + 4)
                            for cs in range(2):
                                for dh in range(2):
                                    ins = t.matmul(
                                        pO[2 * cs + dh],
                                        silu[ff][:, (ch * 2 + cs) * 128:(ch * 2 + cs) * 128 + 128],
                                        w2s[ff][:, dh * 512:(dh + 1) * 512],
                                        start=(ff == 0), stop=(ff == 31))
                            ins.then_inc(s_h2, 1)

            @blk.scalar
            def _(s, l=l):
                for e in range(E):
                    u = 16 * l + e
                    for ff in range(32):
                        fidx = 32 * u + ff
                        s.wait_ge(s_h1, fidx + 1)
                        if u >= 1:
                            s.wait_ge(s_h2, (u - 1) * 64 + 32 + ff + 1)
                        p.afence(s, s.activation(silu[ff][:], pH[ff % 2], AF.Silu))
                        s.nop().then_inc(s_sil, 1)

            @blk.vector
            def _(v, l=l):
                for e in range(E):
                    u = 16 * l + e
                    for cs in range(4):
                        v.wait_ge(s_xtr, 4 * u + cs + 1)
                        if u >= 1 and cs == 0:
                            v.wait_ge(s_h1, u * 32)
                        p.vfence(v, v.tensor_copy(xefm3[:, :, cs * 128:cs * 128 + 128],
                                                   bb[cs % 2][:, 0:1024]
                                                   .rearrange("p (c f) -> p c f", f=128)))
                        v.engine_nop().then_inc(s_xev, 1)
                    for ch in range(2):
                        v.wait_ge(s_h2, 64 * u + 32 * (ch + 1))
                        for cs in range(2):
                            sc = ch * 2 + cs
                            if u >= 1:
                                v.wait_ge(s_sctp[(u - 1) % 2], 16 * 4 * ((u - 1) // 2 + 1))
                            for dh in range(2):
                                p.vfence(v, v.tensor_scalar(
                                    oute[sc][:, dh * 512:(dh + 1) * 512],
                                    pO[2 * cs + dh], wv[e % 2][sc][:], None, ALU.mult))
                                v.engine_nop().then_inc(s_oev, 1)
    pctx.close()


def build_p5(p):
    nc, sem = p.nc, p.sem
    pctx = ExitStack()
    sb = lambda name, shape, dt: pctx.enter_context(nc.sbuf_tensor(name, shape, dt))
    ps = lambda name, shape, dt=F32: pctx.enter_context(nc.psum_tensor(name, shape, dt))

    xld = [[sb(f"p5_x{i}_{c}", [128, 512], F32) for c in range(8)] for i in range(2)]
    ald = [[sb(f"p5_a{i}_{c}", [128, 512], F32) for c in range(8)] for i in range(2)]
    fst = [sb(f"p5_f{i}", [128, C], BF16) for i in range(2)]
    ob = [sb(f"p5_o{i}", [128, 512], F32) for i in range(8)]
    pF = [h[:] for h in [ps(f"p5_p{c}", [128, 512], BF16) for c in range(8)]]

    s_ld = sem("p5_ld")
    s_fsp = [sem("p5_fsa"), sem("p5_fsb")]
    s_tr = sem("p5_tr")
    s_add = sem("p5_add")
    s_st = sem("p5_st")

    with nc.Block() as blk:
        @blk.sync
        def _(sync):
            for bq in range(8):
                b, qg = bq // 4, bq % 4
                if bq >= 2:
                    sync.wait_ge(s_add, 8 * (bq - 1))
                for c in range(8):
                    sync.dma_start(xld[bq % 2][c][:],
                                   p.x_fm[b, c * 128:(c + 1) * 128,
                                          qg * 512:(qg + 1) * 512]).then_inc(s_ld, 16)
                    sync.dma_start(ald[bq % 2][c][:],
                                   p.attnproj[b, c * 128:(c + 1) * 128,
                                              qg * 512:(qg + 1) * 512]).then_inc(s_ld, 16)

        @blk.gpsimd
        def _(gp):
            for bq in range(8):
                for sub in range(4):
                    fi = bq * 4 + sub
                    if fi >= 2:
                        gp.wait_ge(s_tr, fi - 1)
                    row0 = (bq // 4) * S + (bq % 4) * 512 + sub * 128
                    gp.dma_start(fst[fi % 2][:], p.xt_tm[NL - 1][row0:row0 + 128, :]
                                 ).then_inc(s_fsp[fi % 2], 16)

        @blk.tensor
        def _(t):
            for bq in range(8):
                if bq >= 1:
                    t.wait_ge(s_add, 8 * bq)
                for sub in range(4):
                    fi = bq * 4 + sub
                    t.wait_ge(s_fsp[fi % 2], 16 * (fi // 2 + 1))
                    for c in range(8):
                        ins = t.transpose(pF[c][:, sub * 128:(sub + 1) * 128] if False else pF[c].tensor[0:128, sub * 128:(sub + 1) * 128],
                                          fst[fi % 2][:, c * 128:(c + 1) * 128],
                                          p.ident[:])
                    ins.then_inc(s_tr, 1)

        @blk.vector
        def _(v):
            for bq in range(8):
                v.wait_ge(s_ld, 16 * 16 * (bq + 1))
                v.wait_ge(s_tr, 4 * (bq + 1))
                for c in range(8):
                    m = bq * 8 + c
                    if m >= 8:
                        v.wait_ge(s_st, 16 * (m - 7))
                    v.tensor_add(ob[m % 8][:], xld[bq % 2][c][:], ald[bq % 2][c][:])
                    p.vfence(v, v.tensor_add(ob[m % 8][:], ob[m % 8][:], pF[c]))
                    v.engine_nop().then_inc(s_add, 1)

        @blk.scalar
        def _(sc):
            for bq in range(8):
                b, qg = bq // 4, bq % 4
                for c in range(8):
                    m = bq * 8 + c
                    sc.wait_ge(s_add, m + 1)
                    sc.dma_start(p.out_fm[b, c * 128:(c + 1) * 128,
                                          qg * 512:(qg + 1) * 512],
                                 ob[m % 8][:]).then_inc(s_st, 16)
            sc.wait_ge(s_st, 16 * 64)
    pctx.close()


def kernel(**inputs):
    in_maps = _host_prep(inputs)
    nc = build()
    res = run_bass_kernel_spmd(nc, in_maps, core_ids=list(range(NCORES)))
    outs = []
    for c in range(NCORES):
        o = np.asarray(res.results[c]["out_fm"], np.float32)
        outs.append(o.transpose(0, 2, 1))
    return np.ascontiguousarray(np.concatenate(outs, 0))


# version 05
# speedup vs baseline: 1.5253x; 1.2044x over previous
import numpy as np
import ml_dtypes
from contextlib import ExitStack

import concourse.bass as bass
import concourse.mybir as mybir
from concourse.bass_utils import run_bass_kernel_spmd
from concourse.bass import IndirectOffsetOnAxis

F32 = mybir.dt.float32
BF16 = mybir.dt.bfloat16
I32 = mybir.dt.int32
AF = mybir.ActivationFunctionType
ALU = mybir.AluOpType
AX = mybir.AxisListType

NCORES = 8
B = 2
S = 2048
C = 1024
NH = 16
HD = 64
T = B * S
E = 16
NL = 4
FF = 4096
CAP = 512
EXP_SHIFT = -32.0
SENT = 1 << 20



def _host_constants():
    pos = np.arange(S, dtype=np.float64)
    inv_freq = 1.0 / 10000.0 ** (np.arange(0, 2 * HD, 2, dtype=np.float64) / (2 * HD))
    freqs = np.outer(pos, inv_freq)
    cos0 = np.cos(freqs).astype(np.float32).T
    sin0 = np.sin(freqs).astype(np.float32).T
    cos2 = np.ascontiguousarray(np.concatenate([cos0, cos0], 0))
    sin2 = np.ascontiguousarray(np.concatenate([sin0, sin0], 0))
    ident = np.eye(128, dtype=ml_dtypes.bfloat16)
    u128 = np.triu(np.ones((128, 128), np.float32))
    iota16 = (1024.0 - np.broadcast_to(np.arange(16, dtype=np.float32), (128, 16))).copy()
    iota_tok = (np.arange(32)[None, :] * 128 + np.arange(128)[:, None]).astype(np.int32)
    sent = np.full((E, CAP), SENT, np.int32)
    zeros_tm = np.zeros((T, C), ml_dtypes.bfloat16)
    return dict(cos2=cos2, sin2=sin2, ident=ident, u128=u128, iota16=iota16,
                iota_tok=np.ascontiguousarray(iota_tok), sent=sent, zeros_tm=zeros_tm)


def _host_prep(inputs):
    x = np.asarray(inputs["L0_attn_x"], np.float32)
    Wqkv = np.asarray(inputs["L0_attn_W_qkv"], np.float32)
    Wproj = np.asarray(inputs["L0_attn_W_proj"], np.float32)
    WqkT = np.ascontiguousarray(Wqkv[:4 * C].T)
    j = np.arange(2 * C)
    hh = j // 128
    dv = j % 128
    src = (4 + dv // HD) * C + hh * HD + (dv % HD)
    Wv_rhs = np.ascontiguousarray(Wqkv[src].T)
    WprojT = np.ascontiguousarray(Wproj.T)

    shared = dict(WqkT=WqkT, Wv_rhs=Wv_rhs, WprojT=WprojT,
                  routers=np.asarray(inputs["L0_ffn_routers"], np.float32),
                  W1=np.asarray(inputs["L0_ffn_W1"], np.float32),
                  W2=np.asarray(inputs["L0_ffn_W2"], np.float32),
                  lq1=np.asarray(inputs["L0_attn_lambda_q1"], np.float32).reshape(1, HD),
                  lq2=np.asarray(inputs["L0_attn_lambda_q2"], np.float32).reshape(1, HD),
                  lk1=np.asarray(inputs["L0_attn_lambda_k1"], np.float32).reshape(1, HD),
                  lk2=np.asarray(inputs["L0_attn_lambda_k2"], np.float32).reshape(1, HD),
                  **_host_constants())
    in_maps = []
    for c in range(NCORES):
        m = dict(shared)
        m["x_fm"] = np.ascontiguousarray(x[2 * c:2 * c + 2].transpose(0, 2, 1))
        in_maps.append(m)
    return in_maps


class P:
    pass


P4_PARTS = "ab"


def build(debug_taps=(), phases=("p0", "p1", "p2", "p3", "p4", "p5")):
    p = P()
    nc = bass.Bass("TRN2", target_bir_lowering=False, detect_race_conditions=False)
    p.nc = nc
    p.taps = set(debug_taps)

    tmp_sems = []
    try:
        for _ in range(256):
            tmp_sems.append(nc.alloc_semaphore())
    except Exception:
        pass
    with nc.Block() as _blk0:
        @_blk0.gpsimd
        def _(g):
            g.engine_nop()
            nc.clear_and_free_semaphores(tmp_sems)

    def din(name, shape, dt):
        return nc.dram_tensor(name, shape, dt, kind="ExternalInput").ap()
    p.x_fm = din("x_fm", [B, C, S], F32)
    p.WqkT = din("WqkT", [C, 4 * C], F32)
    p.Wv_rhs = din("Wv_rhs", [C, 2 * C], F32)
    p.WprojT = din("WprojT", [2 * C, C], F32)
    p.routers_d = din("routers", [NL, C, E], F32)
    p.W1_d = din("W1", [E, C, FF], F32)
    p.W2_d = din("W2", [E, FF, C], F32)
    p.lq1 = din("lq1", [1, HD], F32)
    p.lq2 = din("lq2", [1, HD], F32)
    p.lk1 = din("lk1", [1, HD], F32)
    p.lk2 = din("lk2", [1, HD], F32)
    p.cos2 = din("cos2", [128, S], F32)
    p.sin2 = din("sin2", [128, S], F32)
    p.ident_in = din("ident", [128, 128], BF16)
    p.u128_in = din("u128", [128, 128], F32)
    p.iota16_in = din("iota16", [128, 16], F32)
    p.iotatok_in = din("iota_tok", [128, 32], I32)
    p.sent_in = din("sent", [E, CAP], I32)
    p.zeros_in = din("zeros_tm", [T, C], BF16)

    p.out_fm = nc.dram_tensor("out_fm", [B, C, S], F32, kind="ExternalOutput").ap()
    p.xn_tm = nc.dram_tensor("xn_tm", [T, C], BF16).ap()
    p.xt_tm = [nc.dram_tensor(f"xt{i}_tm", [T, C], BF16).ap() for i in range(NL)]
    p.attn_fm = nc.dram_tensor("attn_fm", [B, 2 * C, S], BF16).ap()
    p.attnproj = nc.dram_tensor("attnproj", [B, C, S], F32).ap()
    p.glist = nc.dram_tensor("glist", [E, CAP], I32).ap()
    p.w1b = nc.dram_tensor("w1b", [E, 2, C, FF // 2], BF16).ap()
    p.w2b = nc.dram_tensor("w2b", [E, FF, C], BF16).ap()
    p.wlist = nc.dram_tensor("wlist", [E, CAP], F32).ap()

    def tap(name, shape, dt):
        if name in p.taps:
            return nc.dram_tensor("tap_" + name, shape, dt, kind="ExternalOutput").ap()
        return None
    p.tap_xn = tap("xn", [T, C], BF16)
    p.tap_glist = tap("glist", [NL, E, CAP], I32)
    p.tap_wlist = tap("wlist" if "glist" not in p.taps else "glist_w", [NL, E, CAP], F32)
    if "glist" in p.taps and p.tap_wlist is None:
        p.tap_wlist = nc.dram_tensor("tap_wlist", [NL, E, CAP], F32, kind="ExternalOutput").ap()
    p.tap_xt = [tap(f"xt{i}", [T, C], BF16) for i in range(NL)]
    p.tap_attn = tap("attn", [B, 2 * C, S], BF16)
    p.tap_attnproj = tap("attnproj", [B, C, S], F32)
    p.tap_route = tap("route", [32, 128, 4], F32)
    p.tap_oh = tap("oh", [32, 128, 16], F32)
    p.tap_logits = tap("logits", [32, 128, 16], F32)
    p.tap_xfm = tap("xfm", [32, 128, C], BF16)
    p.tap_xstrip = tap("xstrip", [32, 128, C], BF16)
    p.tap_desti = tap("desti", [32, 128], I32)
    p.tap_stats = tap("stats", [6, 128, 16], F32)

    with ExitStack() as ctx:
        p.ctx = ctx
        p.sb = lambda name, shape, dt: ctx.enter_context(nc.sbuf_tensor(name, shape, dt))
        p.ps = lambda name, shape, dt=F32: ctx.enter_context(nc.psum_tensor(name, shape, dt))
        p.sem = lambda name: ctx.enter_context(nc.semaphore(name))
        p.s_wcast = p.sem("wcast")

        p.s_vf = p.sem("dve_fence")
        p.vf_n = [0]

        def vfence(v, ins):
            p.vf_n[0] += 1
            ins.then_inc(p.s_vf, 1)
            v.wait_ge(p.s_vf, p.vf_n[0])
        p.vfence = vfence
        p.s_af = p.sem("act_fence")
        p.af_n = [0]

        def afence(sc, ins):
            p.af_n[0] += 1
            ins.then_inc(p.s_af, 1)
            sc.wait_ge(p.s_af, p.af_n[0])
        p.afence = afence

        sb = p.sb
        p.guard = sb("guardpad", [128, 3072], BF16)
        p.ident = sb("identsb", [128, 128], BF16)
        p.u128 = sb("u128sb", [128, 128], F32)
        p.iota16 = sb("iota16sb", [128, 16], F32)
        p.iotatok = sb("iotatoksb", [128, 32], I32)
        p.onesb = sb("onesbsb", [128, 1], BF16)
        p.onesf = sb("onesfsb", [128, 1], F32)
        p.onesrow = sb("onesrowsb", [1, 128], F32)
        p.biasC = sb("biasCsb", [128, 1], F32)
        p.lamcol = sb("lamcolsb", [128, 1], F32)
        p.lam11 = sb("lam11sb", [1, 1], F32)
        actx = ExitStack()
        p.xnf = [actx.enter_context(nc.sbuf_tensor(f"xnf{c}", [128, T], BF16))
                 for c in range(8)]
        p.cos = actx.enter_context(nc.sbuf_tensor("cossb", [128, S], F32))
        p.sin = actx.enter_context(nc.sbuf_tensor("sinsb", [128, S], F32))

        p.phases = phases
        if "p0" in phases:
            build_p0(p)
        if "p1" in phases:
            build_p1(p)
        if "p2" in phases:
            build_p2(p)
        if "p3" in phases:
            build_p3(p)
        actx.close()
        if "p4" in phases:
            build_p4(p)
        if "p5" in phases:
            build_p5(p)
    return nc


def build_p0(p):
    nc, sem = p.nc, p.sem
    from contextlib import ExitStack
    pctx = ExitStack()
    sb = lambda name, shape, dt: pctx.enter_context(nc.sbuf_tensor(name, shape, dt))
    ps = lambda name, shape, dt=F32: pctx.enter_context(nc.psum_tensor(name, shape, dt))
    p._p0ctx = pctx
    lt = [sb(f"p0_lam{i}", [1, HD], F32) for i in range(4)]
    e12 = sb("p0_e12", [1, 2], F32)
    pb = ps("p0_bc", [128, 1], F32)
    s_ld = sem("p0_ld")
    s_c = sem("p0_c")

    with nc.Block() as blk:
        @blk.sync
        def _(sync):
            sync.dma_start(p.ident[:], p.ident_in).then_inc(s_ld, 16)
            sync.dma_start(p.u128[:], p.u128_in).then_inc(s_ld, 16)
            sync.dma_start(p.iota16[:], p.iota16_in).then_inc(s_ld, 16)
            sync.dma_start(p.iotatok[:], p.iotatok_in).then_inc(s_ld, 16)
            sync.dma_start(p.cos[:], p.cos2).then_inc(s_ld, 16)
            sync.dma_start(p.sin[:], p.sin2).then_inc(s_ld, 16)
            for i, src in enumerate([p.lq1, p.lk1, p.lq2, p.lk2]):
                sync.dma_start(lt[i][:], src).then_inc(s_ld, 16)

        @blk.vector
        def _(v):
            v.memset(p.onesb[:], 1.0)
            v.memset(p.onesf[:], 1.0)
            v.memset(p.onesrow[:], 1.0)
            v.memset(p.biasC[:], EXP_SHIFT)
            v.wait_ge(s_ld, 160)
            v.tensor_mul(lt[0][:], lt[0][:], lt[1][:])
            p.vfence(v, v.tensor_mul(lt[2][:], lt[2][:], lt[3][:]))
            v.reduce_sum(e12[:, 0:1], lt[0][:], AX.X)
            p.vfence(v, v.reduce_sum(e12[:, 1:2], lt[2][:], AX.X))
            v.engine_nop().then_inc(s_c, 1)

        @blk.scalar
        def _(s):
            s.wait_ge(s_c, 1)
            p.afence(s, s.activation(e12[:], e12[:], AF.Exp))
            s.nop().then_inc(s_c, 1)

        @blk.vector
        def _(v2):
            v2.wait_ge(s_c, 2)
            p.vfence(v2, v2.tensor_sub(p.lam11[:], e12[:, 0:1], e12[:, 1:2]))
            v2.tensor_scalar(p.lam11[:], p.lam11[:], 0.2, None, ALU.add).then_inc(s_c, 1)

        @blk.tensor
        def _(t):
            t.wait_ge(s_c, 3)
            t.matmul(pb[:], p.onesrow[:], p.lam11[:], start=True, stop=True).then_inc(s_c, 1)

        @blk.vector
        def _(v3):
            v3.wait_ge(s_c, 4)
            v3.tensor_copy(p.lamcol[:], pb[:])
    pctx.close()


def build_p1(p):
    nc, sem = p.nc, p.sem
    from contextlib import ExitStack
    pctx = ExitStack()
    sb = lambda name, shape, dt: pctx.enter_context(nc.sbuf_tensor(name, shape, dt))
    ps = lambda name, shape, dt=F32: pctx.enter_context(nc.psum_tensor(name, shape, dt))
    NG = 8
    xld = [[sb(f"p1_x{i}_{c}", [128, 512], F32) for c in range(8)] for i in range(2)]
    xbf = [sb(f"p1_xb{c}", [128, 512], BF16) for c in range(8)]
    sqb = [sb(f"p1_sq{c}", [128, 512], BF16) for c in range(8)]
    stat = sb("p1_stat", [1, 512 * 5], F32)
    mean, ex2, var, srt, rstd = (stat[:, i * 512:(i + 1) * 512] for i in range(5))
    xtmb = [sb(f"p1_xtm{i}", [128, C], BF16) for i in range(4)]
    pA = ps("p1_A", [1, 512])
    pB = ps("p1_B", [1, 512])
    pC = ps("p1_C", [128, 512])
    pD = ps("p1_D", [128, 512])
    pE = [ps(f"p1_E{i}", [128, 512], BF16) for i in range(4)]

    s_ld = sem("p1_ld")
    s_cast = sem("p1_cast")
    s_sq = sem("p1_sq")
    s_pe = sem("p1_pe")
    s_stat = sem("p1_stat")
    s_sc = sem("p1_sc")
    s_v2 = sem("p1_v2")
    s_bc = sem("p1_bc")
    s_apl = sem("p1_apl")
    s_ptr = sem("p1_ptr")
    s_tev = sem("p1_tev")
    s_xtm = sem("p1_xtm")
    XM = 32 if p.tap_xn is not None else 16

    def tcol_of(g):
        t0 = (g // 4) * 2048 + (g % 4) * 512
        return slice(t0, t0 + 512)

    with nc.Block() as blk:
        @blk.sync
        def _(sync):
            for g in range(NG):
                b, tg = g // 4, g % 4
                if g >= 2:
                    sync.wait_ge(s_apl, g - 1)
                for c in range(8):
                    sync.dma_start(xld[g % 2][c][:],
                                   p.x_fm[b, c * 128:(c + 1) * 128, tg * 512:(tg + 1) * 512]
                                   ).then_inc(s_ld, 16)

        @blk.vector
        def _(v):
            for g in range(NG):
                v.wait_ge(s_ld, 16 * 8 * (g + 1))
                if g >= 1:
                    v.wait_ge(s_pe, 2 * g - 1)
                for c in range(8):
                    ins = v.tensor_copy(xbf[c][:], xld[g % 2][c][:])
                p.vfence(v, ins)
                v.engine_nop().then_inc(s_cast, 1)
                v.wait_ge(s_pe, 2 * g + 2)
                v.tensor_scalar(mean, pA[:], 1.0 / C, None, ALU.mult)
                p.vfence(v, v.tensor_scalar(ex2, pB[:], 1.0 / C, None, ALU.mult))
                p.vfence(v, v.tensor_mul(var, mean, mean))
                p.vfence(v, v.tensor_sub(var, ex2, var))
                p.vfence(v, v.tensor_scalar(var, var, 1e-5, None, ALU.add))
                v.engine_nop().then_inc(s_stat, 1)
                v.wait_ge(s_sc, g + 1)
                p.vfence(v, v.reciprocal(rstd, srt))
                v.engine_nop().then_inc(s_v2, 1)
                v.wait_ge(s_bc, g + 1)
                tcol = tcol_of(g)
                for c in range(8):
                    v.tensor_sub(xld[g % 2][c][:], xld[g % 2][c][:], pC[:])
                    ins = v.tensor_mul(p.xnf[c][:, tcol], xld[g % 2][c][:], pD[:])
                p.vfence(v, ins)
                v.engine_nop().then_inc(s_apl, 1)
                for sub in range(4):
                    gs = 4 * g + sub
                    pr = 2 * (gs % 2)
                    v.wait_ge(s_ptr, gs + 1)
                    if gs >= 4:
                        v.wait_ge(s_xtm, XM * (gs - 3))
                    v.tensor_copy(xtmb[gs % 4][:, 0:512], pE[pr][:])
                    p.vfence(v, v.tensor_copy(xtmb[gs % 4][:, 512:1024], pE[pr + 1][:]))
                    v.engine_nop().then_inc(s_tev, 1)

        @blk.scalar
        def _(s):
            for g in range(NG):
                s.wait_ge(s_ld, 16 * 8 * (g + 1))
                if g >= 1:
                    s.wait_ge(s_pe, 2 * g)
                for c in range(8):
                    ins = s.activation(sqb[c][:], xld[g % 2][c][:], AF.Square)
                p.afence(s, ins)
                s.nop().then_inc(s_sq, 1)
                s.wait_ge(s_stat, g + 1)
                p.afence(s, s.activation(srt, var, AF.Sqrt))
                s.nop().then_inc(s_sc, 1)

        @blk.tensor
        def _(t):
            for g in range(NG):
                t.wait_ge(s_cast, g + 1)
                if g >= 1:
                    t.wait_ge(s_stat, g)
                for c in range(8):
                    ins = t.matmul(pA[:], p.onesb[:], xbf[c][:], start=(c == 0), stop=(c == 7))
                ins.then_inc(s_pe, 1)
                t.wait_ge(s_sq, g + 1)
                for c in range(8):
                    ins = t.matmul(pB[:], p.onesb[:], sqb[c][:], start=(c == 0), stop=(c == 7))
                ins.then_inc(s_pe, 1)
                t.wait_ge(s_v2, g + 1)
                if g >= 1:
                    t.wait_ge(s_apl, g)
                t.matmul(pC[:], p.onesrow[:], mean, start=True, stop=True)
                t.matmul(pD[:], p.onesrow[:], rstd, start=True, stop=True).then_inc(s_bc, 1)
                t.wait_ge(s_apl, g + 1)
                for sub in range(4):
                    gs = 4 * g + sub
                    pr = 2 * (gs % 2)
                    if gs >= 2:
                        t.wait_ge(s_tev, gs - 1)
                    t0 = (g // 4) * 2048 + (g % 4) * 512 + sub * 128
                    scol = slice(t0, t0 + 128)
                    for c in range(8):
                        ins = t.transpose(pE[pr + c // 4][:, (c % 4) * 128:(c % 4) * 128 + 128],
                                          p.xnf[c][:, scol], p.ident[:])
                    ins.then_inc(s_ptr, 1)

        @blk.gpsimd
        def _(gp):
            for gs in range(32):
                gp.wait_ge(s_tev, gs + 1)
                row0 = gs * 128
                gp.dma_start(p.xn_tm[row0:row0 + 128, :], xtmb[gs % 4][:]).then_inc(s_xtm, 16)
                if p.tap_xn is not None:
                    gp.dma_start(p.tap_xn[row0:row0 + 128, :], xtmb[gs % 4][:]).then_inc(s_xtm, 16)
            gp.wait_ge(s_xtm, 32 * XM)
    pctx.close()


def build_p2(p):
    nc, sem = p.nc, p.sem
    pctx = ExitStack()
    sb = lambda name, shape, dt: pctx.enter_context(nc.sbuf_tensor(name, shape, dt))
    ps = lambda name, shape, dt=F32: pctx.enter_context(nc.psum_tensor(name, shape, dt))

    vtm = sb("p2_vtm", [128, 16 * 512], BF16)
    qsb = [sb(f"p2_q{j}", [128, S], BF16) for j in range(4)]
    ksb = [sb(f"p2_k{j}", [128, S], BF16) for j in range(4)]
    wqt = [sb(f"p2_wq{i}", [128, 1024], BF16) for i in range(8)]
    wvr = sb("p2_wvr", [128, 8 * 512], BF16)
    e1sb = [sb(f"p2_e1_{i}", [128, 512], BF16) for i in range(2)]
    e2sb = [sb(f"p2_e2_{i}", [128, 512], BF16) for i in range(2)]
    tmpa = sb("p2_tmpa", [64, 512], F32)
    tmpb = sb("p2_tmpb", [64, 512], F32)
    tmpc = sb("p2_tmpc", [128, 512], F32)
    tmpd = sb("p2_tmpd", [128, 512], F32)
    zr1 = sb("p2_zr1", [1, 512], F32)
    zr2 = sb("p2_zr2", [1, 512], F32)
    zb1 = sb("p2_zb1", [128, 512], F32)
    zb2 = sb("p2_zb2", [128, 512], F32)
    outh = [sb(f"p2_outh{i}", [128, 512], BF16) for i in range(4)]

    bkh = [ps(f"p2_bk{i}", [128, 512], F32) for i in range(8)]
    bk = [h[:] for h in bkh]
    pZ1 = bkh[6][0:1, :]
    pZ2 = bkh[7][0:1, :]

    s_wq = sem("p2_wq")
    s_wv = sem("p2_wv")
    s_qk = sem("p2_qk")
    s_rot = sem("p2_rot")
    s_vmm = sem("p2_vmm")
    s_vev = sem("p2_vev")
    s_s = sem("p2_s")
    s_exp = sem("p2_exp")
    s_pv = sem("p2_pv")
    s_zr = sem("p2_zr")
    s_bc = sem("p2_bc")
    s_cmb = sem("p2_cmb")
    s_st = sem("p2_st")

    def dcol(g, hg, pr):
        return g * 1024 + hg * 256 + pr * 128

    nG = 8
    for G in range(nG):
        b, hg = G // 4, G % 4

        with nc.Block() as blk:
            @blk.sync
            def _(sync, b=b, hg=hg, G=G):
                for j in range(4):
                    h = hg * 4 + j
                    for qg in range(4):
                        jq = G * 16 + j * 4 + qg
                        sync.wait_ge(s_cmb, jq + 1)
                        sync.dma_start(
                            p.attn_fm[b, h * 128:(h + 1) * 128, qg * 512:(qg + 1) * 512],
                            outh[jq % 4][:]).then_inc(s_st, 16)
                sync.wait_ge(s_st, 16 * 16 * (G + 1))

            @blk.gpsimd
            def _(gp, b=b, hg=hg, G=G):
                for c in range(8):
                    gp.dma_start(wvr[:, c * 512:(c + 1) * 512],
                                 p.Wv_rhs[c * 128:(c + 1) * 128,
                                          hg * 512:(hg + 1) * 512]).then_inc(s_wv, 16)
                for pr in range(2):
                    for g in range(4):
                        i = pr * 4 + g
                        gp.dma_start(
                            wqt[i][:].rearrange("p (c f) -> p c f", f=128),
                            p.WqkT[:, dcol(g, hg, pr):dcol(g, hg, pr) + 128]
                            .rearrange("(c p) f -> p c f", p=128)).then_inc(s_wq, 16)
                for e2 in (2 * G, 2 * G + 1):
                    for hf2 in range(2):
                        gp.dma_start(p.w1b[e2, hf2],
                                     p.W1_d[e2][:, hf2 * 2048:(hf2 + 1) * 2048]
                                     ).then_inc(p.s_wcast, 16)
                    gp.dma_start(p.w2b[e2], p.W2_d[e2]).then_inc(p.s_wcast, 16)

            @blk.tensor
            def _(t, b=b, hg=hg, G=G):
                for tc in range(16):
                    t.wait_ge(s_wv, 16 * 8 * (G + 1))
                    vt = G * 16 + tc
                    if vt >= 2:
                        t.wait_ge(s_vev, vt - 1)
                    if tc == 0 and G >= 1:
                        t.wait_ge(s_pv, 256 * G)
                    for c in range(8):
                        ins = t.matmul(bk[4 + tc % 2],
                                       p.xnf[c][:, b * S + tc * 128:b * S + tc * 128 + 128],
                                       wvr[:, c * 512:(c + 1) * 512],
                                       start=(c == 0), stop=(c == 7))
                    ins.then_inc(s_vmm, 1)
                t.wait_ge(s_wq, 16 * 8 * (G + 1))
                for pr in range(2):
                    for qk in range(2):
                        for tg in range(4):
                            ev = G * 16 + pr * 8 + qk * 4 + tg
                            if ev >= 2:
                                t.wait_ge(s_rot, ev - 1)
                            for gg in range(2):
                                g = 2 * qk + gg
                                i = pr * 4 + g
                                for c in range(8):
                                    ins = t.matmul(bk[(tg % 2) * 2 + gg],
                                                   wqt[i][:, c * 128:(c + 1) * 128],
                                                   p.xnf[c][:, b * S + tg * 512:b * S + tg * 512 + 512],
                                                   start=(c == 0), stop=(c == 7))
                            ins.then_inc(s_qk, 1)
                t.wait_ge(s_rot, G * 16 + 16)
                items = [(j, qg, kc) for j in range(4) for qg in range(4)
                         for kc in range(16)]

                def emit_scores(j, qg, kc):
                    jq = G * 16 + j * 4 + qg
                    ki = jq * 16 + kc
                    if ki >= 2:
                        t.wait_ge(s_exp, 2 * (ki - 1))
                    t.matmul(bk[kc % 2],
                             ksb[j][0:64, kc * 128:(kc + 1) * 128],
                             qsb[j][0:64, qg * 512:(qg + 1) * 512],
                             start=True, stop=True)
                    t.matmul(bk[2 + kc % 2],
                             ksb[j][64:128, kc * 128:(kc + 1) * 128],
                             qsb[j][64:128, qg * 512:(qg + 1) * 512],
                             start=True, stop=True,
                             tile_position=(64, 0)).then_inc(s_s, 1)

                def emit_pv(j, qg, kc):
                    jq = G * 16 + j * 4 + qg
                    ki = jq * 16 + kc
                    t.wait_ge(s_exp, 2 * (ki + 1))
                    t.wait_ge(s_vev, G * 16 + kc + 1)
                    if kc == 0:
                        if jq >= 1:
                            t.wait_ge(s_cmb, jq)
                    t.matmul(bk[4], vtm[:, kc * 512 + j * 128:kc * 512 + j * 128 + 128],
                             e1sb[kc % 2][:], start=(kc == 0), stop=(kc == 15))
                    t.matmul(bk[5], vtm[:, kc * 512 + j * 128:kc * 512 + j * 128 + 128],
                             e2sb[kc % 2][:], start=(kc == 0), stop=(kc == 15))
                    t.matmul(pZ1, p.onesb[:], e1sb[kc % 2][:],
                             start=(kc == 0), stop=(kc == 15))
                    t.matmul(pZ2, p.onesb[:], e2sb[kc % 2][:],
                             start=(kc == 0), stop=(kc == 15)).then_inc(s_pv, 1)
                    if kc == 15:
                        t.wait_ge(s_zr, jq + 1)
                        t.matmul(bk[6], p.onesrow[:], zr1[:], start=True, stop=True)
                        t.matmul(bk[7], p.onesrow[:], zr2[:], start=True, stop=True
                                 ).then_inc(s_bc, 1)

                for idx in range(len(items) + 1):
                    if idx < len(items):
                        emit_scores(*items[idx])
                    if idx >= 1:
                        emit_pv(*items[idx - 1])

            @blk.vector
            def _(v, b=b, hg=hg, G=G):
                for pr in range(2):
                    for qk in range(2):
                        for tg in range(4):
                            need = G * 16 + pr * 8 + qk * 4 + tg + 1
                            v.wait_ge(s_qk, need)
                            tgc = slice(tg * 512, (tg + 1) * 512)
                            for jj in range(2):
                                j = pr * 2 + jj
                                off = jj * 64
                                dst = (qsb if qk == 0 else ksb)[j]
                                A = bkh[(tg % 2) * 2][off:off + 64, :]
                                Bv = bkh[(tg % 2) * 2 + 1][off:off + 64, :]
                                cosx = p.cos[0:64, tgc]
                                sinx = p.sin[0:64, tgc]
                                v.tensor_mul(tmpa[:], A, cosx)
                                v.tensor_mul(tmpb[:], Bv, sinx)
                                v.tensor_sub(dst[0:64, tgc], tmpa[:], tmpb[:])
                                v.tensor_mul(tmpa[:], Bv, cosx)
                                v.tensor_mul(tmpb[:], A, sinx)
                                ins = v.tensor_add(dst[64:128, tgc], tmpa[:], tmpb[:])
                            p.vfence(v, ins)
                            v.engine_nop().then_inc(s_rot, 1)
                for j in range(4):
                    for qg in range(4):
                        jq = G * 16 + j * 4 + qg
                        v.wait_ge(s_pv, jq * 16 + 16)
                        v.reciprocal(zr1[:], pZ1)
                        p.vfence(v, v.reciprocal(zr2[:], pZ2))
                        v.tensor_scalar(zr2[:], zr2[:], p.lam11[:], None, ALU.mult)
                        v.engine_nop().then_inc(s_zr, 1)
                        v.wait_ge(s_bc, jq + 1)
                        v.tensor_copy(zb1[:], bk[6])
                        p.vfence(v, v.tensor_copy(zb2[:], bk[7]))
                        v.tensor_mul(tmpc[:], bk[5], zb2[:])
                        p.vfence(v, v.tensor_mul(tmpd[:], bk[4], zb1[:]))
                        if jq >= 4:
                            v.wait_ge(s_st, 16 * (jq - 3))
                        p.vfence(v, v.tensor_sub(outh[jq % 4][:], tmpd[:], tmpc[:]))
                        v.engine_nop().then_inc(s_cmb, 1)

            @blk.scalar
            def _(s, b=b, hg=hg, G=G):
                for tc in range(16):
                    s.wait_ge(s_vmm, G * 16 + tc + 1)
                    p.afence(s, s.copy(vtm[:, tc * 512:(tc + 1) * 512], bk[4 + tc % 2]))
                    s.nop().then_inc(s_vev, 1)
                for j in range(4):
                    for qg in range(4):
                        jq = G * 16 + j * 4 + qg
                        for kc in range(16):
                            ki = jq * 16 + kc
                            s.wait_ge(s_s, ki + 1)
                            if ki >= 2:
                                s.wait_ge(s_pv, ki - 1)
                            s.activation(e1sb[kc % 2][:], bk[kc % 2], AF.Exp,
                                         bias=p.biasC[:], scale=0.125)
                            p.afence(s, s.activation(e2sb[kc % 2][:], bk[2 + kc % 2],
                                                     AF.Exp, bias=p.biasC[:], scale=0.125))
                            s.nop().then_inc(s_exp, 2)
    pctx.close()


def build_p3(p):
    nc, sem = p.nc, p.sem
    pctx = ExitStack()
    sb = lambda name, shape, dt: pctx.enter_context(nc.sbuf_tensor(name, shape, dt))
    ps = lambda name, shape, dt=F32: pctx.enter_context(nc.psum_tensor(name, shape, dt))

    wpj = [sb(f"p3_w{d}", [128, 1024], BF16) for d in range(16)]
    arb = [[sb(f"p3_a{i}_{d}", [128, 512], BF16) for d in range(16)] for i in range(2)]
    ob = [sb(f"p3_o{i}", [128, 512], F32) for i in range(4)]
    pk = [h[:] for h in [ps(f"p3_p{i}", [128, 512], F32) for i in range(2)]]

    s_w = sem("p3_w")
    s_a = sem("p3_a")
    s_mm = sem("p3_mm")
    s_ev = sem("p3_ev")
    s_st = sem("p3_st")

    with nc.Block() as blk:
        @blk.gpsimd
        def _(gp):
            for d in range(16):
                gp.dma_start(wpj[d][:], p.WprojT[d * 128:(d + 1) * 128, :]
                             ).then_inc(s_w, 16)

        @blk.sync
        def _(sync):
            for bq in range(8):
                b, qg = bq // 4, bq % 4
                if bq >= 2:
                    sync.wait_ge(s_mm, 8 * (bq - 1))
                for d in range(16):
                    sync.dma_start(arb[bq % 2][d][:],
                                   p.attn_fm[b, d * 128:(d + 1) * 128,
                                             qg * 512:(qg + 1) * 512]).then_inc(s_a, 16)

        @blk.tensor
        def _(t):
            t.wait_ge(s_w, 256)
            for bq in range(8):
                t.wait_ge(s_a, 16 * 16 * (bq + 1))
                for c in range(8):
                    m = bq * 8 + c
                    if m >= 2:
                        t.wait_ge(s_ev, m - 1)
                    for d in range(16):
                        ins = t.matmul(pk[m % 2], wpj[d][:, c * 128:(c + 1) * 128],
                                       arb[bq % 2][d][:], start=(d == 0), stop=(d == 15))
                    ins.then_inc(s_mm, 1)

        @blk.vector
        def _(v):
            for bq in range(8):
                for c in range(8):
                    m = bq * 8 + c
                    v.wait_ge(s_mm, m + 1)
                    if m >= 4:
                        v.wait_ge(s_st, 16 * (m - 3))
                    p.vfence(v, v.tensor_copy(ob[m % 4][:], pk[m % 2]))
                    v.engine_nop().then_inc(s_ev, 1)

        @blk.scalar
        def _(sc):
            for bq in range(8):
                b, qg = bq // 4, bq % 4
                for c in range(8):
                    m = bq * 8 + c
                    sc.wait_ge(s_ev, m + 1)
                    sc.dma_start(p.attnproj[b, c * 128:(c + 1) * 128,
                                            qg * 512:(qg + 1) * 512],
                                 ob[m % 4][:]).then_inc(s_st, 16)
            sc.wait_ge(s_st, 16 * 64)
            nst = 64
            if p.tap_attn is not None:
                for b2 in range(B):
                    sc.dma_start(p.tap_attn[b2], p.attn_fm[b2]).then_inc(s_st, 16)
                    nst += 1
            if p.tap_attnproj is not None:
                for b2 in range(B):
                    sc.dma_start(p.tap_attnproj[b2], p.attnproj[b2]).then_inc(s_st, 16)
                    nst += 1
            sc.wait_ge(s_st, 16 * nst)
    pctx.close()


def build_p4(p):
    nc, sem = p.nc, p.sem
    pctx = ExitStack()
    sb = lambda name, shape, dt: pctx.enter_context(nc.sbuf_tensor(name, shape, dt))
    ps = lambda name, shape, dt=F32: pctx.enter_context(nc.psum_tensor(name, shape, dt))

    rtr = sb("p4_rtr", [128, 8 * 16], BF16)
    xstrip = [sb(f"p4_xs{i}", [128, C], BF16) for i in range(2)]
    xfmc = [sb(f"p4_xf{i}", [128, C], BF16) for i in range(2)]
    negm = sb("p4_negm", [128, 1], F32)
    et = sb("p4_et", [128, 16], F32)
    zrow = sb("p4_z", [128, 1], F32)
    ind = sb("p4_ind", [128, 16], F32)
    val = sb("p4_val", [128, 16], F32)
    vmn = sb("p4_vmn", [128, 1], F32)
    tmp16 = sb("p4_tmp16", [128, 16], F32)
    slotp = sb("p4_slotp", [128, 1], F32)
    over = sb("p4_over", [128, 1], F32)
    destf = sb("p4_destf", [128, 1], F32)
    oh = [sb(f"p4_oh{i}", [128, 16], F32) for i in range(32)]
    idxv = [sb(f"p4_idx{i}", [128, 1], F32) for i in range(32)]
    wch = [sb(f"p4_w{i}", [128, 1], F32) for i in range(32)]
    desti = [sb(f"p4_di{i}", [128, 1], I32) for i in range(32)]
    routebuf = [sb(f"p4_rb{i}", [128, 4], F32) for i in range(2)]
    logbuf = [sb(f"p4_lb{i}", [128, 16], F32) for i in range(2)]
    carry = sb("p4_carry", [1, 16], F32)
    gi = [[sb(f"p4_gi{b}_{cs}", [128, 1], I32) for cs in range(4)] for b in range(2)]
    wv = [[sb(f"p4_wv{b}_{cs}", [128, 1], F32) for cs in range(4)] for b in range(2)]
    xetm = [sb(f"p4_xet{cs}", [128, C], BF16) for cs in range(4)]
    xefm = sb("p4_xefm", [128, 8 * CAP], BF16)
    w1s = [[sb(f"p4_w1_{b}_{c}", [128, FF // 2], BF16) for c in range(8)] for b in range(2)]
    w2s = [sb(f"p4_w2_{ff}", [128, C], BF16) for ff in range(32)]
    silu = [sb(f"p4_sil{ff}", [128, CAP], BF16) for ff in range(32)]
    oute = [sb(f"p4_oute{cs}", [128, C], BF16) for cs in range(4)]

    bk = [ps(f"p4_bk{i}", [128, 512], F32) for i in range(6)]
    bb = [ps(f"p4_bb{i}", [128, 1024], BF16) for i in range(2)]
    pL = [bk[4][:, 0:16], bk[5][:, 0:16]]
    pcum = bk[2][:, 0:16]
    ptot = bk[3][0:1, 0:16]
    pH = [bk[0][:], bk[1][:]]
    pO = [bk[2][:], bk[3][:], bk[4][:], bk[5][:]]

    s_init = sem("p4_init"); s_rtr = sem("p4_rtr")
    s_ld = sem("p4_ld"); s_tr = sem("p4_tr"); s_tev = sem("p4_tev")
    s_lg = sem("p4_lg"); s_nm = sem("p4_nm"); s_et = sem("p4_et"); s_oh = sem("p4_oh")
    s_cum = sem("p4_cum"); s_tot = sem("p4_tot"); s_cw = sem("p4_cw"); s_di = sem("p4_di")
    s_sc = sem("p4_sc"); s_gl = sem("p4_gl"); s_ga = sem("p4_ga")
    s_xtr = sem("p4_xtr"); s_xev = sem("p4_xev")
    s_w1p = [sem("p4_w1a"), sem("p4_w1b")]; s_w2 = sem("p4_w2")
    s_gac = [sem(f"p4_gac{i}") for i in range(4)]
    s_sctp = [sem("p4_scta"), sem("p4_sctb")]
    s_h1 = sem("p4_h1"); s_sil = sem("p4_sil"); s_h2 = sem("p4_h2")
    s_oev = sem("p4_oev"); s_sct = sem("p4_sct"); s_tap = sem("p4_tap")
    s_rt = sem("p4_rtp"); s_rt2 = sem("p4_rt2")

    glist_flat = p.glist.rearrange("e (c one) -> (e c) one", one=1)
    wlist_flat = p.wlist.rearrange("e (c one) -> (e c) one", one=1)
    xefm3 = xefm[:].rearrange("p (c f) -> p c f", f=CAP)

    for l in range(NL):
        xsrc = p.xn_tm if l == 0 else p.xt_tm[l - 1]
        xdst = p.xt_tm[l]

        with nc.Block() as blk:
            @blk.gpsimd
            def _(gp, l=l, xdst=xdst):
                gp.dma_start(p.glist[:, :], p.sent_in).then_inc(s_init, 16)
                gp.dma_start(xdst, p.zeros_in).then_inc(s_init, 16)
                gp.dma_start(p.wlist[:, :], p.zeros_in[0:E, 0:CAP]).then_inc(s_init, 16)
                gp.dma_start(rtr[:].rearrange("p (c e) -> p c e", e=16),
                             p.routers_d[l].rearrange("(c p) e -> p c e", p=128)
                             ).then_inc(s_rtr, 16)
                if l == 0 and "p2" not in p.phases:
                    for e2 in range(E):
                        for hf2 in range(2):
                            gp.dma_start(p.w1b[e2, hf2],
                                         p.W1_d[e2][:, hf2 * 2048:(hf2 + 1) * 2048]
                                         ).then_inc(p.s_wcast, 16)
                        gp.dma_start(p.w2b[e2], p.W2_d[e2]).then_inc(p.s_wcast, 16)
                IM = 80 if p.tap_glist is not None else 48
                bc_l = gp.to_reg(E * CAP - 1)
                gp.wait_ge(s_init, IM * l + 48)
                for i in range(32):
                    gp.wait_ge(s_di, 33 * l + i + 2)
                    gp.indirect_dma_start(
                        glist_flat, IndirectOffsetOnAxis(desti[i][:], 0),
                        p.iotatok[:, i:i + 1], None,
                        bounds_check=bc_l, oob_is_err=False).then_inc(s_sc, 16)
                    gp.indirect_dma_start(
                        wlist_flat, IndirectOffsetOnAxis(desti[i][:], 0),
                        wch[i][:], None,
                        bounds_check=bc_l, oob_is_err=False).then_inc(s_sc, 16)
                gp.wait_ge(s_sc, 32 * 32 * (l + 1))
                if p.tap_glist is not None:
                    gp.dma_start(p.tap_glist[l], p.glist[:, :]).then_inc(s_init, 16)
                    gp.dma_start(p.tap_wlist[l], p.wlist[:, :]).then_inc(s_init, 16)
                    gp.wait_ge(s_init, IM * (l + 1))

            @blk.sync
            def _(sync, l=l, xsrc=xsrc):
                for pos in range(33):
                    i = pos % 32
                    gc = 33 * l + pos
                    if gc >= 2:
                        sync.wait_ge(s_tr, gc - 1)
                    sync.dma_start(xstrip[pos % 2][:], xsrc[i * 128:(i + 1) * 128, :]
                                   ).then_inc(s_ld, 16)

            @blk.scalar
            def _(s, l=l):
                for pos in range(33):
                    i = pos % 32
                    gc = 33 * l + pos
                    if l == 0 and p.tap_logits is not None:
                        s.wait_ge(s_lg, gc + 1)
                        if pos >= 2:
                            s.wait_ge(s_rt2, 16 * (pos - 1))
                        s.copy(logbuf[pos % 2][:], pL[pos % 2])
                        s.dma_start(p.tap_logits[i], logbuf[pos % 2][:]).then_inc(s_rt2, 16)
                    s.wait_ge(s_nm, gc + 1)
                    if gc >= 1:
                        s.wait_ge(s_oh, gc)
                    p.afence(s, s.activation(et[:], pL[pos % 2], AF.Exp, bias=negm[:],
                                              accum_out=zrow[:]))
                    s.nop().then_inc(s_et, 1)

            @blk.tensor
            def _(t, l=l):
                def emit_transposes(pos, gc):
                    t.wait_ge(s_ld, 16 * (gc + 1))
                    if gc >= 2:
                        t.wait_ge(s_tev, gc - 1)
                    for c in range(8):
                        ins = t.transpose(bb[pos % 2][:, c * 128:(c + 1) * 128],
                                          xstrip[pos % 2][:, c * 128:(c + 1) * 128],
                                          p.ident[:])
                    ins.then_inc(s_tr, 1)

                def emit_logits(pos, gc):
                    t.wait_ge(s_tev, gc + 1)
                    if pos == 0:
                        t.wait_ge(s_rtr, 16 * (l + 1))
                    if gc >= 2:
                        t.wait_ge(s_oh, gc - 1)
                        t.wait_ge(s_et, gc - 1)
                    for c in range(8):
                        ins = t.matmul(pL[pos % 2], xfmc[pos % 2][:, c * 128:(c + 1) * 128],
                                       rtr[:, c * 16:(c + 1) * 16],
                                       start=(c == 0), stop=(c == 7))
                    ins.then_inc(s_lg, 1)
                for pos in range(33):
                    emit_transposes(pos, 33 * l + pos)
                    if pos >= 1:
                        emit_logits(pos - 1, 33 * l + pos - 1)
                emit_logits(32, 33 * l + 32)
                for pos in range(33):
                    i = max(0, pos - 1)
                    gc = 33 * l + pos
                    t.wait_ge(s_oh, 33 * (l + 1))
                    t.wait_ge(s_cw, gc + 1)
                    t.matmul(pcum, p.u128[:], oh[i][:], start=True, stop=False)
                    t.matmul(pcum, p.onesrow[:], carry[:], start=False, stop=True
                             ).then_inc(s_cum, 1)
                    t.matmul(ptot, p.onesf[:], oh[i][:], start=True,
                             stop=True).then_inc(s_tot, 1)

            @blk.vector
            def _(v, l=l):
                for pos in range(33):
                    i = pos % 32
                    gc = 33 * l + pos
                    v.wait_ge(s_tr, gc + 1)
                    if gc >= 2:
                        v.wait_ge(s_lg, gc - 1)
                    p.vfence(v, v.tensor_copy(xfmc[pos % 2][:], bb[pos % 2][:, 0:1024]))
                    v.engine_nop().then_inc(s_tev, 1)
                    v.wait_ge(s_lg, gc + 1)
                    p.vfence(v, v.reduce_max(negm[:], pL[pos % 2], AX.X, negate=True))
                    v.engine_nop().then_inc(s_nm, 1)
                    v.wait_ge(s_et, gc + 1)
                    p.vfence(v, v.reciprocal(wch[i][:], zrow[:]))
                    p.vfence(v, v.tensor_scalar(tmp16[:], pL[pos % 2], negm[:], 1e8,
                                                ALU.add, ALU.mult))
                    p.vfence(v, v.tensor_scalar(ind[:], tmp16[:], 1.0, 0.0,
                                                ALU.add, ALU.max))
                    p.vfence(v, v.tensor_mul(val[:], ind[:], p.iota16[:]))
                    p.vfence(v, v.reduce_max(vmn[:], val[:], AX.X))
                    p.vfence(v, v.tensor_scalar(over[:], vmn[:], -1.0, 1.0,
                                                ALU.mult, ALU.add))
                    p.vfence(v, v.tensor_scalar(oh[i][:], val[:], over[:], 0.0,
                                                ALU.add, ALU.max))
                    p.vfence(v, v.tensor_scalar(idxv[i][:], vmn[:], -1.0, 1024.0,
                                                ALU.mult, ALU.add))
                    v.engine_nop().then_inc(s_oh, 1)
                for pos in range(33):
                    i = max(0, pos - 1)
                    gc = 33 * l + pos
                    v.wait_ge(s_oh, 33 * (l + 1))
                    if pos <= 1:
                        p.vfence(v, v.memset(carry[:], 0.0))
                    else:
                        v.wait_ge(s_tot, gc)
                        p.vfence(v, v.tensor_add(carry[:], carry[:], ptot))
                    v.engine_nop().then_inc(s_cw, 1)
                    v.wait_ge(s_cum, gc + 1)
                    p.vfence(v, v.tensor_mul(tmp16[:], pcum, oh[i][:]))
                    p.vfence(v, v.reduce_sum(slotp[:], tmp16[:], AX.X))
                    p.vfence(v, v.tensor_scalar(destf[:], idxv[i][:], float(CAP), slotp[:],
                                                ALU.mult, ALU.add))
                    p.vfence(v, v.tensor_scalar(over[:], slotp[:], -(float(CAP) + 0.5), 0.0,
                                                ALU.add, ALU.max))
                    p.vfence(v, v.tensor_scalar(over[:], over[:], 4e4, -1.0,
                                                ALU.mult, ALU.add))
                    p.vfence(v, v.tensor_add(destf[:], destf[:], over[:]))
                    p.vfence(v, v.tensor_copy(desti[i][:], destf[:]))
                    v.engine_nop().then_inc(s_di, 1)

        if l == NL - 1 and p.tap_desti is not None:
            with nc.Block() as blkd:
                @blkd.sync
                def _(sync):
                    n = 0
                    for i2 in range(32):
                        sync.dma_start(p.tap_desti[i2].rearrange("(x y) -> x y", y=1),
                                       desti[i2][:]).then_inc(s_rt, 16)
                        n += 16
                        if p.tap_oh is not None:
                            sync.dma_start(p.tap_oh[i2], oh[i2][:]).then_inc(s_rt, 16)
                            n += 16
                        if p.tap_route is not None:
                            with nc.allow_non_contiguous_dma(reason="debug tap"):
                                sync.dma_start(p.tap_route[i2][:, 0:1], idxv[i2][:]
                                               ).then_inc(s_rt, 16)
                            n += 16
                    if p.tap_stats is not None:
                        slabs = [p.iota16[:], ind[:], val[:], tmp16[:], et[:]]
                        cols = [negm[:], vmn[:], over[:], zrow[:]]
                        for j2, t2 in enumerate(slabs):
                            sync.dma_start(p.tap_stats[j2], t2).then_inc(s_rt, 16)
                            n += 16
                        with nc.allow_non_contiguous_dma(reason="debug tap"):
                            for j2, t2 in enumerate(cols):
                                sync.dma_start(p.tap_stats[5][:, j2:j2 + 1], t2
                                               ).then_inc(s_rt, 16)
                                n += 16
                    sync.wait_ge(s_rt, n)
        if "b" not in P4_PARTS:
            continue
        with nc.Block() as blk:
            @blk.sync
            def _(sync, l=l):
                for e in range(E):
                    u = 16 * l + e
                    for cs in range(4):
                        if u >= 2:
                            sync.wait_ge(s_sctp[u % 2], 16 * 4 * ((u - 2) // 2 + 1))
                        sync.dma_start(gi[e % 2][cs][:],
                                       p.glist[e, cs * 128:(cs + 1) * 128]
                                       .rearrange("(x y) -> x y", y=1)).then_inc(s_gl, 16)
                        sync.dma_start(wv[e % 2][cs][:],
                                       p.wlist[e, cs * 128:(cs + 1) * 128]
                                       .rearrange("(x y) -> x y", y=1)).then_inc(s_gl, 16)
                    if l == 0:
                        sync.wait_ge(p.s_wcast, 16 * 3 * (e + 1))
                    for hf in range(2):
                        hidx = 2 * u + hf
                        if hidx >= 2:
                            sync.wait_ge(s_h1, (hidx - 1) * 16)
                        for c in range(8):
                            sync.dma_start(w1s[hidx % 2][c][:],
                                           p.w1b[e, hf, c * 128:(c + 1) * 128, :]
                                           ).then_inc(s_w1p[hidx % 2], 16)
                    for ff in range(32):
                        if u >= 1:
                            sync.wait_ge(s_h2, (u - 1) * 64 + 32 + ff + 1)
                        sync.dma_start(w2s[ff][:],
                                       p.w2b[e, ff * 128:(ff + 1) * 128, :]
                                       ).then_inc(s_w2, 16)

            @blk.gpsimd
            def _(gp, l=l, xsrc=xsrc, xdst=xdst):
                bc_t = gp.to_reg(T - 1)
                for e in range(E):
                    u = 16 * l + e
                    for cs in range(4):
                        gp.wait_ge(s_gl, 16 * (8 * u + 2 * cs + 1))
                        if u >= 1:
                            gp.wait_ge(s_xtr, 4 * (u - 1) + cs + 1)
                        gp.indirect_dma_start(
                            xetm[cs][:], None, xsrc,
                            IndirectOffsetOnAxis(gi[e % 2][cs][:], 0),
                            bounds_check=bc_t, oob_is_err=False).then_inc(s_gac[cs], 16)
                    for cs in range(4):
                        gp.wait_ge(s_oev, 8 * u + 2 * (cs + 1))
                        gp.indirect_dma_start(
                            xdst, IndirectOffsetOnAxis(gi[e % 2][cs][:], 0),
                            oute[cs][:], None,
                            bounds_check=bc_t, oob_is_err=False).then_inc(s_sctp[u % 2], 16)
                gp.wait_ge(s_sctp[0], 16 * 4 * 8 * (l + 1))
                gp.wait_ge(s_sctp[1], 16 * 4 * 8 * (l + 1))
                if p.tap_xt[l] is not None:
                    gp.dma_start(p.tap_xt[l], xdst).then_inc(s_tap, 16)
                    gp.wait_ge(s_tap, 16 * (l + 1))

            @blk.tensor
            def _(t, l=l):
                for e in range(E):
                    u = 16 * l + e
                    for cs in range(4):
                        t.wait_ge(s_gac[cs], 16 * (u + 1))
                        if 4 * u + cs >= 2:
                            t.wait_ge(s_xev, 4 * u + cs - 1)
                        for c in range(8):
                            ins = t.transpose(bb[cs % 2][:, c * 128:(c + 1) * 128],
                                              xetm[cs][:, c * 128:(c + 1) * 128],
                                              p.ident[:])
                        ins.then_inc(s_xtr, 1)
                    t.wait_ge(s_xev, 4 * u + 4)
                    for ff in range(32):
                        fidx = 32 * u + ff
                        hidx = 2 * u + ff // 16
                        t.wait_ge(s_w1p[hidx % 2], 16 * 8 * (hidx // 2 + 1))
                        if fidx >= 2:
                            t.wait_ge(s_sil, fidx - 1)
                        for c in range(8):
                            ins = t.matmul(pH[ff % 2],
                                           w1s[hidx % 2][c][:, (ff % 16) * 128:(ff % 16) * 128 + 128],
                                           xefm[:, c * CAP:(c + 1) * CAP],
                                           start=(c == 0), stop=(c == 7))
                        ins.then_inc(s_h1, 1)
                    for ch in range(2):
                        for ff in range(32):
                            t.wait_ge(s_sil, 32 * u + ff + 1)
                            if ch == 0 and ff == 0:
                                t.wait_ge(s_w2, 16 * 32 * (u + 1))
                            if ch == 1 and ff == 0:
                                t.wait_ge(s_oev, 8 * u + 4)
                            for cs in range(2):
                                for dh in range(2):
                                    ins = t.matmul(
                                        pO[2 * cs + dh],
                                        silu[ff][:, (ch * 2 + cs) * 128:(ch * 2 + cs) * 128 + 128],
                                        w2s[ff][:, dh * 512:(dh + 1) * 512],
                                        start=(ff == 0), stop=(ff == 31))
                            ins.then_inc(s_h2, 1)

            @blk.scalar
            def _(s, l=l):
                for e in range(E):
                    u = 16 * l + e
                    for ff in range(32):
                        fidx = 32 * u + ff
                        s.wait_ge(s_h1, fidx + 1)
                        if u >= 1:
                            s.wait_ge(s_h2, (u - 1) * 64 + 32 + ff + 1)
                        p.afence(s, s.activation(silu[ff][:], pH[ff % 2], AF.Silu))
                        s.nop().then_inc(s_sil, 1)

            @blk.vector
            def _(v, l=l):
                for e in range(E):
                    u = 16 * l + e
                    for cs in range(4):
                        v.wait_ge(s_xtr, 4 * u + cs + 1)
                        if u >= 1 and cs == 0:
                            v.wait_ge(s_h1, u * 32)
                        p.vfence(v, v.tensor_copy(xefm3[:, :, cs * 128:cs * 128 + 128],
                                                   bb[cs % 2][:, 0:1024]
                                                   .rearrange("p (c f) -> p c f", f=128)))
                        v.engine_nop().then_inc(s_xev, 1)
                    for ch in range(2):
                        v.wait_ge(s_h2, 64 * u + 32 * (ch + 1))
                        for cs in range(2):
                            sc = ch * 2 + cs
                            if u >= 1:
                                v.wait_ge(s_sctp[(u - 1) % 2], 16 * 4 * ((u - 1) // 2 + 1))
                            for dh in range(2):
                                p.vfence(v, v.tensor_scalar(
                                    oute[sc][:, dh * 512:(dh + 1) * 512],
                                    pO[2 * cs + dh], wv[e % 2][sc][:], None, ALU.mult))
                                v.engine_nop().then_inc(s_oev, 1)
    pctx.close()


def build_p5(p):
    nc, sem = p.nc, p.sem
    pctx = ExitStack()
    sb = lambda name, shape, dt: pctx.enter_context(nc.sbuf_tensor(name, shape, dt))
    ps = lambda name, shape, dt=F32: pctx.enter_context(nc.psum_tensor(name, shape, dt))

    xld = [[sb(f"p5_x{i}_{c}", [128, 512], F32) for c in range(8)] for i in range(2)]
    ald = [[sb(f"p5_a{i}_{c}", [128, 512], F32) for c in range(8)] for i in range(2)]
    fst = [sb(f"p5_f{i}", [128, C], BF16) for i in range(2)]
    ob = [sb(f"p5_o{i}", [128, 512], F32) for i in range(8)]
    pF = [h[:] for h in [ps(f"p5_p{c}", [128, 512], BF16) for c in range(8)]]

    s_ld = sem("p5_ld")
    s_fsp = [sem("p5_fsa"), sem("p5_fsb")]
    s_tr = sem("p5_tr")
    s_add = sem("p5_add")
    s_st = sem("p5_st")

    with nc.Block() as blk:
        @blk.sync
        def _(sync):
            for bq in range(8):
                b, qg = bq // 4, bq % 4
                if bq >= 2:
                    sync.wait_ge(s_add, 8 * (bq - 1))
                for c in range(8):
                    sync.dma_start(xld[bq % 2][c][:],
                                   p.x_fm[b, c * 128:(c + 1) * 128,
                                          qg * 512:(qg + 1) * 512]).then_inc(s_ld, 16)
                    sync.dma_start(ald[bq % 2][c][:],
                                   p.attnproj[b, c * 128:(c + 1) * 128,
                                              qg * 512:(qg + 1) * 512]).then_inc(s_ld, 16)

        @blk.gpsimd
        def _(gp):
            for bq in range(8):
                for sub in range(4):
                    fi = bq * 4 + sub
                    if fi >= 2:
                        gp.wait_ge(s_tr, fi - 1)
                    row0 = (bq // 4) * S + (bq % 4) * 512 + sub * 128
                    gp.dma_start(fst[fi % 2][:], p.xt_tm[NL - 1][row0:row0 + 128, :]
                                 ).then_inc(s_fsp[fi % 2], 16)

        @blk.tensor
        def _(t):
            for bq in range(8):
                if bq >= 1:
                    t.wait_ge(s_add, 8 * bq)
                for sub in range(4):
                    fi = bq * 4 + sub
                    t.wait_ge(s_fsp[fi % 2], 16 * (fi // 2 + 1))
                    for c in range(8):
                        ins = t.transpose(pF[c][:, sub * 128:(sub + 1) * 128] if False else pF[c].tensor[0:128, sub * 128:(sub + 1) * 128],
                                          fst[fi % 2][:, c * 128:(c + 1) * 128],
                                          p.ident[:])
                    ins.then_inc(s_tr, 1)

        @blk.vector
        def _(v):
            for bq in range(8):
                v.wait_ge(s_ld, 16 * 16 * (bq + 1))
                v.wait_ge(s_tr, 4 * (bq + 1))
                for c in range(8):
                    m = bq * 8 + c
                    if m >= 8:
                        v.wait_ge(s_st, 16 * (m - 7))
                    v.tensor_add(ob[m % 8][:], xld[bq % 2][c][:], ald[bq % 2][c][:])
                    p.vfence(v, v.tensor_add(ob[m % 8][:], ob[m % 8][:], pF[c]))
                    v.engine_nop().then_inc(s_add, 1)

        @blk.scalar
        def _(sc):
            for bq in range(8):
                b, qg = bq // 4, bq % 4
                for c in range(8):
                    m = bq * 8 + c
                    sc.wait_ge(s_add, m + 1)
                    sc.dma_start(p.out_fm[b, c * 128:(c + 1) * 128,
                                          qg * 512:(qg + 1) * 512],
                                 ob[m % 8][:]).then_inc(s_st, 16)
            sc.wait_ge(s_st, 16 * 64)
    pctx.close()


def kernel(**inputs):
    in_maps = _host_prep(inputs)
    nc = build()
    res = run_bass_kernel_spmd(nc, in_maps, core_ids=list(range(NCORES)))
    outs = []
    for c in range(NCORES):
        o = np.asarray(res.results[c]["out_fm"], np.float32)
        outs.append(o.transpose(0, 2, 1))
    return np.ascontiguousarray(np.concatenate(outs, 0))


# version 06 (final)
# speedup vs baseline: 1.6150x; 1.0589x over previous
import numpy as np
import ml_dtypes
from contextlib import ExitStack

import concourse.bass as bass
import concourse.mybir as mybir
from concourse.bass_utils import run_bass_kernel_spmd
from concourse.bass import IndirectOffsetOnAxis

F32 = mybir.dt.float32
BF16 = mybir.dt.bfloat16
I32 = mybir.dt.int32
AF = mybir.ActivationFunctionType
ALU = mybir.AluOpType
AX = mybir.AxisListType

NCORES = 8
B = 2
S = 2048
C = 1024
NH = 16
HD = 64
T = B * S
E = 16
NL = 4
FF = 4096
CAP = 512
EXP_SHIFT = -32.0
SENT = 1 << 20



def _host_constants():
    pos = np.arange(S, dtype=np.float64)
    inv_freq = 1.0 / 10000.0 ** (np.arange(0, 2 * HD, 2, dtype=np.float64) / (2 * HD))
    freqs = np.outer(pos, inv_freq)
    cos0 = np.cos(freqs).astype(np.float32).T
    sin0 = np.sin(freqs).astype(np.float32).T
    cos2 = np.ascontiguousarray(np.concatenate([cos0, cos0], 0))
    sin2 = np.ascontiguousarray(np.concatenate([sin0, sin0], 0))
    ident = np.eye(128, dtype=ml_dtypes.bfloat16)
    u128 = np.triu(np.ones((128, 128), np.float32))
    iota16 = (1024.0 - np.broadcast_to(np.arange(16, dtype=np.float32), (128, 16))).copy()
    iota_tok = (np.arange(32)[None, :] * 128 + np.arange(128)[:, None]).astype(np.int32)
    sent = np.full((E, CAP), SENT, np.int32)
    zeros_tm = np.zeros((T, C), ml_dtypes.bfloat16)
    return dict(cos2=cos2, sin2=sin2, ident=ident, u128=u128, iota16=iota16,
                iota_tok=np.ascontiguousarray(iota_tok), sent=sent, zeros_tm=zeros_tm)


def _host_prep(inputs):
    x = np.asarray(inputs["L0_attn_x"], np.float32)
    Wqkv = np.asarray(inputs["L0_attn_W_qkv"], np.float32)
    Wproj = np.asarray(inputs["L0_attn_W_proj"], np.float32)
    WqkT = np.ascontiguousarray(Wqkv[:4 * C].T)
    j = np.arange(2 * C)
    hh = j // 128
    dv = j % 128
    src = (4 + dv // HD) * C + hh * HD + (dv % HD)
    Wv_rhs = np.ascontiguousarray(Wqkv[src].T)
    WprojT = np.ascontiguousarray(Wproj.T)

    shared = dict(WqkT=WqkT, Wv_rhs=Wv_rhs, WprojT=WprojT,
                  routers=np.asarray(inputs["L0_ffn_routers"], np.float32),
                  W1=np.asarray(inputs["L0_ffn_W1"], np.float32),
                  W2=np.asarray(inputs["L0_ffn_W2"], np.float32),
                  lq1=np.asarray(inputs["L0_attn_lambda_q1"], np.float32).reshape(1, HD),
                  lq2=np.asarray(inputs["L0_attn_lambda_q2"], np.float32).reshape(1, HD),
                  lk1=np.asarray(inputs["L0_attn_lambda_k1"], np.float32).reshape(1, HD),
                  lk2=np.asarray(inputs["L0_attn_lambda_k2"], np.float32).reshape(1, HD),
                  **_host_constants())
    in_maps = []
    for c in range(NCORES):
        m = dict(shared)
        m["x_fm"] = np.ascontiguousarray(x[2 * c:2 * c + 2].transpose(0, 2, 1))
        in_maps.append(m)
    return in_maps


class P:
    pass


P4_PARTS = "ab"


def build(debug_taps=(), phases=("p0", "p1", "p2", "p3", "p4", "p5")):
    p = P()
    nc = bass.Bass("TRN2", target_bir_lowering=False, detect_race_conditions=False)
    p.nc = nc
    p.taps = set(debug_taps)

    tmp_sems = []
    try:
        for _ in range(256):
            tmp_sems.append(nc.alloc_semaphore())
    except Exception:
        pass
    with nc.Block() as _blk0:
        @_blk0.gpsimd
        def _(g):
            g.engine_nop()
            nc.clear_and_free_semaphores(tmp_sems)

    def din(name, shape, dt):
        return nc.dram_tensor(name, shape, dt, kind="ExternalInput").ap()
    p.x_fm = din("x_fm", [B, C, S], F32)
    p.WqkT = din("WqkT", [C, 4 * C], F32)
    p.Wv_rhs = din("Wv_rhs", [C, 2 * C], F32)
    p.WprojT = din("WprojT", [2 * C, C], F32)
    p.routers_d = din("routers", [NL, C, E], F32)
    p.W1_d = din("W1", [E, C, FF], F32)
    p.W2_d = din("W2", [E, FF, C], F32)
    p.lq1 = din("lq1", [1, HD], F32)
    p.lq2 = din("lq2", [1, HD], F32)
    p.lk1 = din("lk1", [1, HD], F32)
    p.lk2 = din("lk2", [1, HD], F32)
    p.cos2 = din("cos2", [128, S], F32)
    p.sin2 = din("sin2", [128, S], F32)
    p.ident_in = din("ident", [128, 128], BF16)
    p.u128_in = din("u128", [128, 128], F32)
    p.iota16_in = din("iota16", [128, 16], F32)
    p.iotatok_in = din("iota_tok", [128, 32], I32)
    p.sent_in = din("sent", [E, CAP], I32)
    p.zeros_in = din("zeros_tm", [T, C], BF16)

    p.out_fm = nc.dram_tensor("out_fm", [B, C, S], F32, kind="ExternalOutput").ap()
    p.xn_tm = nc.dram_tensor("xn_tm", [T, C], BF16).ap()
    p.xt_tm = [nc.dram_tensor(f"xt{i}_tm", [T, C], BF16).ap() for i in range(NL)]
    p.attn_fm = nc.dram_tensor("attn_fm", [B, 2 * C, S], BF16).ap()
    p.attnproj = nc.dram_tensor("attnproj", [B, C, S], F32).ap()
    p.glist = nc.dram_tensor("glist", [E, CAP], I32).ap()
    p.w1b = nc.dram_tensor("w1b", [E, 2, C, FF // 2], BF16).ap()
    p.w2b = nc.dram_tensor("w2b", [E, FF, C], BF16).ap()
    p.wlist = nc.dram_tensor("wlist", [E, CAP], F32).ap()

    def tap(name, shape, dt):
        if name in p.taps:
            return nc.dram_tensor("tap_" + name, shape, dt, kind="ExternalOutput").ap()
        return None
    p.tap_xn = tap("xn", [T, C], BF16)
    p.tap_glist = tap("glist", [NL, E, CAP], I32)
    p.tap_wlist = tap("wlist" if "glist" not in p.taps else "glist_w", [NL, E, CAP], F32)
    if "glist" in p.taps and p.tap_wlist is None:
        p.tap_wlist = nc.dram_tensor("tap_wlist", [NL, E, CAP], F32, kind="ExternalOutput").ap()
    p.tap_xt = [tap(f"xt{i}", [T, C], BF16) for i in range(NL)]
    p.tap_attn = tap("attn", [B, 2 * C, S], BF16)
    p.tap_attnproj = tap("attnproj", [B, C, S], F32)
    p.tap_route = tap("route", [32, 128, 4], F32)
    p.tap_oh = tap("oh", [32, 128, 16], F32)
    p.tap_logits = tap("logits", [32, 128, 16], F32)
    p.tap_xfm = tap("xfm", [32, 128, C], BF16)
    p.tap_xstrip = tap("xstrip", [32, 128, C], BF16)
    p.tap_desti = tap("desti", [32, 128], I32)
    p.tap_stats = tap("stats", [6, 128, 16], F32)

    with ExitStack() as ctx:
        p.ctx = ctx
        p.sb = lambda name, shape, dt: ctx.enter_context(nc.sbuf_tensor(name, shape, dt))
        p.ps = lambda name, shape, dt=F32: ctx.enter_context(nc.psum_tensor(name, shape, dt))
        p.sem = lambda name: ctx.enter_context(nc.semaphore(name))
        p.s_wcast = p.sem("wcast")

        p.s_vf = p.sem("dve_fence")
        p.vf_n = [0]

        def vfence(v, ins):
            p.vf_n[0] += 1
            ins.then_inc(p.s_vf, 1)
            v.wait_ge(p.s_vf, p.vf_n[0])
        p.vfence = vfence
        p.s_af = p.sem("act_fence")
        p.af_n = [0]

        def afence(sc, ins):
            p.af_n[0] += 1
            ins.then_inc(p.s_af, 1)
            sc.wait_ge(p.s_af, p.af_n[0])
        p.afence = afence

        sb = p.sb
        p.guard = sb("guardpad", [128, 3072], BF16)
        p.ident = sb("identsb", [128, 128], BF16)
        p.u128 = sb("u128sb", [128, 128], F32)
        p.iota16 = sb("iota16sb", [128, 16], F32)
        p.iotatok = sb("iotatoksb", [128, 32], I32)
        p.onesb = sb("onesbsb", [128, 1], BF16)
        p.onesf = sb("onesfsb", [128, 1], F32)
        p.onesrow = sb("onesrowsb", [1, 128], F32)
        p.biasC = sb("biasCsb", [128, 1], F32)
        p.lamcol = sb("lamcolsb", [128, 1], F32)
        p.lam11 = sb("lam11sb", [1, 1], F32)
        actx = ExitStack()
        p.xnf = [actx.enter_context(nc.sbuf_tensor(f"xnf{c}", [128, T], BF16))
                 for c in range(8)]
        p.cos = actx.enter_context(nc.sbuf_tensor("cossb", [128, S], F32))
        p.sin = actx.enter_context(nc.sbuf_tensor("sinsb", [128, S], F32))

        p.phases = phases
        if "p0" in phases:
            build_p0(p)
        if "p1" in phases:
            build_p1(p)
        if "p2" in phases:
            build_p2(p)
        if "p3" in phases:
            build_p3(p)
        actx.close()
        if "p4" in phases:
            build_p4(p)
        if "p5" in phases:
            build_p5(p)
    return nc


def build_p0(p):
    nc, sem = p.nc, p.sem
    from contextlib import ExitStack
    pctx = ExitStack()
    sb = lambda name, shape, dt: pctx.enter_context(nc.sbuf_tensor(name, shape, dt))
    ps = lambda name, shape, dt=F32: pctx.enter_context(nc.psum_tensor(name, shape, dt))
    p._p0ctx = pctx
    lt = [sb(f"p0_lam{i}", [1, HD], F32) for i in range(4)]
    e12 = sb("p0_e12", [1, 2], F32)
    pb = ps("p0_bc", [128, 1], F32)
    s_ld = sem("p0_ld")
    s_c = sem("p0_c")

    with nc.Block() as blk:
        @blk.sync
        def _(sync):
            sync.dma_start(p.ident[:], p.ident_in).then_inc(s_ld, 16)
            sync.dma_start(p.u128[:], p.u128_in).then_inc(s_ld, 16)
            sync.dma_start(p.iota16[:], p.iota16_in).then_inc(s_ld, 16)
            sync.dma_start(p.iotatok[:], p.iotatok_in).then_inc(s_ld, 16)
            sync.dma_start(p.cos[:], p.cos2).then_inc(s_ld, 16)
            sync.dma_start(p.sin[:], p.sin2).then_inc(s_ld, 16)
            for i, src in enumerate([p.lq1, p.lk1, p.lq2, p.lk2]):
                sync.dma_start(lt[i][:], src).then_inc(s_ld, 16)

        @blk.vector
        def _(v):
            v.memset(p.onesb[:], 1.0)
            v.memset(p.onesf[:], 1.0)
            v.memset(p.onesrow[:], 1.0)
            v.memset(p.biasC[:], EXP_SHIFT)
            v.wait_ge(s_ld, 160)
            v.tensor_mul(lt[0][:], lt[0][:], lt[1][:])
            p.vfence(v, v.tensor_mul(lt[2][:], lt[2][:], lt[3][:]))
            v.reduce_sum(e12[:, 0:1], lt[0][:], AX.X)
            p.vfence(v, v.reduce_sum(e12[:, 1:2], lt[2][:], AX.X))
            v.engine_nop().then_inc(s_c, 1)

        @blk.scalar
        def _(s):
            s.wait_ge(s_c, 1)
            p.afence(s, s.activation(e12[:], e12[:], AF.Exp))
            s.nop().then_inc(s_c, 1)

        @blk.vector
        def _(v2):
            v2.wait_ge(s_c, 2)
            p.vfence(v2, v2.tensor_sub(p.lam11[:], e12[:, 0:1], e12[:, 1:2]))
            v2.tensor_scalar(p.lam11[:], p.lam11[:], 0.2, None, ALU.add).then_inc(s_c, 1)

        @blk.tensor
        def _(t):
            t.wait_ge(s_c, 3)
            t.matmul(pb[:], p.onesrow[:], p.lam11[:], start=True, stop=True).then_inc(s_c, 1)

        @blk.vector
        def _(v3):
            v3.wait_ge(s_c, 4)
            v3.tensor_copy(p.lamcol[:], pb[:])
    pctx.close()


def build_p1(p):
    nc, sem = p.nc, p.sem
    from contextlib import ExitStack
    pctx = ExitStack()
    sb = lambda name, shape, dt: pctx.enter_context(nc.sbuf_tensor(name, shape, dt))
    ps = lambda name, shape, dt=F32: pctx.enter_context(nc.psum_tensor(name, shape, dt))
    NG = 8
    xld = [[sb(f"p1_x{i}_{c}", [128, 512], F32) for c in range(8)] for i in range(2)]
    xbf = [sb(f"p1_xb{c}", [128, 512], BF16) for c in range(8)]
    sqb = [sb(f"p1_sq{c}", [128, 512], BF16) for c in range(8)]
    stat = sb("p1_stat", [1, 512 * 5], F32)
    mean, ex2, var, srt, rstd = (stat[:, i * 512:(i + 1) * 512] for i in range(5))
    xtmb = [sb(f"p1_xtm{i}", [128, C], BF16) for i in range(4)]
    pA = ps("p1_A", [1, 512])
    pB = ps("p1_B", [1, 512])
    pC = ps("p1_C", [128, 512])
    pD = ps("p1_D", [128, 512])
    pE = [ps(f"p1_E{i}", [128, 512], BF16) for i in range(4)]

    s_ld = sem("p1_ld")
    s_cast = sem("p1_cast")
    s_sq = sem("p1_sq")
    s_pe = sem("p1_pe")
    s_stat = sem("p1_stat")
    s_sc = sem("p1_sc")
    s_v2 = sem("p1_v2")
    s_bc = sem("p1_bc")
    s_apl = sem("p1_apl")
    s_ptr = sem("p1_ptr")
    s_tev = sem("p1_tev")
    s_xtm = sem("p1_xtm")
    XM = 32 if p.tap_xn is not None else 16

    def tcol_of(g):
        t0 = (g // 4) * 2048 + (g % 4) * 512
        return slice(t0, t0 + 512)

    with nc.Block() as blk:
        @blk.sync
        def _(sync):
            for g in range(NG):
                b, tg = g // 4, g % 4
                if g >= 2:
                    sync.wait_ge(s_apl, g - 1)
                for c in range(8):
                    sync.dma_start(xld[g % 2][c][:],
                                   p.x_fm[b, c * 128:(c + 1) * 128, tg * 512:(tg + 1) * 512]
                                   ).then_inc(s_ld, 16)

        @blk.vector
        def _(v):
            for g in range(NG):
                v.wait_ge(s_ld, 16 * 8 * (g + 1))
                if g >= 1:
                    v.wait_ge(s_pe, 2 * g - 1)
                for c in range(8):
                    ins = v.tensor_copy(xbf[c][:], xld[g % 2][c][:])
                p.vfence(v, ins)
                v.engine_nop().then_inc(s_cast, 1)
                v.wait_ge(s_pe, 2 * g + 2)
                v.tensor_scalar(mean, pA[:], 1.0 / C, None, ALU.mult)
                p.vfence(v, v.tensor_scalar(ex2, pB[:], 1.0 / C, None, ALU.mult))
                p.vfence(v, v.tensor_mul(var, mean, mean))
                p.vfence(v, v.tensor_sub(var, ex2, var))
                p.vfence(v, v.tensor_scalar(var, var, 1e-5, None, ALU.add))
                v.engine_nop().then_inc(s_stat, 1)
                v.wait_ge(s_sc, g + 1)
                p.vfence(v, v.reciprocal(rstd, srt))
                v.engine_nop().then_inc(s_v2, 1)
                v.wait_ge(s_bc, g + 1)
                tcol = tcol_of(g)
                for c in range(8):
                    v.tensor_sub(xld[g % 2][c][:], xld[g % 2][c][:], pC[:])
                    ins = v.tensor_mul(p.xnf[c][:, tcol], xld[g % 2][c][:], pD[:])
                p.vfence(v, ins)
                v.engine_nop().then_inc(s_apl, 1)
                for sub in range(4):
                    gs = 4 * g + sub
                    pr = 2 * (gs % 2)
                    v.wait_ge(s_ptr, gs + 1)
                    if gs >= 4:
                        v.wait_ge(s_xtm, XM * (gs - 3))
                    v.tensor_copy(xtmb[gs % 4][:, 0:512], pE[pr][:])
                    p.vfence(v, v.tensor_copy(xtmb[gs % 4][:, 512:1024], pE[pr + 1][:]))
                    v.engine_nop().then_inc(s_tev, 1)

        @blk.scalar
        def _(s):
            for g in range(NG):
                s.wait_ge(s_ld, 16 * 8 * (g + 1))
                if g >= 1:
                    s.wait_ge(s_pe, 2 * g)
                for c in range(8):
                    ins = s.activation(sqb[c][:], xld[g % 2][c][:], AF.Square)
                p.afence(s, ins)
                s.nop().then_inc(s_sq, 1)
                s.wait_ge(s_stat, g + 1)
                p.afence(s, s.activation(srt, var, AF.Sqrt))
                s.nop().then_inc(s_sc, 1)

        @blk.tensor
        def _(t):
            for g in range(NG):
                t.wait_ge(s_cast, g + 1)
                if g >= 1:
                    t.wait_ge(s_stat, g)
                for c in range(8):
                    ins = t.matmul(pA[:], p.onesb[:], xbf[c][:], start=(c == 0), stop=(c == 7))
                ins.then_inc(s_pe, 1)
                t.wait_ge(s_sq, g + 1)
                for c in range(8):
                    ins = t.matmul(pB[:], p.onesb[:], sqb[c][:], start=(c == 0), stop=(c == 7))
                ins.then_inc(s_pe, 1)
                t.wait_ge(s_v2, g + 1)
                if g >= 1:
                    t.wait_ge(s_apl, g)
                t.matmul(pC[:], p.onesrow[:], mean, start=True, stop=True)
                t.matmul(pD[:], p.onesrow[:], rstd, start=True, stop=True).then_inc(s_bc, 1)
                t.wait_ge(s_apl, g + 1)
                for sub in range(4):
                    gs = 4 * g + sub
                    pr = 2 * (gs % 2)
                    if gs >= 2:
                        t.wait_ge(s_tev, gs - 1)
                    t0 = (g // 4) * 2048 + (g % 4) * 512 + sub * 128
                    scol = slice(t0, t0 + 128)
                    for c in range(8):
                        ins = t.transpose(pE[pr + c // 4][:, (c % 4) * 128:(c % 4) * 128 + 128],
                                          p.xnf[c][:, scol], p.ident[:])
                    ins.then_inc(s_ptr, 1)

        @blk.gpsimd
        def _(gp):
            for gs in range(32):
                gp.wait_ge(s_tev, gs + 1)
                row0 = gs * 128
                gp.dma_start(p.xn_tm[row0:row0 + 128, :], xtmb[gs % 4][:]).then_inc(s_xtm, 16)
                if p.tap_xn is not None:
                    gp.dma_start(p.tap_xn[row0:row0 + 128, :], xtmb[gs % 4][:]).then_inc(s_xtm, 16)
            gp.wait_ge(s_xtm, 32 * XM)
    pctx.close()


def build_p2(p):
    nc, sem = p.nc, p.sem
    pctx = ExitStack()
    sb = lambda name, shape, dt: pctx.enter_context(nc.sbuf_tensor(name, shape, dt))
    ps = lambda name, shape, dt=F32: pctx.enter_context(nc.psum_tensor(name, shape, dt))

    vtm = sb("p2_vtm", [128, 16 * 512], BF16)
    qsb = [sb(f"p2_q{j}", [128, S], BF16) for j in range(4)]
    ksb = [sb(f"p2_k{j}", [128, S], BF16) for j in range(4)]
    wqt = [sb(f"p2_wq{i}", [128, 1024], BF16) for i in range(8)]
    wvr = sb("p2_wvr", [128, 8 * 512], BF16)
    e1sb = [sb(f"p2_e1_{i}", [128, 512], BF16) for i in range(2)]
    e2sb = [sb(f"p2_e2_{i}", [128, 512], BF16) for i in range(2)]
    tmpa = sb("p2_tmpa", [64, 512], F32)
    tmpb = sb("p2_tmpb", [64, 512], F32)
    tmpc = sb("p2_tmpc", [128, 512], F32)
    tmpd = sb("p2_tmpd", [128, 512], F32)
    zr1 = sb("p2_zr1", [1, 512], F32)
    zr2 = sb("p2_zr2", [1, 512], F32)
    zb1 = sb("p2_zb1", [128, 512], F32)
    zb2 = sb("p2_zb2", [128, 512], F32)
    outh = [sb(f"p2_outh{i}", [128, 512], BF16) for i in range(4)]

    bkh = [ps(f"p2_bk{i}", [128, 512], F32) for i in range(8)]
    bk = [h[:] for h in bkh]
    pZ1 = bkh[6][0:1, :]
    pZ2 = bkh[7][0:1, :]

    s_wq = sem("p2_wq")
    s_wv = sem("p2_wv")
    s_qk = sem("p2_qk")
    s_rot = sem("p2_rot")
    s_vmm = sem("p2_vmm")
    s_vev = sem("p2_vev")
    s_s = sem("p2_s")
    s_exp = sem("p2_exp")
    s_pv = sem("p2_pv")
    s_zr = sem("p2_zr")
    s_bc = sem("p2_bc")
    s_cmb = sem("p2_cmb")
    s_st = sem("p2_st")

    def dcol(g, hg, pr):
        return g * 1024 + hg * 256 + pr * 128

    nG = 8
    for G in range(nG):
        b, hg = G // 4, G % 4

        with nc.Block() as blk:
            @blk.sync
            def _(sync, b=b, hg=hg, G=G):
                for j in range(4):
                    h = hg * 4 + j
                    for qg in range(4):
                        jq = G * 16 + j * 4 + qg
                        sync.wait_ge(s_cmb, jq + 1)
                        sync.dma_start(
                            p.attn_fm[b, h * 128:(h + 1) * 128, qg * 512:(qg + 1) * 512],
                            outh[jq % 4][:]).then_inc(s_st, 16)
                sync.wait_ge(s_st, 16 * 16 * (G + 1))

            @blk.gpsimd
            def _(gp, b=b, hg=hg, G=G):
                for c in range(8):
                    gp.dma_start(wvr[:, c * 512:(c + 1) * 512],
                                 p.Wv_rhs[c * 128:(c + 1) * 128,
                                          hg * 512:(hg + 1) * 512]).then_inc(s_wv, 16)
                for pr in range(2):
                    for g in range(4):
                        i = pr * 4 + g
                        gp.dma_start(
                            wqt[i][:].rearrange("p (c f) -> p c f", f=128),
                            p.WqkT[:, dcol(g, hg, pr):dcol(g, hg, pr) + 128]
                            .rearrange("(c p) f -> p c f", p=128)).then_inc(s_wq, 16)
                for e2 in (2 * G, 2 * G + 1):
                    for hf2 in range(2):
                        gp.dma_start(p.w1b[e2, hf2],
                                     p.W1_d[e2][:, hf2 * 2048:(hf2 + 1) * 2048]
                                     ).then_inc(p.s_wcast, 16)
                    gp.dma_start(p.w2b[e2], p.W2_d[e2]).then_inc(p.s_wcast, 16)

            @blk.tensor
            def _(t, b=b, hg=hg, G=G):
                for tc in range(16):
                    t.wait_ge(s_wv, 16 * 8 * (G + 1))
                    vt = G * 16 + tc
                    if vt >= 2:
                        t.wait_ge(s_vev, vt - 1)
                    if tc == 0 and G >= 1:
                        t.wait_ge(s_pv, 256 * G)
                    for c in range(8):
                        ins = t.matmul(bk[4 + tc % 2],
                                       p.xnf[c][:, b * S + tc * 128:b * S + tc * 128 + 128],
                                       wvr[:, c * 512:(c + 1) * 512],
                                       start=(c == 0), stop=(c == 7))
                    ins.then_inc(s_vmm, 1)
                t.wait_ge(s_wq, 16 * 8 * (G + 1))
                for pr in range(2):
                    for qk in range(2):
                        for tg in range(4):
                            ev = G * 16 + pr * 8 + qk * 4 + tg
                            if ev >= 2:
                                t.wait_ge(s_rot, ev - 1)
                            for gg in range(2):
                                g = 2 * qk + gg
                                i = pr * 4 + g
                                for c in range(8):
                                    ins = t.matmul(bk[(tg % 2) * 2 + gg],
                                                   wqt[i][:, c * 128:(c + 1) * 128],
                                                   p.xnf[c][:, b * S + tg * 512:b * S + tg * 512 + 512],
                                                   start=(c == 0), stop=(c == 7))
                            ins.then_inc(s_qk, 1)
                t.wait_ge(s_rot, G * 16 + 16)
                items = [(j, qg, kc) for j in range(4) for qg in range(4)
                         for kc in range(16)]

                def emit_scores(j, qg, kc):
                    jq = G * 16 + j * 4 + qg
                    ki = jq * 16 + kc
                    if ki >= 2:
                        t.wait_ge(s_exp, 2 * (ki - 1))
                    t.matmul(bk[kc % 2],
                             ksb[j][0:64, kc * 128:(kc + 1) * 128],
                             qsb[j][0:64, qg * 512:(qg + 1) * 512],
                             start=True, stop=True)
                    t.matmul(bk[2 + kc % 2],
                             ksb[j][64:128, kc * 128:(kc + 1) * 128],
                             qsb[j][64:128, qg * 512:(qg + 1) * 512],
                             start=True, stop=True,
                             tile_position=(64, 0)).then_inc(s_s, 1)

                def emit_pv(j, qg, kc):
                    jq = G * 16 + j * 4 + qg
                    ki = jq * 16 + kc
                    t.wait_ge(s_exp, 2 * (ki + 1))
                    t.wait_ge(s_vev, G * 16 + kc + 1)
                    if kc == 0:
                        if jq >= 1:
                            t.wait_ge(s_cmb, jq)
                    t.matmul(bk[4], vtm[:, kc * 512 + j * 128:kc * 512 + j * 128 + 128],
                             e1sb[kc % 2][:], start=(kc == 0), stop=(kc == 15))
                    t.matmul(bk[5], vtm[:, kc * 512 + j * 128:kc * 512 + j * 128 + 128],
                             e2sb[kc % 2][:], start=(kc == 0), stop=(kc == 15))
                    t.matmul(pZ1, p.onesb[:], e1sb[kc % 2][:],
                             start=(kc == 0), stop=(kc == 15))
                    t.matmul(pZ2, p.onesb[:], e2sb[kc % 2][:],
                             start=(kc == 0), stop=(kc == 15)).then_inc(s_pv, 1)
                    if kc == 15:
                        t.wait_ge(s_zr, jq + 1)
                        t.matmul(bk[6], p.onesrow[:], zr1[:], start=True, stop=True)
                        t.matmul(bk[7], p.onesrow[:], zr2[:], start=True, stop=True
                                 ).then_inc(s_bc, 1)

                for idx in range(len(items) + 1):
                    if idx < len(items):
                        emit_scores(*items[idx])
                    if idx >= 1:
                        emit_pv(*items[idx - 1])

            @blk.vector
            def _(v, b=b, hg=hg, G=G):
                for pr in range(2):
                    for qk in range(2):
                        for tg in range(4):
                            need = G * 16 + pr * 8 + qk * 4 + tg + 1
                            v.wait_ge(s_qk, need)
                            tgc = slice(tg * 512, (tg + 1) * 512)
                            for jj in range(2):
                                j = pr * 2 + jj
                                off = jj * 64
                                dst = (qsb if qk == 0 else ksb)[j]
                                A = bkh[(tg % 2) * 2][off:off + 64, :]
                                Bv = bkh[(tg % 2) * 2 + 1][off:off + 64, :]
                                cosx = p.cos[0:64, tgc]
                                sinx = p.sin[0:64, tgc]
                                v.tensor_mul(tmpa[:], A, cosx)
                                v.tensor_mul(tmpb[:], Bv, sinx)
                                v.tensor_sub(dst[0:64, tgc], tmpa[:], tmpb[:])
                                v.tensor_mul(tmpa[:], Bv, cosx)
                                v.tensor_mul(tmpb[:], A, sinx)
                                ins = v.tensor_add(dst[64:128, tgc], tmpa[:], tmpb[:])
                            p.vfence(v, ins)
                            v.engine_nop().then_inc(s_rot, 1)
                for j in range(4):
                    for qg in range(4):
                        jq = G * 16 + j * 4 + qg
                        v.wait_ge(s_pv, jq * 16 + 16)
                        v.reciprocal(zr1[:], pZ1)
                        p.vfence(v, v.reciprocal(zr2[:], pZ2))
                        v.tensor_scalar(zr2[:], zr2[:], p.lam11[:], None, ALU.mult)
                        v.engine_nop().then_inc(s_zr, 1)
                        v.wait_ge(s_bc, jq + 1)
                        v.tensor_copy(zb1[:], bk[6])
                        p.vfence(v, v.tensor_copy(zb2[:], bk[7]))
                        v.tensor_mul(tmpc[:], bk[5], zb2[:])
                        p.vfence(v, v.tensor_mul(tmpd[:], bk[4], zb1[:]))
                        if jq >= 4:
                            v.wait_ge(s_st, 16 * (jq - 3))
                        p.vfence(v, v.tensor_sub(outh[jq % 4][:], tmpd[:], tmpc[:]))
                        v.engine_nop().then_inc(s_cmb, 1)

            @blk.scalar
            def _(s, b=b, hg=hg, G=G):
                for tc in range(16):
                    s.wait_ge(s_vmm, G * 16 + tc + 1)
                    p.afence(s, s.copy(vtm[:, tc * 512:(tc + 1) * 512], bk[4 + tc % 2]))
                    s.nop().then_inc(s_vev, 1)
                for j in range(4):
                    for qg in range(4):
                        jq = G * 16 + j * 4 + qg
                        for kc in range(16):
                            ki = jq * 16 + kc
                            s.wait_ge(s_s, ki + 1)
                            if ki >= 2:
                                s.wait_ge(s_pv, ki - 1)
                            s.activation(e1sb[kc % 2][:], bk[kc % 2], AF.Exp,
                                         bias=p.biasC[:], scale=0.125)
                            p.afence(s, s.activation(e2sb[kc % 2][:], bk[2 + kc % 2],
                                                     AF.Exp, bias=p.biasC[:], scale=0.125))
                            s.nop().then_inc(s_exp, 2)
    pctx.close()


def build_p3(p):
    nc, sem = p.nc, p.sem
    pctx = ExitStack()
    sb = lambda name, shape, dt: pctx.enter_context(nc.sbuf_tensor(name, shape, dt))
    ps = lambda name, shape, dt=F32: pctx.enter_context(nc.psum_tensor(name, shape, dt))

    wpj = [sb(f"p3_w{d}", [128, 1024], BF16) for d in range(16)]
    arb = [[sb(f"p3_a{i}_{d}", [128, 512], BF16) for d in range(16)] for i in range(2)]
    ob = [sb(f"p3_o{i}", [128, 512], F32) for i in range(4)]
    pk = [h[:] for h in [ps(f"p3_p{i}", [128, 512], F32) for i in range(2)]]

    s_w = sem("p3_w")
    s_a = sem("p3_a")
    s_mm = sem("p3_mm")
    s_ev = sem("p3_ev")
    s_st = sem("p3_st")

    with nc.Block() as blk:
        @blk.gpsimd
        def _(gp):
            for d in range(16):
                gp.dma_start(wpj[d][:], p.WprojT[d * 128:(d + 1) * 128, :]
                             ).then_inc(s_w, 16)

        @blk.sync
        def _(sync):
            for bq in range(8):
                b, qg = bq // 4, bq % 4
                if bq >= 2:
                    sync.wait_ge(s_mm, 8 * (bq - 1))
                for d in range(16):
                    sync.dma_start(arb[bq % 2][d][:],
                                   p.attn_fm[b, d * 128:(d + 1) * 128,
                                             qg * 512:(qg + 1) * 512]).then_inc(s_a, 16)

        @blk.tensor
        def _(t):
            t.wait_ge(s_w, 256)
            for bq in range(8):
                t.wait_ge(s_a, 16 * 16 * (bq + 1))
                for c in range(8):
                    m = bq * 8 + c
                    if m >= 2:
                        t.wait_ge(s_ev, m - 1)
                    for d in range(16):
                        ins = t.matmul(pk[m % 2], wpj[d][:, c * 128:(c + 1) * 128],
                                       arb[bq % 2][d][:], start=(d == 0), stop=(d == 15))
                    ins.then_inc(s_mm, 1)

        @blk.vector
        def _(v):
            for bq in range(8):
                for c in range(8):
                    m = bq * 8 + c
                    v.wait_ge(s_mm, m + 1)
                    if m >= 4:
                        v.wait_ge(s_st, 16 * (m - 3))
                    p.vfence(v, v.tensor_copy(ob[m % 4][:], pk[m % 2]))
                    v.engine_nop().then_inc(s_ev, 1)

        @blk.scalar
        def _(sc):
            for bq in range(8):
                b, qg = bq // 4, bq % 4
                for c in range(8):
                    m = bq * 8 + c
                    sc.wait_ge(s_ev, m + 1)
                    sc.dma_start(p.attnproj[b, c * 128:(c + 1) * 128,
                                            qg * 512:(qg + 1) * 512],
                                 ob[m % 4][:]).then_inc(s_st, 16)
            sc.wait_ge(s_st, 16 * 64)
            nst = 64
            if p.tap_attn is not None:
                for b2 in range(B):
                    sc.dma_start(p.tap_attn[b2], p.attn_fm[b2]).then_inc(s_st, 16)
                    nst += 1
            if p.tap_attnproj is not None:
                for b2 in range(B):
                    sc.dma_start(p.tap_attnproj[b2], p.attnproj[b2]).then_inc(s_st, 16)
                    nst += 1
            sc.wait_ge(s_st, 16 * nst)
    pctx.close()


def build_p4(p):
    nc, sem = p.nc, p.sem
    pctx = ExitStack()
    sb = lambda name, shape, dt: pctx.enter_context(nc.sbuf_tensor(name, shape, dt))
    ps = lambda name, shape, dt=F32: pctx.enter_context(nc.psum_tensor(name, shape, dt))

    rtr = sb("p4_rtr", [128, 8 * 16], BF16)
    xstrip = [sb(f"p4_xs{i}", [128, C], BF16) for i in range(2)]
    xfmc = [sb(f"p4_xf{i}", [128, C], BF16) for i in range(2)]
    negm = sb("p4_negm", [128, 1], F32)
    et = sb("p4_et", [128, 16], F32)
    zrow = sb("p4_z", [128, 1], F32)
    ind = sb("p4_ind", [128, 16], F32)
    val = sb("p4_val", [128, 16], F32)
    vmn = sb("p4_vmn", [128, 1], F32)
    tmp16 = sb("p4_tmp16", [128, 16], F32)
    slotp = sb("p4_slotp", [128, 1], F32)
    over = sb("p4_over", [128, 1], F32)
    destf = sb("p4_destf", [128, 1], F32)
    oh = [sb(f"p4_oh{i}", [128, 16], F32) for i in range(32)]
    idxv = [sb(f"p4_idx{i}", [128, 1], F32) for i in range(32)]
    wch = [sb(f"p4_w{i}", [128, 1], F32) for i in range(32)]
    desti = [sb(f"p4_di{i}", [128, 1], I32) for i in range(32)]
    routebuf = [sb(f"p4_rb{i}", [128, 4], F32) for i in range(2)]
    logbuf = [sb(f"p4_lb{i}", [128, 16], F32) for i in range(2)]
    carry = sb("p4_carry", [1, 16], F32)
    gi = [[sb(f"p4_gi{b}_{cs}", [128, 1], I32) for cs in range(4)] for b in range(2)]
    wv = [[sb(f"p4_wv{b}_{cs}", [128, 1], F32) for cs in range(4)] for b in range(2)]
    xetm = [sb(f"p4_xet{cs}", [128, C], BF16) for cs in range(4)]
    xefm = sb("p4_xefm", [128, 8 * CAP], BF16)
    w1s = [[sb(f"p4_w1_{b}_{c}", [128, FF // 2], BF16) for c in range(8)] for b in range(2)]
    w2s = [sb(f"p4_w2_{ff}", [128, C], BF16) for ff in range(32)]
    silu = [sb(f"p4_sil{ff}", [128, CAP], BF16) for ff in range(32)]
    oute = [sb(f"p4_oute{cs}", [128, C], BF16) for cs in range(4)]

    bk = [ps(f"p4_bk{i}", [128, 512], F32) for i in range(6)]
    bb = [ps(f"p4_bb{i}", [128, 1024], BF16) for i in range(2)]
    pL = [bk[4][:, 0:16], bk[5][:, 0:16]]
    pcum = bk[2][:, 0:16]
    ptot = bk[3][0:1, 0:16]
    pH = [bk[0][:], bk[1][:]]
    pO = [bk[2][:], bk[3][:], bk[4][:], bk[5][:]]

    s_init = sem("p4_init"); s_rtr = sem("p4_rtr")
    s_ld = sem("p4_ld"); s_tr = sem("p4_tr"); s_tev = sem("p4_tev")
    s_lg = sem("p4_lg"); s_nm = sem("p4_nm"); s_et = sem("p4_et"); s_oh = sem("p4_oh")
    s_cum = sem("p4_cum"); s_tot = sem("p4_tot"); s_cw = sem("p4_cw"); s_di = sem("p4_di")
    s_sc = sem("p4_sc"); s_gl = sem("p4_gl"); s_ga = sem("p4_ga")
    s_xtr = sem("p4_xtr"); s_xev = sem("p4_xev")
    s_w1p = [sem("p4_w1a"), sem("p4_w1b")]; s_w2 = sem("p4_w2")
    s_gac = [sem(f"p4_gac{i}") for i in range(4)]
    s_sctp = [sem("p4_scta"), sem("p4_sctb")]
    s_h1 = sem("p4_h1"); s_sil = sem("p4_sil"); s_h2 = sem("p4_h2")
    s_oev = sem("p4_oev"); s_sct = sem("p4_sct"); s_tap = sem("p4_tap")
    s_rt = sem("p4_rtp"); s_rt2 = sem("p4_rt2")

    glist_flat = p.glist.rearrange("e (c one) -> (e c) one", one=1)
    wlist_flat = p.wlist.rearrange("e (c one) -> (e c) one", one=1)
    xefm3 = xefm[:].rearrange("p (c f) -> p c f", f=CAP)

    for l in range(NL):
        xsrc = p.xn_tm if l == 0 else p.xt_tm[l - 1]
        xdst = p.xt_tm[l]

        with nc.Block() as blk:
            @blk.gpsimd
            def _(gp, l=l, xdst=xdst):
                gp.dma_start(p.glist[:, :], p.sent_in).then_inc(s_init, 16)
                gp.dma_start(xdst, p.zeros_in).then_inc(s_init, 16)
                gp.dma_start(p.wlist[:, :], p.zeros_in[0:E, 0:CAP]).then_inc(s_init, 16)
                gp.dma_start(rtr[:].rearrange("p (c e) -> p c e", e=16),
                             p.routers_d[l].rearrange("(c p) e -> p c e", p=128)
                             ).then_inc(s_rtr, 16)
                if l == 0 and "p2" not in p.phases:
                    for e2 in range(E):
                        for hf2 in range(2):
                            gp.dma_start(p.w1b[e2, hf2],
                                         p.W1_d[e2][:, hf2 * 2048:(hf2 + 1) * 2048]
                                         ).then_inc(p.s_wcast, 16)
                        gp.dma_start(p.w2b[e2], p.W2_d[e2]).then_inc(p.s_wcast, 16)
                IM = 80 if p.tap_glist is not None else 48
                bc_l = gp.to_reg(E * CAP - 1)
                gp.wait_ge(s_init, IM * l + 48)
                for i in range(32):
                    gp.wait_ge(s_di, 33 * l + i + 2)
                    gp.indirect_dma_start(
                        glist_flat, IndirectOffsetOnAxis(desti[i][:], 0),
                        p.iotatok[:, i:i + 1], None,
                        bounds_check=bc_l, oob_is_err=False).then_inc(s_sc, 16)
                    gp.indirect_dma_start(
                        wlist_flat, IndirectOffsetOnAxis(desti[i][:], 0),
                        wch[i][:], None,
                        bounds_check=bc_l, oob_is_err=False).then_inc(s_sc, 16)
                gp.wait_ge(s_sc, 32 * 32 * (l + 1))
                if p.tap_glist is not None:
                    gp.dma_start(p.tap_glist[l], p.glist[:, :]).then_inc(s_init, 16)
                    gp.dma_start(p.tap_wlist[l], p.wlist[:, :]).then_inc(s_init, 16)
                    gp.wait_ge(s_init, IM * (l + 1))

            @blk.sync
            def _(sync, l=l, xsrc=xsrc):
                for pos in range(33):
                    i = pos % 32
                    gc = 33 * l + pos
                    if gc >= 2:
                        sync.wait_ge(s_tr, gc - 1)
                    sync.dma_start(xstrip[pos % 2][:], xsrc[i * 128:(i + 1) * 128, :]
                                   ).then_inc(s_ld, 16)

            @blk.scalar
            def _(s, l=l):
                for pos in range(33):
                    i = pos % 32
                    gc = 33 * l + pos
                    if l == 0 and p.tap_logits is not None:
                        s.wait_ge(s_lg, gc + 1)
                        if pos >= 2:
                            s.wait_ge(s_rt2, 16 * (pos - 1))
                        s.copy(logbuf[pos % 2][:], pL[pos % 2])
                        s.dma_start(p.tap_logits[i], logbuf[pos % 2][:]).then_inc(s_rt2, 16)
                    s.wait_ge(s_nm, gc + 1)
                    if gc >= 1:
                        s.wait_ge(s_oh, gc)
                    p.afence(s, s.activation(et[:], pL[pos % 2], AF.Exp, bias=negm[:],
                                              accum_out=zrow[:]))
                    s.nop().then_inc(s_et, 1)

            @blk.tensor
            def _(t, l=l):
                def emit_transposes(pos, gc):
                    t.wait_ge(s_ld, 16 * (gc + 1))
                    if gc >= 2:
                        t.wait_ge(s_tev, gc - 1)
                    for c in range(8):
                        ins = t.transpose(bb[pos % 2][:, c * 128:(c + 1) * 128],
                                          xstrip[pos % 2][:, c * 128:(c + 1) * 128],
                                          p.ident[:])
                    ins.then_inc(s_tr, 1)

                def emit_logits(pos, gc):
                    t.wait_ge(s_tev, gc + 1)
                    if pos == 0:
                        t.wait_ge(s_rtr, 16 * (l + 1))
                    if gc >= 2:
                        t.wait_ge(s_oh, gc - 1)
                        t.wait_ge(s_et, gc - 1)
                    for c in range(8):
                        ins = t.matmul(pL[pos % 2], xfmc[pos % 2][:, c * 128:(c + 1) * 128],
                                       rtr[:, c * 16:(c + 1) * 16],
                                       start=(c == 0), stop=(c == 7))
                    ins.then_inc(s_lg, 1)
                for pos in range(33):
                    emit_transposes(pos, 33 * l + pos)
                    if pos >= 1:
                        emit_logits(pos - 1, 33 * l + pos - 1)
                emit_logits(32, 33 * l + 32)
                for pos in range(33):
                    i = max(0, pos - 1)
                    gc = 33 * l + pos
                    t.wait_ge(s_oh, 33 * (l + 1))
                    t.wait_ge(s_cw, gc + 1)
                    t.matmul(pcum, p.u128[:], oh[i][:], start=True, stop=False)
                    t.matmul(pcum, p.onesrow[:], carry[:], start=False, stop=True
                             ).then_inc(s_cum, 1)
                    t.matmul(ptot, p.onesf[:], oh[i][:], start=True,
                             stop=True).then_inc(s_tot, 1)

            @blk.vector
            def _(v, l=l):
                for pos in range(33):
                    i = pos % 32
                    gc = 33 * l + pos
                    v.wait_ge(s_tr, gc + 1)
                    if gc >= 2:
                        v.wait_ge(s_lg, gc - 1)
                    p.vfence(v, v.tensor_copy(xfmc[pos % 2][:], bb[pos % 2][:, 0:1024]))
                    v.engine_nop().then_inc(s_tev, 1)
                    v.wait_ge(s_lg, gc + 1)
                    p.vfence(v, v.reduce_max(negm[:], pL[pos % 2], AX.X, negate=True))
                    v.engine_nop().then_inc(s_nm, 1)
                    v.wait_ge(s_et, gc + 1)
                    p.vfence(v, v.reciprocal(wch[i][:], zrow[:]))
                    p.vfence(v, v.tensor_scalar(tmp16[:], pL[pos % 2], negm[:], 1e8,
                                                ALU.add, ALU.mult))
                    p.vfence(v, v.tensor_scalar(ind[:], tmp16[:], 1.0, 0.0,
                                                ALU.add, ALU.max))
                    p.vfence(v, v.tensor_mul(val[:], ind[:], p.iota16[:]))
                    p.vfence(v, v.reduce_max(vmn[:], val[:], AX.X))
                    p.vfence(v, v.tensor_scalar(over[:], vmn[:], -1.0, 1.0,
                                                ALU.mult, ALU.add))
                    p.vfence(v, v.tensor_scalar(oh[i][:], val[:], over[:], 0.0,
                                                ALU.add, ALU.max))
                    p.vfence(v, v.tensor_scalar(idxv[i][:], vmn[:], -1.0, 1024.0,
                                                ALU.mult, ALU.add))
                    v.engine_nop().then_inc(s_oh, 1)
                for pos in range(33):
                    i = max(0, pos - 1)
                    gc = 33 * l + pos
                    v.wait_ge(s_oh, 33 * (l + 1))
                    if pos <= 1:
                        p.vfence(v, v.memset(carry[:], 0.0))
                    else:
                        v.wait_ge(s_tot, gc)
                        p.vfence(v, v.tensor_add(carry[:], carry[:], ptot))
                    v.engine_nop().then_inc(s_cw, 1)
                    v.wait_ge(s_cum, gc + 1)
                    p.vfence(v, v.tensor_mul(tmp16[:], pcum, oh[i][:]))
                    p.vfence(v, v.reduce_sum(slotp[:], tmp16[:], AX.X))
                    p.vfence(v, v.tensor_scalar(destf[:], idxv[i][:], float(CAP), slotp[:],
                                                ALU.mult, ALU.add))
                    p.vfence(v, v.tensor_scalar(over[:], slotp[:], -(float(CAP) + 0.5), 0.0,
                                                ALU.add, ALU.max))
                    p.vfence(v, v.tensor_scalar(over[:], over[:], 4e4, -1.0,
                                                ALU.mult, ALU.add))
                    p.vfence(v, v.tensor_add(destf[:], destf[:], over[:]))
                    p.vfence(v, v.tensor_copy(desti[i][:], destf[:]))
                    v.engine_nop().then_inc(s_di, 1)

        if l == NL - 1 and p.tap_desti is not None:
            with nc.Block() as blkd:
                @blkd.sync
                def _(sync):
                    n = 0
                    for i2 in range(32):
                        sync.dma_start(p.tap_desti[i2].rearrange("(x y) -> x y", y=1),
                                       desti[i2][:]).then_inc(s_rt, 16)
                        n += 16
                        if p.tap_oh is not None:
                            sync.dma_start(p.tap_oh[i2], oh[i2][:]).then_inc(s_rt, 16)
                            n += 16
                        if p.tap_route is not None:
                            with nc.allow_non_contiguous_dma(reason="debug tap"):
                                sync.dma_start(p.tap_route[i2][:, 0:1], idxv[i2][:]
                                               ).then_inc(s_rt, 16)
                            n += 16
                    if p.tap_stats is not None:
                        slabs = [p.iota16[:], ind[:], val[:], tmp16[:], et[:]]
                        cols = [negm[:], vmn[:], over[:], zrow[:]]
                        for j2, t2 in enumerate(slabs):
                            sync.dma_start(p.tap_stats[j2], t2).then_inc(s_rt, 16)
                            n += 16
                        with nc.allow_non_contiguous_dma(reason="debug tap"):
                            for j2, t2 in enumerate(cols):
                                sync.dma_start(p.tap_stats[5][:, j2:j2 + 1], t2
                                               ).then_inc(s_rt, 16)
                                n += 16
                    sync.wait_ge(s_rt, n)
        if "b" not in P4_PARTS:
            continue
        with nc.Block() as blk:
            @blk.sync
            def _(sync, l=l):
                for e in range(E):
                    u = 16 * l + e
                    for cs in range(4):
                        if u >= 2:
                            sync.wait_ge(s_sctp[u % 2], 16 * 4 * ((u - 2) // 2 + 1))
                        sync.dma_start(gi[e % 2][cs][:],
                                       p.glist[e, cs * 128:(cs + 1) * 128]
                                       .rearrange("(x y) -> x y", y=1)).then_inc(s_gl, 16)
                        sync.dma_start(wv[e % 2][cs][:],
                                       p.wlist[e, cs * 128:(cs + 1) * 128]
                                       .rearrange("(x y) -> x y", y=1)).then_inc(s_gl, 16)
                    if l == 0:
                        sync.wait_ge(p.s_wcast, 16 * 3 * (e + 1))
                    for hf in range(2):
                        hidx = 2 * u + hf
                        if hidx >= 2:
                            sync.wait_ge(s_h1, (hidx - 1) * 16)
                        for c in range(8):
                            sync.dma_start(w1s[hidx % 2][c][:],
                                           p.w1b[e, hf, c * 128:(c + 1) * 128, :]
                                           ).then_inc(s_w1p[hidx % 2], 16)
                    for ff in range(32):
                        if u >= 1:
                            sync.wait_ge(s_h2, (u - 1) * 64 + 32 + ff + 1)
                        sync.dma_start(w2s[ff][:],
                                       p.w2b[e, ff * 128:(ff + 1) * 128, :]
                                       ).then_inc(s_w2, 16)

            @blk.gpsimd
            def _(gp, l=l, xsrc=xsrc, xdst=xdst):
                bc_t = gp.to_reg(T - 1)

                def emit_gathers(e):
                    u = 16 * l + e
                    for cs in range(4):
                        gp.wait_ge(s_gl, 16 * (8 * u + 2 * cs + 1))
                        if u >= 1:
                            gp.wait_ge(s_xtr, 4 * (u - 1) + cs + 1)
                        gp.indirect_dma_start(
                            xetm[cs][:], None, xsrc,
                            IndirectOffsetOnAxis(gi[e % 2][cs][:], 0),
                            bounds_check=bc_t, oob_is_err=False).then_inc(s_gac[cs], 16)

                def emit_scatters(e):
                    u = 16 * l + e
                    for cs in range(4):
                        gp.wait_ge(s_oev, 8 * u + 2 * (cs + 1))
                        gp.indirect_dma_start(
                            xdst, IndirectOffsetOnAxis(gi[e % 2][cs][:], 0),
                            oute[cs][:], None,
                            bounds_check=bc_t, oob_is_err=False).then_inc(s_sctp[u % 2], 16)

                emit_gathers(0)
                for e in range(1, E):
                    emit_gathers(e)
                    emit_scatters(e - 1)
                emit_scatters(E - 1)
                gp.wait_ge(s_sctp[0], 16 * 4 * 8 * (l + 1))
                gp.wait_ge(s_sctp[1], 16 * 4 * 8 * (l + 1))
                if p.tap_xt[l] is not None:
                    gp.dma_start(p.tap_xt[l], xdst).then_inc(s_tap, 16)
                    gp.wait_ge(s_tap, 16 * (l + 1))

            @blk.tensor
            def _(t, l=l):
                def emit_transposes(e):
                    u = 16 * l + e
                    for cs in range(4):
                        t.wait_ge(s_gac[cs], 16 * (u + 1))
                        if 4 * u + cs >= 2:
                            t.wait_ge(s_xev, 4 * u + cs - 1)
                        for c in range(8):
                            ins = t.transpose(bb[cs % 2][:, c * 128:(c + 1) * 128],
                                              xetm[cs][:, c * 128:(c + 1) * 128],
                                              p.ident[:])
                        ins.then_inc(s_xtr, 1)

                for e in range(E):
                    u = 16 * l + e
                    if e == 0:
                        emit_transposes(0)
                    t.wait_ge(s_xev, 4 * u + 4)
                    for ff in range(32):
                        fidx = 32 * u + ff
                        hidx = 2 * u + ff // 16
                        t.wait_ge(s_w1p[hidx % 2], 16 * 8 * (hidx // 2 + 1))
                        if fidx >= 2:
                            t.wait_ge(s_sil, fidx - 1)
                        for c in range(8):
                            ins = t.matmul(pH[ff % 2],
                                           w1s[hidx % 2][c][:, (ff % 16) * 128:(ff % 16) * 128 + 128],
                                           xefm[:, c * CAP:(c + 1) * CAP],
                                           start=(c == 0), stop=(c == 7))
                        ins.then_inc(s_h1, 1)
                    if e + 1 < E:
                        emit_transposes(e + 1)
                    for ch in range(2):
                        for ff in range(32):
                            t.wait_ge(s_sil, 32 * u + ff + 1)
                            if ch == 0 and ff == 0:
                                t.wait_ge(s_w2, 16 * 32 * (u + 1))
                            if ch == 1 and ff == 0:
                                t.wait_ge(s_oev, 8 * u + 4)
                            for cs in range(2):
                                for dh in range(2):
                                    ins = t.matmul(
                                        pO[2 * cs + dh],
                                        silu[ff][:, (ch * 2 + cs) * 128:(ch * 2 + cs) * 128 + 128],
                                        w2s[ff][:, dh * 512:(dh + 1) * 512],
                                        start=(ff == 0), stop=(ff == 31))
                            ins.then_inc(s_h2, 1)

            @blk.scalar
            def _(s, l=l):
                for e in range(E):
                    u = 16 * l + e
                    for ff in range(32):
                        fidx = 32 * u + ff
                        s.wait_ge(s_h1, fidx + 1)
                        if u >= 1:
                            s.wait_ge(s_h2, (u - 1) * 64 + 32 + ff + 1)
                        p.afence(s, s.activation(silu[ff][:], pH[ff % 2], AF.Silu))
                        s.nop().then_inc(s_sil, 1)

            @blk.vector
            def _(v, l=l):
                def emit_evacs(e):
                    u = 16 * l + e
                    for cs in range(4):
                        v.wait_ge(s_xtr, 4 * u + cs + 1)
                        if u >= 1 and cs == 0:
                            v.wait_ge(s_h1, u * 32)
                        p.vfence(v, v.tensor_copy(xefm3[:, :, cs * 128:cs * 128 + 128],
                                                   bb[cs % 2][:, 0:1024]
                                                   .rearrange("p (c f) -> p c f", f=128)))
                        v.engine_nop().then_inc(s_xev, 1)

                emit_evacs(0)
                for e in range(E):
                    u = 16 * l + e
                    if e + 1 < E:
                        emit_evacs(e + 1)
                    for ch in range(2):
                        v.wait_ge(s_h2, 64 * u + 32 * (ch + 1))
                        for cs in range(2):
                            sc = ch * 2 + cs
                            if u >= 1:
                                v.wait_ge(s_sctp[(u - 1) % 2], 16 * 4 * ((u - 1) // 2 + 1))
                            for dh in range(2):
                                p.vfence(v, v.tensor_scalar(
                                    oute[sc][:, dh * 512:(dh + 1) * 512],
                                    pO[2 * cs + dh], wv[e % 2][sc][:], None, ALU.mult))
                                v.engine_nop().then_inc(s_oev, 1)
    pctx.close()


def build_p5(p):
    nc, sem = p.nc, p.sem
    pctx = ExitStack()
    sb = lambda name, shape, dt: pctx.enter_context(nc.sbuf_tensor(name, shape, dt))
    ps = lambda name, shape, dt=F32: pctx.enter_context(nc.psum_tensor(name, shape, dt))

    xld = [[sb(f"p5_x{i}_{c}", [128, 512], F32) for c in range(8)] for i in range(2)]
    ald = [[sb(f"p5_a{i}_{c}", [128, 512], F32) for c in range(8)] for i in range(2)]
    fst = [sb(f"p5_f{i}", [128, C], BF16) for i in range(2)]
    ob = [sb(f"p5_o{i}", [128, 512], F32) for i in range(8)]
    pF = [h[:] for h in [ps(f"p5_p{c}", [128, 512], BF16) for c in range(8)]]

    s_ld = sem("p5_ld")
    s_fsp = [sem("p5_fsa"), sem("p5_fsb")]
    s_tr = sem("p5_tr")
    s_add = sem("p5_add")
    s_st = sem("p5_st")

    with nc.Block() as blk:
        @blk.sync
        def _(sync):
            for bq in range(8):
                b, qg = bq // 4, bq % 4
                if bq >= 2:
                    sync.wait_ge(s_add, 8 * (bq - 1))
                for c in range(8):
                    sync.dma_start(xld[bq % 2][c][:],
                                   p.x_fm[b, c * 128:(c + 1) * 128,
                                          qg * 512:(qg + 1) * 512]).then_inc(s_ld, 16)
                    sync.dma_start(ald[bq % 2][c][:],
                                   p.attnproj[b, c * 128:(c + 1) * 128,
                                              qg * 512:(qg + 1) * 512]).then_inc(s_ld, 16)

        @blk.gpsimd
        def _(gp):
            for bq in range(8):
                for sub in range(4):
                    fi = bq * 4 + sub
                    if fi >= 2:
                        gp.wait_ge(s_tr, fi - 1)
                    row0 = (bq // 4) * S + (bq % 4) * 512 + sub * 128
                    gp.dma_start(fst[fi % 2][:], p.xt_tm[NL - 1][row0:row0 + 128, :]
                                 ).then_inc(s_fsp[fi % 2], 16)

        @blk.tensor
        def _(t):
            for bq in range(8):
                if bq >= 1:
                    t.wait_ge(s_add, 8 * bq)
                for sub in range(4):
                    fi = bq * 4 + sub
                    t.wait_ge(s_fsp[fi % 2], 16 * (fi // 2 + 1))
                    for c in range(8):
                        ins = t.transpose(pF[c][:, sub * 128:(sub + 1) * 128] if False else pF[c].tensor[0:128, sub * 128:(sub + 1) * 128],
                                          fst[fi % 2][:, c * 128:(c + 1) * 128],
                                          p.ident[:])
                    ins.then_inc(s_tr, 1)

        @blk.vector
        def _(v):
            for bq in range(8):
                v.wait_ge(s_ld, 16 * 16 * (bq + 1))
                v.wait_ge(s_tr, 4 * (bq + 1))
                for c in range(8):
                    m = bq * 8 + c
                    if m >= 8:
                        v.wait_ge(s_st, 16 * (m - 7))
                    v.tensor_add(ob[m % 8][:], xld[bq % 2][c][:], ald[bq % 2][c][:])
                    p.vfence(v, v.tensor_add(ob[m % 8][:], ob[m % 8][:], pF[c]))
                    v.engine_nop().then_inc(s_add, 1)

        @blk.scalar
        def _(sc):
            for bq in range(8):
                b, qg = bq // 4, bq % 4
                for c in range(8):
                    m = bq * 8 + c
                    sc.wait_ge(s_add, m + 1)
                    sc.dma_start(p.out_fm[b, c * 128:(c + 1) * 128,
                                          qg * 512:(qg + 1) * 512],
                                 ob[m % 8][:]).then_inc(s_st, 16)
            sc.wait_ge(s_st, 16 * 64)
    pctx.close()


def kernel(**inputs):
    in_maps = _host_prep(inputs)
    nc = build()
    res = run_bass_kernel_spmd(nc, in_maps, core_ids=list(range(NCORES)))
    outs = []
    for c in range(NCORES):
        o = np.asarray(res.results[c]["out_fm"], np.float32)
        outs.append(o.transpose(0, 2, 1))
    return np.ascontiguousarray(np.concatenate(outs, 0))


# version 07
# speedup vs baseline: 1.6573x; 1.0261x over previous
import numpy as np
import ml_dtypes
from contextlib import ExitStack

import concourse.bass as bass
import concourse.mybir as mybir
from concourse.bass_utils import run_bass_kernel_spmd
from concourse.bass import IndirectOffsetOnAxis

F32 = mybir.dt.float32
BF16 = mybir.dt.bfloat16
I32 = mybir.dt.int32
AF = mybir.ActivationFunctionType
ALU = mybir.AluOpType
AX = mybir.AxisListType

NCORES = 8
B = 2
S = 2048
C = 1024
NH = 16
HD = 64
T = B * S
E = 16
NL = 4
FF = 4096
CAP = 512
EXP_SHIFT = -32.0
SENT = 1 << 20



def _host_constants():
    pos = np.arange(S, dtype=np.float64)
    inv_freq = 1.0 / 10000.0 ** (np.arange(0, 2 * HD, 2, dtype=np.float64) / (2 * HD))
    freqs = np.outer(pos, inv_freq)
    cos0 = np.cos(freqs).astype(np.float32).T
    sin0 = np.sin(freqs).astype(np.float32).T
    cos2 = np.ascontiguousarray(np.concatenate([cos0, cos0], 0))
    sin2 = np.ascontiguousarray(np.concatenate([sin0, sin0], 0))
    ident = np.eye(128, dtype=ml_dtypes.bfloat16)
    u128 = np.triu(np.ones((128, 128), np.float32))
    iota16 = (1024.0 - np.broadcast_to(np.arange(16, dtype=np.float32), (128, 16))).copy()
    iota_tok = (np.arange(32)[None, :] * 128 + np.arange(128)[:, None]).astype(np.int32)
    sent = np.full((E, CAP), SENT, np.int32)
    zeros_tm = np.zeros((T, C), ml_dtypes.bfloat16)
    return dict(cos2=cos2, sin2=sin2, ident=ident, u128=u128, iota16=iota16,
                iota_tok=np.ascontiguousarray(iota_tok), sent=sent, zeros_tm=zeros_tm)


def _host_prep(inputs):
    x = np.asarray(inputs["L0_attn_x"], np.float32)
    Wqkv = np.asarray(inputs["L0_attn_W_qkv"], np.float32)
    Wproj = np.asarray(inputs["L0_attn_W_proj"], np.float32)
    WqkT = np.ascontiguousarray(Wqkv[:4 * C].T)
    j = np.arange(2 * C)
    hh = j // 128
    dv = j % 128
    src = (4 + dv // HD) * C + hh * HD + (dv % HD)
    Wv_rhs = np.ascontiguousarray(Wqkv[src].T)
    WprojT = np.ascontiguousarray(Wproj.T)

    shared = dict(WqkT=WqkT, Wv_rhs=Wv_rhs, WprojT=WprojT,
                  routers=np.asarray(inputs["L0_ffn_routers"], np.float32),
                  W1=np.asarray(inputs["L0_ffn_W1"], np.float32),
                  W2=np.asarray(inputs["L0_ffn_W2"], np.float32),
                  lq1=np.asarray(inputs["L0_attn_lambda_q1"], np.float32).reshape(1, HD),
                  lq2=np.asarray(inputs["L0_attn_lambda_q2"], np.float32).reshape(1, HD),
                  lk1=np.asarray(inputs["L0_attn_lambda_k1"], np.float32).reshape(1, HD),
                  lk2=np.asarray(inputs["L0_attn_lambda_k2"], np.float32).reshape(1, HD),
                  **_host_constants())
    in_maps = []
    for c in range(NCORES):
        m = dict(shared)
        m["x_fm"] = np.ascontiguousarray(x[2 * c:2 * c + 2].transpose(0, 2, 1))
        in_maps.append(m)
    return in_maps


class P:
    pass


P4_PARTS = "ab"


def build(debug_taps=(), phases=("p0", "p1", "p2", "p3", "p4", "p5")):
    p = P()
    nc = bass.Bass("TRN2", target_bir_lowering=False, detect_race_conditions=False)
    p.nc = nc
    p.taps = set(debug_taps)

    tmp_sems = []
    try:
        for _ in range(256):
            tmp_sems.append(nc.alloc_semaphore())
    except Exception:
        pass
    with nc.Block() as _blk0:
        @_blk0.gpsimd
        def _(g):
            g.engine_nop()
            nc.clear_and_free_semaphores(tmp_sems)

    def din(name, shape, dt):
        return nc.dram_tensor(name, shape, dt, kind="ExternalInput").ap()
    p.x_fm = din("x_fm", [B, C, S], F32)
    p.WqkT = din("WqkT", [C, 4 * C], F32)
    p.Wv_rhs = din("Wv_rhs", [C, 2 * C], F32)
    p.WprojT = din("WprojT", [2 * C, C], F32)
    p.routers_d = din("routers", [NL, C, E], F32)
    p.W1_d = din("W1", [E, C, FF], F32)
    p.W2_d = din("W2", [E, FF, C], F32)
    p.lq1 = din("lq1", [1, HD], F32)
    p.lq2 = din("lq2", [1, HD], F32)
    p.lk1 = din("lk1", [1, HD], F32)
    p.lk2 = din("lk2", [1, HD], F32)
    p.cos2 = din("cos2", [128, S], F32)
    p.sin2 = din("sin2", [128, S], F32)
    p.ident_in = din("ident", [128, 128], BF16)
    p.u128_in = din("u128", [128, 128], F32)
    p.iota16_in = din("iota16", [128, 16], F32)
    p.iotatok_in = din("iota_tok", [128, 32], I32)
    p.sent_in = din("sent", [E, CAP], I32)
    p.zeros_in = din("zeros_tm", [T, C], BF16)

    p.out_fm = nc.dram_tensor("out_fm", [B, C, S], F32, kind="ExternalOutput").ap()
    p.xn_tm = nc.dram_tensor("xn_tm", [T, C], BF16).ap()
    p.xt_tm = [nc.dram_tensor(f"xt{i}_tm", [T, C], BF16).ap() for i in range(NL)]
    p.attn_fm = nc.dram_tensor("attn_fm", [B, 2 * C, S], BF16).ap()
    p.attnproj = nc.dram_tensor("attnproj", [B, C, S], F32).ap()
    p.glist = nc.dram_tensor("glist", [E, CAP], I32).ap()
    p.w1b = nc.dram_tensor("w1b", [E, 2, C, FF // 2], BF16).ap()
    p.w2b = nc.dram_tensor("w2b", [E, FF, C], BF16).ap()
    p.wlist = nc.dram_tensor("wlist", [E, CAP], F32).ap()

    def tap(name, shape, dt):
        if name in p.taps:
            return nc.dram_tensor("tap_" + name, shape, dt, kind="ExternalOutput").ap()
        return None
    p.tap_xn = tap("xn", [T, C], BF16)
    p.tap_glist = tap("glist", [NL, E, CAP], I32)
    p.tap_wlist = tap("wlist" if "glist" not in p.taps else "glist_w", [NL, E, CAP], F32)
    if "glist" in p.taps and p.tap_wlist is None:
        p.tap_wlist = nc.dram_tensor("tap_wlist", [NL, E, CAP], F32, kind="ExternalOutput").ap()
    p.tap_xt = [tap(f"xt{i}", [T, C], BF16) for i in range(NL)]
    p.tap_attn = tap("attn", [B, 2 * C, S], BF16)
    p.tap_attnproj = tap("attnproj", [B, C, S], F32)
    p.tap_route = tap("route", [32, 128, 4], F32)
    p.tap_oh = tap("oh", [32, 128, 16], F32)
    p.tap_logits = tap("logits", [32, 128, 16], F32)
    p.tap_xfm = tap("xfm", [32, 128, C], BF16)
    p.tap_xstrip = tap("xstrip", [32, 128, C], BF16)
    p.tap_desti = tap("desti", [32, 128], I32)
    p.tap_stats = tap("stats", [6, 128, 16], F32)

    with ExitStack() as ctx:
        p.ctx = ctx
        p.sb = lambda name, shape, dt: ctx.enter_context(nc.sbuf_tensor(name, shape, dt))
        p.ps = lambda name, shape, dt=F32: ctx.enter_context(nc.psum_tensor(name, shape, dt))
        p.sem = lambda name: ctx.enter_context(nc.semaphore(name))
        p.s_wcast = p.sem("wcast")

        p.s_vf = p.sem("dve_fence")
        p.vf_n = [0]

        def vfence(v, ins):
            p.vf_n[0] += 1
            ins.then_inc(p.s_vf, 1)
            v.wait_ge(p.s_vf, p.vf_n[0])
        p.vfence = vfence
        p.s_af = p.sem("act_fence")
        p.af_n = [0]

        def afence(sc, ins):
            p.af_n[0] += 1
            ins.then_inc(p.s_af, 1)
            sc.wait_ge(p.s_af, p.af_n[0])
        p.afence = afence

        sb = p.sb
        p.guard = sb("guardpad", [128, 3072], BF16)
        p.ident = sb("identsb", [128, 128], BF16)
        p.u128 = sb("u128sb", [128, 128], F32)
        p.iota16 = sb("iota16sb", [128, 16], F32)
        p.iotatok = sb("iotatoksb", [128, 32], I32)
        p.onesb = sb("onesbsb", [128, 1], BF16)
        p.onesf = sb("onesfsb", [128, 1], F32)
        p.onesrow = sb("onesrowsb", [1, 128], F32)
        p.biasC = sb("biasCsb", [128, 1], F32)
        p.lamcol = sb("lamcolsb", [128, 1], F32)
        p.lam11 = sb("lam11sb", [1, 1], F32)
        actx = ExitStack()
        p.xnf = [actx.enter_context(nc.sbuf_tensor(f"xnf{c}", [128, T], BF16))
                 for c in range(8)]
        p.cos = actx.enter_context(nc.sbuf_tensor("cossb", [128, S], F32))
        p.sin = actx.enter_context(nc.sbuf_tensor("sinsb", [128, S], F32))

        p.phases = phases
        if "p0" in phases:
            build_p0(p)
        if "p1" in phases:
            build_p1(p)
        if "p2" in phases:
            build_p2(p)
        if "p3" in phases:
            build_p3(p)
        actx.close()
        if "p4" in phases:
            build_p4(p)
        if "p5" in phases:
            build_p5(p)
    return nc


def build_p0(p):
    nc, sem = p.nc, p.sem
    from contextlib import ExitStack
    pctx = ExitStack()
    sb = lambda name, shape, dt: pctx.enter_context(nc.sbuf_tensor(name, shape, dt))
    ps = lambda name, shape, dt=F32: pctx.enter_context(nc.psum_tensor(name, shape, dt))
    p._p0ctx = pctx
    lt = [sb(f"p0_lam{i}", [1, HD], F32) for i in range(4)]
    e12 = sb("p0_e12", [1, 2], F32)
    pb = ps("p0_bc", [128, 1], F32)
    s_ld = sem("p0_ld")
    s_c = sem("p0_c")

    with nc.Block() as blk:
        @blk.sync
        def _(sync):
            sync.dma_start(p.ident[:], p.ident_in).then_inc(s_ld, 16)
            sync.dma_start(p.u128[:], p.u128_in).then_inc(s_ld, 16)
            sync.dma_start(p.iota16[:], p.iota16_in).then_inc(s_ld, 16)
            sync.dma_start(p.iotatok[:], p.iotatok_in).then_inc(s_ld, 16)
            sync.dma_start(p.cos[:], p.cos2).then_inc(s_ld, 16)
            sync.dma_start(p.sin[:], p.sin2).then_inc(s_ld, 16)
            for i, src in enumerate([p.lq1, p.lk1, p.lq2, p.lk2]):
                sync.dma_start(lt[i][:], src).then_inc(s_ld, 16)

        @blk.vector
        def _(v):
            v.memset(p.onesb[:], 1.0)
            v.memset(p.onesf[:], 1.0)
            v.memset(p.onesrow[:], 1.0)
            v.memset(p.biasC[:], EXP_SHIFT)
            v.wait_ge(s_ld, 160)
            v.tensor_mul(lt[0][:], lt[0][:], lt[1][:])
            p.vfence(v, v.tensor_mul(lt[2][:], lt[2][:], lt[3][:]))
            v.reduce_sum(e12[:, 0:1], lt[0][:], AX.X)
            p.vfence(v, v.reduce_sum(e12[:, 1:2], lt[2][:], AX.X))
            v.engine_nop().then_inc(s_c, 1)

        @blk.scalar
        def _(s):
            s.wait_ge(s_c, 1)
            p.afence(s, s.activation(e12[:], e12[:], AF.Exp))
            s.nop().then_inc(s_c, 1)

        @blk.vector
        def _(v2):
            v2.wait_ge(s_c, 2)
            p.vfence(v2, v2.tensor_sub(p.lam11[:], e12[:, 0:1], e12[:, 1:2]))
            v2.tensor_scalar(p.lam11[:], p.lam11[:], 0.2, None, ALU.add).then_inc(s_c, 1)

        @blk.tensor
        def _(t):
            t.wait_ge(s_c, 3)
            t.matmul(pb[:], p.onesrow[:], p.lam11[:], start=True, stop=True).then_inc(s_c, 1)

        @blk.vector
        def _(v3):
            v3.wait_ge(s_c, 4)
            v3.tensor_copy(p.lamcol[:], pb[:])
    pctx.close()


def build_p1(p):
    nc, sem = p.nc, p.sem
    from contextlib import ExitStack
    pctx = ExitStack()
    sb = lambda name, shape, dt: pctx.enter_context(nc.sbuf_tensor(name, shape, dt))
    ps = lambda name, shape, dt=F32: pctx.enter_context(nc.psum_tensor(name, shape, dt))
    NG = 8
    xld = [[sb(f"p1_x{i}_{c}", [128, 512], F32) for c in range(8)] for i in range(2)]
    xbf = [sb(f"p1_xb{c}", [128, 512], BF16) for c in range(8)]
    sqb = [sb(f"p1_sq{c}", [128, 512], BF16) for c in range(8)]
    stat = sb("p1_stat", [1, 512 * 5], F32)
    mean, ex2, var, srt, rstd = (stat[:, i * 512:(i + 1) * 512] for i in range(5))
    xtmb = [sb(f"p1_xtm{i}", [128, C], BF16) for i in range(4)]
    pA = ps("p1_A", [1, 512])
    pB = ps("p1_B", [1, 512])
    pC = ps("p1_C", [128, 512])
    pD = ps("p1_D", [128, 512])
    pE = [ps(f"p1_E{i}", [128, 512], BF16) for i in range(4)]

    s_ld = sem("p1_ld")
    s_cast = sem("p1_cast")
    s_sq = sem("p1_sq")
    s_pe = sem("p1_pe")
    s_stat = sem("p1_stat")
    s_sc = sem("p1_sc")
    s_v2 = sem("p1_v2")
    s_bc = sem("p1_bc")
    s_apl = sem("p1_apl")
    s_ptr = sem("p1_ptr")
    s_tev = sem("p1_tev")
    s_xtm = sem("p1_xtm")
    XM = 32 if p.tap_xn is not None else 16

    def tcol_of(g):
        t0 = (g // 4) * 2048 + (g % 4) * 512
        return slice(t0, t0 + 512)

    with nc.Block() as blk:
        @blk.sync
        def _(sync):
            for g in range(NG):
                b, tg = g // 4, g % 4
                if g >= 2:
                    sync.wait_ge(s_apl, g - 1)
                for c in range(8):
                    sync.dma_start(xld[g % 2][c][:],
                                   p.x_fm[b, c * 128:(c + 1) * 128, tg * 512:(tg + 1) * 512]
                                   ).then_inc(s_ld, 16)

        @blk.vector
        def _(v):
            for g in range(NG):
                v.wait_ge(s_ld, 16 * 8 * (g + 1))
                if g >= 1:
                    v.wait_ge(s_pe, 2 * g - 1)
                for c in range(8):
                    ins = v.tensor_copy(xbf[c][:], xld[g % 2][c][:])
                p.vfence(v, ins)
                v.engine_nop().then_inc(s_cast, 1)
                v.wait_ge(s_pe, 2 * g + 2)
                v.tensor_scalar(mean, pA[:], 1.0 / C, None, ALU.mult)
                p.vfence(v, v.tensor_scalar(ex2, pB[:], 1.0 / C, None, ALU.mult))
                p.vfence(v, v.tensor_mul(var, mean, mean))
                p.vfence(v, v.tensor_sub(var, ex2, var))
                p.vfence(v, v.tensor_scalar(var, var, 1e-5, None, ALU.add))
                v.engine_nop().then_inc(s_stat, 1)
                v.wait_ge(s_sc, g + 1)
                p.vfence(v, v.reciprocal(rstd, srt))
                v.engine_nop().then_inc(s_v2, 1)
                v.wait_ge(s_bc, g + 1)
                tcol = tcol_of(g)
                for c in range(8):
                    v.tensor_sub(xld[g % 2][c][:], xld[g % 2][c][:], pC[:])
                    ins = v.tensor_mul(p.xnf[c][:, tcol], xld[g % 2][c][:], pD[:])
                p.vfence(v, ins)
                v.engine_nop().then_inc(s_apl, 1)
                for sub in range(4):
                    gs = 4 * g + sub
                    pr = 2 * (gs % 2)
                    v.wait_ge(s_ptr, gs + 1)
                    if gs >= 4:
                        v.wait_ge(s_xtm, XM * (gs - 3))
                    v.tensor_copy(xtmb[gs % 4][:, 0:512], pE[pr][:])
                    p.vfence(v, v.tensor_copy(xtmb[gs % 4][:, 512:1024], pE[pr + 1][:]))
                    v.engine_nop().then_inc(s_tev, 1)

        @blk.scalar
        def _(s):
            for g in range(NG):
                s.wait_ge(s_ld, 16 * 8 * (g + 1))
                if g >= 1:
                    s.wait_ge(s_pe, 2 * g)
                for c in range(8):
                    ins = s.activation(sqb[c][:], xld[g % 2][c][:], AF.Square)
                p.afence(s, ins)
                s.nop().then_inc(s_sq, 1)
                s.wait_ge(s_stat, g + 1)
                p.afence(s, s.activation(srt, var, AF.Sqrt))
                s.nop().then_inc(s_sc, 1)

        @blk.tensor
        def _(t):
            for g in range(NG):
                t.wait_ge(s_cast, g + 1)
                if g >= 1:
                    t.wait_ge(s_stat, g)
                for c in range(8):
                    ins = t.matmul(pA[:], p.onesb[:], xbf[c][:], start=(c == 0), stop=(c == 7))
                ins.then_inc(s_pe, 1)
                t.wait_ge(s_sq, g + 1)
                for c in range(8):
                    ins = t.matmul(pB[:], p.onesb[:], sqb[c][:], start=(c == 0), stop=(c == 7))
                ins.then_inc(s_pe, 1)
                t.wait_ge(s_v2, g + 1)
                if g >= 1:
                    t.wait_ge(s_apl, g)
                t.matmul(pC[:], p.onesrow[:], mean, start=True, stop=True)
                t.matmul(pD[:], p.onesrow[:], rstd, start=True, stop=True).then_inc(s_bc, 1)
                t.wait_ge(s_apl, g + 1)
                for sub in range(4):
                    gs = 4 * g + sub
                    pr = 2 * (gs % 2)
                    if gs >= 2:
                        t.wait_ge(s_tev, gs - 1)
                    t0 = (g // 4) * 2048 + (g % 4) * 512 + sub * 128
                    scol = slice(t0, t0 + 128)
                    for c in range(8):
                        ins = t.transpose(pE[pr + c // 4][:, (c % 4) * 128:(c % 4) * 128 + 128],
                                          p.xnf[c][:, scol], p.ident[:])
                    ins.then_inc(s_ptr, 1)

        @blk.gpsimd
        def _(gp):
            for gs in range(32):
                gp.wait_ge(s_tev, gs + 1)
                row0 = gs * 128
                gp.dma_start(p.xn_tm[row0:row0 + 128, :], xtmb[gs % 4][:]).then_inc(s_xtm, 16)
                if p.tap_xn is not None:
                    gp.dma_start(p.tap_xn[row0:row0 + 128, :], xtmb[gs % 4][:]).then_inc(s_xtm, 16)
            gp.wait_ge(s_xtm, 32 * XM)
    pctx.close()


def build_p2(p):
    nc, sem = p.nc, p.sem
    pctx = ExitStack()
    sb = lambda name, shape, dt: pctx.enter_context(nc.sbuf_tensor(name, shape, dt))
    ps = lambda name, shape, dt=F32: pctx.enter_context(nc.psum_tensor(name, shape, dt))

    vtm = sb("p2_vtm", [128, 16 * 512], BF16)
    qsb = [sb(f"p2_q{j}", [128, S], BF16) for j in range(4)]
    ksb = [sb(f"p2_k{j}", [128, S], BF16) for j in range(4)]
    wqt = [sb(f"p2_wq{i}", [128, 1024], BF16) for i in range(8)]
    wvr = sb("p2_wvr", [128, 8 * 512], BF16)
    e1sb = [sb(f"p2_e1_{i}", [128, 512], BF16) for i in range(2)]
    e2sb = [sb(f"p2_e2_{i}", [128, 512], BF16) for i in range(2)]
    tmpa = sb("p2_tmpa", [64, 512], F32)
    tmpb = sb("p2_tmpb", [64, 512], F32)
    tmpc = sb("p2_tmpc", [128, 512], F32)
    tmpd = sb("p2_tmpd", [128, 512], F32)
    zr1 = sb("p2_zr1", [1, 512], F32)
    zr2 = sb("p2_zr2", [1, 512], F32)
    zb1 = sb("p2_zb1", [128, 512], F32)
    zb2 = sb("p2_zb2", [128, 512], F32)
    outh = [sb(f"p2_outh{i}", [128, 512], BF16) for i in range(4)]

    bkh = [ps(f"p2_bk{i}", [128, 512], F32) for i in range(8)]
    bk = [h[:] for h in bkh]
    pZ1 = bkh[6][0:1, :]
    pZ2 = bkh[7][0:1, :]

    s_wq = sem("p2_wq")
    s_wv = sem("p2_wv")
    s_qk = sem("p2_qk")
    s_rot = sem("p2_rot")
    s_vmm = sem("p2_vmm")
    s_vev = sem("p2_vev")
    s_s = sem("p2_s")
    s_exp = sem("p2_exp")
    s_pv = sem("p2_pv")
    s_zr = sem("p2_zr")
    s_bc = sem("p2_bc")
    s_cmb = sem("p2_cmb")
    s_st = sem("p2_st")

    def dcol(g, hg, pr):
        return g * 1024 + hg * 256 + pr * 128

    nG = 8
    for G in range(nG):
        b, hg = G // 4, G % 4

        with nc.Block() as blk:
            @blk.sync
            def _(sync, b=b, hg=hg, G=G):
                for j in range(4):
                    h = hg * 4 + j
                    for qg in range(4):
                        jq = G * 16 + j * 4 + qg
                        sync.wait_ge(s_cmb, jq + 1)
                        sync.dma_start(
                            p.attn_fm[b, h * 128:(h + 1) * 128, qg * 512:(qg + 1) * 512],
                            outh[jq % 4][:]).then_inc(s_st, 16)
                sync.wait_ge(s_st, 16 * 16 * (G + 1))

            @blk.gpsimd
            def _(gp, b=b, hg=hg, G=G):
                for c in range(8):
                    gp.dma_start(wvr[:, c * 512:(c + 1) * 512],
                                 p.Wv_rhs[c * 128:(c + 1) * 128,
                                          hg * 512:(hg + 1) * 512]).then_inc(s_wv, 16)
                for pr in range(2):
                    for g in range(4):
                        i = pr * 4 + g
                        gp.dma_start(
                            wqt[i][:].rearrange("p (c f) -> p c f", f=128),
                            p.WqkT[:, dcol(g, hg, pr):dcol(g, hg, pr) + 128]
                            .rearrange("(c p) f -> p c f", p=128)).then_inc(s_wq, 16)
                for e2 in (2 * G, 2 * G + 1):
                    for hf2 in range(2):
                        gp.dma_start(p.w1b[e2, hf2],
                                     p.W1_d[e2][:, hf2 * 2048:(hf2 + 1) * 2048]
                                     ).then_inc(p.s_wcast, 16)
                    gp.dma_start(p.w2b[e2], p.W2_d[e2]).then_inc(p.s_wcast, 16)

            @blk.tensor
            def _(t, b=b, hg=hg, G=G):
                for tc in range(16):
                    t.wait_ge(s_wv, 16 * 8 * (G + 1))
                    vt = G * 16 + tc
                    if vt >= 2:
                        t.wait_ge(s_vev, vt - 1)
                    if tc == 0 and G >= 1:
                        t.wait_ge(s_pv, 256 * G)
                    for c in range(8):
                        ins = t.matmul(bk[4 + tc % 2],
                                       p.xnf[c][:, b * S + tc * 128:b * S + tc * 128 + 128],
                                       wvr[:, c * 512:(c + 1) * 512],
                                       start=(c == 0), stop=(c == 7))
                    ins.then_inc(s_vmm, 1)
                t.wait_ge(s_wq, 16 * 8 * (G + 1))
                for pr in range(2):
                    for qk in range(2):
                        for tg in range(4):
                            ev = G * 16 + pr * 8 + qk * 4 + tg
                            if ev >= 2:
                                t.wait_ge(s_rot, ev - 1)
                            for gg in range(2):
                                g = 2 * qk + gg
                                i = pr * 4 + g
                                for c in range(8):
                                    ins = t.matmul(bk[(tg % 2) * 2 + gg],
                                                   wqt[i][:, c * 128:(c + 1) * 128],
                                                   p.xnf[c][:, b * S + tg * 512:b * S + tg * 512 + 512],
                                                   start=(c == 0), stop=(c == 7))
                            ins.then_inc(s_qk, 1)
                t.wait_ge(s_rot, G * 16 + 16)
                items = [(j, qg, kc) for j in range(4) for qg in range(4)
                         for kc in range(16)]

                def emit_scores(j, qg, kc):
                    jq = G * 16 + j * 4 + qg
                    ki = jq * 16 + kc
                    if ki >= 2:
                        t.wait_ge(s_exp, 2 * (ki - 1))
                    t.matmul(bk[kc % 2],
                             ksb[j][0:64, kc * 128:(kc + 1) * 128],
                             qsb[j][0:64, qg * 512:(qg + 1) * 512],
                             start=True, stop=True)
                    t.matmul(bk[2 + kc % 2],
                             ksb[j][64:128, kc * 128:(kc + 1) * 128],
                             qsb[j][64:128, qg * 512:(qg + 1) * 512],
                             start=True, stop=True,
                             tile_position=(64, 0)).then_inc(s_s, 1)

                def emit_pv(j, qg, kc):
                    jq = G * 16 + j * 4 + qg
                    ki = jq * 16 + kc
                    t.wait_ge(s_exp, 2 * (ki + 1))
                    t.wait_ge(s_vev, G * 16 + kc + 1)
                    if kc == 0:
                        if jq >= 1:
                            t.wait_ge(s_cmb, jq)
                    t.matmul(bk[4], vtm[:, kc * 512 + j * 128:kc * 512 + j * 128 + 128],
                             e1sb[kc % 2][:], start=(kc == 0), stop=(kc == 15))
                    t.matmul(bk[5], vtm[:, kc * 512 + j * 128:kc * 512 + j * 128 + 128],
                             e2sb[kc % 2][:], start=(kc == 0), stop=(kc == 15))
                    t.matmul(pZ1, p.onesb[:], e1sb[kc % 2][:],
                             start=(kc == 0), stop=(kc == 15))
                    t.matmul(pZ2, p.onesb[:], e2sb[kc % 2][:],
                             start=(kc == 0), stop=(kc == 15)).then_inc(s_pv, 1)
                    if kc == 15:
                        t.wait_ge(s_zr, jq + 1)
                        t.matmul(bk[6], p.onesrow[:], zr1[:], start=True, stop=True)
                        t.matmul(bk[7], p.onesrow[:], zr2[:], start=True, stop=True
                                 ).then_inc(s_bc, 1)

                for idx in range(len(items) + 2):
                    if idx < len(items):
                        emit_scores(*items[idx])
                    if idx >= 2:
                        emit_pv(*items[idx - 2])

            @blk.vector
            def _(v, b=b, hg=hg, G=G):
                for pr in range(2):
                    for qk in range(2):
                        for tg in range(4):
                            need = G * 16 + pr * 8 + qk * 4 + tg + 1
                            v.wait_ge(s_qk, need)
                            tgc = slice(tg * 512, (tg + 1) * 512)
                            for jj in range(2):
                                j = pr * 2 + jj
                                off = jj * 64
                                dst = (qsb if qk == 0 else ksb)[j]
                                A = bkh[(tg % 2) * 2][off:off + 64, :]
                                Bv = bkh[(tg % 2) * 2 + 1][off:off + 64, :]
                                cosx = p.cos[0:64, tgc]
                                sinx = p.sin[0:64, tgc]
                                v.tensor_mul(tmpa[:], A, cosx)
                                v.tensor_mul(tmpb[:], Bv, sinx)
                                v.tensor_sub(dst[0:64, tgc], tmpa[:], tmpb[:])
                                v.tensor_mul(tmpa[:], Bv, cosx)
                                v.tensor_mul(tmpb[:], A, sinx)
                                ins = v.tensor_add(dst[64:128, tgc], tmpa[:], tmpb[:])
                            p.vfence(v, ins)
                            v.engine_nop().then_inc(s_rot, 1)
                for j in range(4):
                    for qg in range(4):
                        jq = G * 16 + j * 4 + qg
                        v.wait_ge(s_pv, jq * 16 + 16)
                        v.reciprocal(zr1[:], pZ1)
                        p.vfence(v, v.reciprocal(zr2[:], pZ2))
                        v.tensor_scalar(zr2[:], zr2[:], p.lam11[:], None, ALU.mult)
                        v.engine_nop().then_inc(s_zr, 1)
                        v.wait_ge(s_bc, jq + 1)
                        v.tensor_copy(zb1[:], bk[6])
                        p.vfence(v, v.tensor_copy(zb2[:], bk[7]))
                        v.tensor_mul(tmpc[:], bk[5], zb2[:])
                        p.vfence(v, v.tensor_mul(tmpd[:], bk[4], zb1[:]))
                        if jq >= 4:
                            v.wait_ge(s_st, 16 * (jq - 3))
                        p.vfence(v, v.tensor_sub(outh[jq % 4][:], tmpd[:], tmpc[:]))
                        v.engine_nop().then_inc(s_cmb, 1)

            @blk.scalar
            def _(s, b=b, hg=hg, G=G):
                for tc in range(16):
                    s.wait_ge(s_vmm, G * 16 + tc + 1)
                    p.afence(s, s.copy(vtm[:, tc * 512:(tc + 1) * 512], bk[4 + tc % 2]))
                    s.nop().then_inc(s_vev, 1)
                for j in range(4):
                    for qg in range(4):
                        jq = G * 16 + j * 4 + qg
                        for kc in range(16):
                            ki = jq * 16 + kc
                            s.wait_ge(s_s, ki + 1)
                            if ki >= 2:
                                s.wait_ge(s_pv, ki - 1)
                            s.activation(e1sb[kc % 2][:], bk[kc % 2], AF.Exp,
                                         bias=p.biasC[:], scale=0.125)
                            p.afence(s, s.activation(e2sb[kc % 2][:], bk[2 + kc % 2],
                                                     AF.Exp, bias=p.biasC[:], scale=0.125))
                            s.nop().then_inc(s_exp, 2)
    pctx.close()


def build_p3(p):
    nc, sem = p.nc, p.sem
    pctx = ExitStack()
    sb = lambda name, shape, dt: pctx.enter_context(nc.sbuf_tensor(name, shape, dt))
    ps = lambda name, shape, dt=F32: pctx.enter_context(nc.psum_tensor(name, shape, dt))

    wpj = [sb(f"p3_w{d}", [128, 1024], BF16) for d in range(16)]
    arb = [[sb(f"p3_a{i}_{d}", [128, 512], BF16) for d in range(16)] for i in range(2)]
    ob = [sb(f"p3_o{i}", [128, 512], F32) for i in range(4)]
    pk = [h[:] for h in [ps(f"p3_p{i}", [128, 512], F32) for i in range(2)]]

    s_w = sem("p3_w")
    s_a = sem("p3_a")
    s_mm = sem("p3_mm")
    s_ev = sem("p3_ev")
    s_st = sem("p3_st")

    with nc.Block() as blk:
        @blk.gpsimd
        def _(gp):
            for d in range(16):
                gp.dma_start(wpj[d][:], p.WprojT[d * 128:(d + 1) * 128, :]
                             ).then_inc(s_w, 16)

        @blk.sync
        def _(sync):
            for bq in range(8):
                b, qg = bq // 4, bq % 4
                if bq >= 2:
                    sync.wait_ge(s_mm, 8 * (bq - 1))
                for d in range(16):
                    sync.dma_start(arb[bq % 2][d][:],
                                   p.attn_fm[b, d * 128:(d + 1) * 128,
                                             qg * 512:(qg + 1) * 512]).then_inc(s_a, 16)

        @blk.tensor
        def _(t):
            t.wait_ge(s_w, 256)
            for bq in range(8):
                t.wait_ge(s_a, 16 * 16 * (bq + 1))
                for c in range(8):
                    m = bq * 8 + c
                    if m >= 2:
                        t.wait_ge(s_ev, m - 1)
                    for d in range(16):
                        ins = t.matmul(pk[m % 2], wpj[d][:, c * 128:(c + 1) * 128],
                                       arb[bq % 2][d][:], start=(d == 0), stop=(d == 15))
                    ins.then_inc(s_mm, 1)

        @blk.vector
        def _(v):
            for bq in range(8):
                for c in range(8):
                    m = bq * 8 + c
                    v.wait_ge(s_mm, m + 1)
                    if m >= 4:
                        v.wait_ge(s_st, 16 * (m - 3))
                    p.vfence(v, v.tensor_copy(ob[m % 4][:], pk[m % 2]))
                    v.engine_nop().then_inc(s_ev, 1)

        @blk.scalar
        def _(sc):
            for bq in range(8):
                b, qg = bq // 4, bq % 4
                for c in range(8):
                    m = bq * 8 + c
                    sc.wait_ge(s_ev, m + 1)
                    sc.dma_start(p.attnproj[b, c * 128:(c + 1) * 128,
                                            qg * 512:(qg + 1) * 512],
                                 ob[m % 4][:]).then_inc(s_st, 16)
            sc.wait_ge(s_st, 16 * 64)
            nst = 64
            if p.tap_attn is not None:
                for b2 in range(B):
                    sc.dma_start(p.tap_attn[b2], p.attn_fm[b2]).then_inc(s_st, 16)
                    nst += 1
            if p.tap_attnproj is not None:
                for b2 in range(B):
                    sc.dma_start(p.tap_attnproj[b2], p.attnproj[b2]).then_inc(s_st, 16)
                    nst += 1
            sc.wait_ge(s_st, 16 * nst)
    pctx.close()


def build_p4(p):
    nc, sem = p.nc, p.sem
    pctx = ExitStack()
    sb = lambda name, shape, dt: pctx.enter_context(nc.sbuf_tensor(name, shape, dt))
    ps = lambda name, shape, dt=F32: pctx.enter_context(nc.psum_tensor(name, shape, dt))

    rtr = sb("p4_rtr", [128, 8 * 16], BF16)
    xstrip = [sb(f"p4_xs{i}", [128, C], BF16) for i in range(2)]
    xfmc = [sb(f"p4_xf{i}", [128, C], BF16) for i in range(2)]
    negm = sb("p4_negm", [128, 1], F32)
    et = sb("p4_et", [128, 16], F32)
    zrow = sb("p4_z", [128, 1], F32)
    ind = sb("p4_ind", [128, 16], F32)
    val = sb("p4_val", [128, 16], F32)
    vmn = sb("p4_vmn", [128, 1], F32)
    tmp16 = sb("p4_tmp16", [128, 16], F32)
    slotp = sb("p4_slotp", [128, 1], F32)
    over = sb("p4_over", [128, 1], F32)
    destf = sb("p4_destf", [128, 1], F32)
    oh = [sb(f"p4_oh{i}", [128, 16], F32) for i in range(32)]
    idxv = [sb(f"p4_idx{i}", [128, 1], F32) for i in range(32)]
    wch = [sb(f"p4_w{i}", [128, 1], F32) for i in range(32)]
    desti = [sb(f"p4_di{i}", [128, 1], I32) for i in range(32)]
    routebuf = [sb(f"p4_rb{i}", [128, 4], F32) for i in range(2)]
    logbuf = [sb(f"p4_lb{i}", [128, 16], F32) for i in range(2)]
    carry = sb("p4_carry", [1, 16], F32)
    gi = [[sb(f"p4_gi{b}_{cs}", [128, 1], I32) for cs in range(4)] for b in range(2)]
    wv = [[sb(f"p4_wv{b}_{cs}", [128, 1], F32) for cs in range(4)] for b in range(2)]
    xetm = [sb(f"p4_xet{cs}", [128, C], BF16) for cs in range(4)]
    xefm = sb("p4_xefm", [128, 8 * CAP], BF16)
    w1s = [[sb(f"p4_w1_{b}_{c}", [128, FF // 2], BF16) for c in range(8)] for b in range(2)]
    w2s = [sb(f"p4_w2_{ff}", [128, C], BF16) for ff in range(32)]
    silu = [sb(f"p4_sil{ff}", [128, CAP], BF16) for ff in range(32)]
    oute = [sb(f"p4_oute{cs}", [128, C], BF16) for cs in range(4)]

    bk = [ps(f"p4_bk{i}", [128, 512], F32) for i in range(6)]
    bb = [ps(f"p4_bb{i}", [128, 1024], BF16) for i in range(2)]
    pL = [bk[4][:, 0:16], bk[5][:, 0:16]]
    pcum = bk[2][:, 0:16]
    ptot = bk[3][0:1, 0:16]
    pH = [bk[0][:], bk[1][:]]
    pO = [bk[2][:], bk[3][:], bk[4][:], bk[5][:]]

    s_init = sem("p4_init"); s_rtr = sem("p4_rtr")
    s_ld = sem("p4_ld"); s_tr = sem("p4_tr"); s_tev = sem("p4_tev")
    s_lg = sem("p4_lg"); s_nm = sem("p4_nm"); s_et = sem("p4_et"); s_oh = sem("p4_oh")
    s_cum = sem("p4_cum"); s_tot = sem("p4_tot"); s_cw = sem("p4_cw"); s_di = sem("p4_di")
    s_sc = sem("p4_sc"); s_gl = sem("p4_gl"); s_ga = sem("p4_ga")
    s_xtr = sem("p4_xtr"); s_xev = sem("p4_xev")
    s_w1p = [sem("p4_w1a"), sem("p4_w1b")]; s_w2 = sem("p4_w2")
    s_gac = [sem(f"p4_gac{i}") for i in range(4)]
    s_sctp = [sem("p4_scta"), sem("p4_sctb")]
    s_h1 = sem("p4_h1"); s_sil = sem("p4_sil"); s_h2 = sem("p4_h2")
    s_oev = sem("p4_oev"); s_sct = sem("p4_sct"); s_tap = sem("p4_tap")
    s_rt = sem("p4_rtp"); s_rt2 = sem("p4_rt2")

    glist_flat = p.glist.rearrange("e (c one) -> (e c) one", one=1)
    wlist_flat = p.wlist.rearrange("e (c one) -> (e c) one", one=1)
    xefm3 = xefm[:].rearrange("p (c f) -> p c f", f=CAP)

    for l in range(NL):
        xsrc = p.xn_tm if l == 0 else p.xt_tm[l - 1]
        xdst = p.xt_tm[l]

        with nc.Block() as blk:
            @blk.gpsimd
            def _(gp, l=l, xdst=xdst):
                gp.dma_start(p.glist[:, :], p.sent_in).then_inc(s_init, 16)
                gp.dma_start(xdst, p.zeros_in).then_inc(s_init, 16)
                gp.dma_start(p.wlist[:, :], p.zeros_in[0:E, 0:CAP]).then_inc(s_init, 16)
                gp.dma_start(rtr[:].rearrange("p (c e) -> p c e", e=16),
                             p.routers_d[l].rearrange("(c p) e -> p c e", p=128)
                             ).then_inc(s_rtr, 16)
                if l == 0 and "p2" not in p.phases:
                    for e2 in range(E):
                        for hf2 in range(2):
                            gp.dma_start(p.w1b[e2, hf2],
                                         p.W1_d[e2][:, hf2 * 2048:(hf2 + 1) * 2048]
                                         ).then_inc(p.s_wcast, 16)
                        gp.dma_start(p.w2b[e2], p.W2_d[e2]).then_inc(p.s_wcast, 16)
                IM = 80 if p.tap_glist is not None else 48
                bc_l = gp.to_reg(E * CAP - 1)
                gp.wait_ge(s_init, IM * l + 48)
                for i in range(32):
                    gp.wait_ge(s_di, 33 * l + i + 2)
                    gp.indirect_dma_start(
                        glist_flat, IndirectOffsetOnAxis(desti[i][:], 0),
                        p.iotatok[:, i:i + 1], None,
                        bounds_check=bc_l, oob_is_err=False).then_inc(s_sc, 16)
                    gp.indirect_dma_start(
                        wlist_flat, IndirectOffsetOnAxis(desti[i][:], 0),
                        wch[i][:], None,
                        bounds_check=bc_l, oob_is_err=False).then_inc(s_sc, 16)
                gp.wait_ge(s_sc, 32 * 32 * (l + 1))
                if p.tap_glist is not None:
                    gp.dma_start(p.tap_glist[l], p.glist[:, :]).then_inc(s_init, 16)
                    gp.dma_start(p.tap_wlist[l], p.wlist[:, :]).then_inc(s_init, 16)
                    gp.wait_ge(s_init, IM * (l + 1))

            @blk.sync
            def _(sync, l=l, xsrc=xsrc):
                for pos in range(33):
                    i = pos % 32
                    gc = 33 * l + pos
                    if gc >= 2:
                        sync.wait_ge(s_tr, gc - 1)
                    sync.dma_start(xstrip[pos % 2][:], xsrc[i * 128:(i + 1) * 128, :]
                                   ).then_inc(s_ld, 16)

            @blk.scalar
            def _(s, l=l):
                for pos in range(33):
                    i = pos % 32
                    gc = 33 * l + pos
                    if l == 0 and p.tap_logits is not None:
                        s.wait_ge(s_lg, gc + 1)
                        if pos >= 2:
                            s.wait_ge(s_rt2, 16 * (pos - 1))
                        s.copy(logbuf[pos % 2][:], pL[pos % 2])
                        s.dma_start(p.tap_logits[i], logbuf[pos % 2][:]).then_inc(s_rt2, 16)
                    s.wait_ge(s_nm, gc + 1)
                    if gc >= 1:
                        s.wait_ge(s_oh, gc)
                    p.afence(s, s.activation(et[:], pL[pos % 2], AF.Exp, bias=negm[:],
                                              accum_out=zrow[:]))
                    s.nop().then_inc(s_et, 1)

            @blk.tensor
            def _(t, l=l):
                def emit_transposes(pos, gc):
                    t.wait_ge(s_ld, 16 * (gc + 1))
                    if gc >= 2:
                        t.wait_ge(s_tev, gc - 1)
                    for c in range(8):
                        ins = t.transpose(bb[pos % 2][:, c * 128:(c + 1) * 128],
                                          xstrip[pos % 2][:, c * 128:(c + 1) * 128],
                                          p.ident[:])
                    ins.then_inc(s_tr, 1)

                def emit_logits(pos, gc):
                    t.wait_ge(s_tev, gc + 1)
                    if pos == 0:
                        t.wait_ge(s_rtr, 16 * (l + 1))
                    if gc >= 2:
                        t.wait_ge(s_oh, gc - 1)
                        t.wait_ge(s_et, gc - 1)
                    for c in range(8):
                        ins = t.matmul(pL[pos % 2], xfmc[pos % 2][:, c * 128:(c + 1) * 128],
                                       rtr[:, c * 16:(c + 1) * 16],
                                       start=(c == 0), stop=(c == 7))
                    ins.then_inc(s_lg, 1)
                for pos in range(33):
                    emit_transposes(pos, 33 * l + pos)
                    if pos >= 1:
                        emit_logits(pos - 1, 33 * l + pos - 1)
                emit_logits(32, 33 * l + 32)
                for pos in range(33):
                    i = max(0, pos - 1)
                    gc = 33 * l + pos
                    t.wait_ge(s_oh, 33 * (l + 1))
                    t.wait_ge(s_cw, gc + 1)
                    t.matmul(pcum, p.u128[:], oh[i][:], start=True, stop=False)
                    t.matmul(pcum, p.onesrow[:], carry[:], start=False, stop=True
                             ).then_inc(s_cum, 1)
                    t.matmul(ptot, p.onesf[:], oh[i][:], start=True,
                             stop=True).then_inc(s_tot, 1)

            @blk.vector
            def _(v, l=l):
                for pos in range(33):
                    i = pos % 32
                    gc = 33 * l + pos
                    v.wait_ge(s_tr, gc + 1)
                    if gc >= 2:
                        v.wait_ge(s_lg, gc - 1)
                    p.vfence(v, v.tensor_copy(xfmc[pos % 2][:], bb[pos % 2][:, 0:1024]))
                    v.engine_nop().then_inc(s_tev, 1)
                    v.wait_ge(s_lg, gc + 1)
                    p.vfence(v, v.reduce_max(negm[:], pL[pos % 2], AX.X, negate=True))
                    v.engine_nop().then_inc(s_nm, 1)
                    v.wait_ge(s_et, gc + 1)
                    p.vfence(v, v.reciprocal(wch[i][:], zrow[:]))
                    p.vfence(v, v.tensor_scalar(tmp16[:], pL[pos % 2], negm[:], 1e8,
                                                ALU.add, ALU.mult))
                    p.vfence(v, v.tensor_scalar(ind[:], tmp16[:], 1.0, 0.0,
                                                ALU.add, ALU.max))
                    p.vfence(v, v.tensor_mul(val[:], ind[:], p.iota16[:]))
                    p.vfence(v, v.reduce_max(vmn[:], val[:], AX.X))
                    p.vfence(v, v.tensor_scalar(over[:], vmn[:], -1.0, 1.0,
                                                ALU.mult, ALU.add))
                    p.vfence(v, v.tensor_scalar(oh[i][:], val[:], over[:], 0.0,
                                                ALU.add, ALU.max))
                    p.vfence(v, v.tensor_scalar(idxv[i][:], vmn[:], -1.0, 1024.0,
                                                ALU.mult, ALU.add))
                    v.engine_nop().then_inc(s_oh, 1)
                for pos in range(33):
                    i = max(0, pos - 1)
                    gc = 33 * l + pos
                    v.wait_ge(s_oh, 33 * (l + 1))
                    if pos <= 1:
                        p.vfence(v, v.memset(carry[:], 0.0))
                    else:
                        v.wait_ge(s_tot, gc)
                        p.vfence(v, v.tensor_add(carry[:], carry[:], ptot))
                    v.engine_nop().then_inc(s_cw, 1)
                    v.wait_ge(s_cum, gc + 1)
                    p.vfence(v, v.tensor_mul(tmp16[:], pcum, oh[i][:]))
                    p.vfence(v, v.reduce_sum(slotp[:], tmp16[:], AX.X))
                    p.vfence(v, v.tensor_scalar(destf[:], idxv[i][:], float(CAP), slotp[:],
                                                ALU.mult, ALU.add))
                    p.vfence(v, v.tensor_scalar(over[:], slotp[:], -(float(CAP) + 0.5), 0.0,
                                                ALU.add, ALU.max))
                    p.vfence(v, v.tensor_scalar(over[:], over[:], 4e4, -1.0,
                                                ALU.mult, ALU.add))
                    p.vfence(v, v.tensor_add(destf[:], destf[:], over[:]))
                    p.vfence(v, v.tensor_copy(desti[i][:], destf[:]))
                    v.engine_nop().then_inc(s_di, 1)

        if l == NL - 1 and p.tap_desti is not None:
            with nc.Block() as blkd:
                @blkd.sync
                def _(sync):
                    n = 0
                    for i2 in range(32):
                        sync.dma_start(p.tap_desti[i2].rearrange("(x y) -> x y", y=1),
                                       desti[i2][:]).then_inc(s_rt, 16)
                        n += 16
                        if p.tap_oh is not None:
                            sync.dma_start(p.tap_oh[i2], oh[i2][:]).then_inc(s_rt, 16)
                            n += 16
                        if p.tap_route is not None:
                            with nc.allow_non_contiguous_dma(reason="debug tap"):
                                sync.dma_start(p.tap_route[i2][:, 0:1], idxv[i2][:]
                                               ).then_inc(s_rt, 16)
                            n += 16
                    if p.tap_stats is not None:
                        slabs = [p.iota16[:], ind[:], val[:], tmp16[:], et[:]]
                        cols = [negm[:], vmn[:], over[:], zrow[:]]
                        for j2, t2 in enumerate(slabs):
                            sync.dma_start(p.tap_stats[j2], t2).then_inc(s_rt, 16)
                            n += 16
                        with nc.allow_non_contiguous_dma(reason="debug tap"):
                            for j2, t2 in enumerate(cols):
                                sync.dma_start(p.tap_stats[5][:, j2:j2 + 1], t2
                                               ).then_inc(s_rt, 16)
                                n += 16
                    sync.wait_ge(s_rt, n)
        if "b" not in P4_PARTS:
            continue
        with nc.Block() as blk:
            @blk.sync
            def _(sync, l=l):
                for e in range(E):
                    u = 16 * l + e
                    for cs in range(4):
                        if u >= 2:
                            sync.wait_ge(s_sctp[u % 2], 16 * 4 * ((u - 2) // 2 + 1))
                        sync.dma_start(gi[e % 2][cs][:],
                                       p.glist[e, cs * 128:(cs + 1) * 128]
                                       .rearrange("(x y) -> x y", y=1)).then_inc(s_gl, 16)
                        sync.dma_start(wv[e % 2][cs][:],
                                       p.wlist[e, cs * 128:(cs + 1) * 128]
                                       .rearrange("(x y) -> x y", y=1)).then_inc(s_gl, 16)
                    if l == 0:
                        sync.wait_ge(p.s_wcast, 16 * 3 * (e + 1))
                    for hf in range(2):
                        hidx = 2 * u + hf
                        if hidx >= 2:
                            sync.wait_ge(s_h1, (hidx - 1) * 16)
                        for c in range(8):
                            sync.dma_start(w1s[hidx % 2][c][:],
                                           p.w1b[e, hf, c * 128:(c + 1) * 128, :]
                                           ).then_inc(s_w1p[hidx % 2], 16)
                    for ff in range(32):
                        if u >= 1:
                            sync.wait_ge(s_h2, (u - 1) * 64 + 32 + ff + 1)
                        sync.dma_start(w2s[ff][:],
                                       p.w2b[e, ff * 128:(ff + 1) * 128, :]
                                       ).then_inc(s_w2, 16)

            @blk.gpsimd
            def _(gp, l=l, xsrc=xsrc, xdst=xdst):
                bc_t = gp.to_reg(T - 1)

                def emit_gathers(e):
                    u = 16 * l + e
                    for cs in range(4):
                        gp.wait_ge(s_gl, 16 * (8 * u + 2 * cs + 1))
                        if u >= 1:
                            gp.wait_ge(s_xtr, 4 * (u - 1) + cs + 1)
                        gp.indirect_dma_start(
                            xetm[cs][:], None, xsrc,
                            IndirectOffsetOnAxis(gi[e % 2][cs][:], 0),
                            bounds_check=bc_t, oob_is_err=False).then_inc(s_gac[cs], 16)

                def emit_scatters(e):
                    u = 16 * l + e
                    for cs in range(4):
                        gp.wait_ge(s_oev, 8 * u + 2 * (cs + 1))
                        gp.indirect_dma_start(
                            xdst, IndirectOffsetOnAxis(gi[e % 2][cs][:], 0),
                            oute[cs][:], None,
                            bounds_check=bc_t, oob_is_err=False).then_inc(s_sctp[u % 2], 16)

                emit_gathers(0)
                for e in range(1, E):
                    emit_gathers(e)
                    emit_scatters(e - 1)
                emit_scatters(E - 1)
                gp.wait_ge(s_sctp[0], 16 * 4 * 8 * (l + 1))
                gp.wait_ge(s_sctp[1], 16 * 4 * 8 * (l + 1))
                if p.tap_xt[l] is not None:
                    gp.dma_start(p.tap_xt[l], xdst).then_inc(s_tap, 16)
                    gp.wait_ge(s_tap, 16 * (l + 1))

            @blk.tensor
            def _(t, l=l):
                def emit_transposes(e):
                    u = 16 * l + e
                    for cs in range(4):
                        t.wait_ge(s_gac[cs], 16 * (u + 1))
                        if 4 * u + cs >= 2:
                            t.wait_ge(s_xev, 4 * u + cs - 1)
                        for c in range(8):
                            ins = t.transpose(bb[cs % 2][:, c * 128:(c + 1) * 128],
                                              xetm[cs][:, c * 128:(c + 1) * 128],
                                              p.ident[:])
                        ins.then_inc(s_xtr, 1)

                for e in range(E):
                    u = 16 * l + e
                    if e == 0:
                        emit_transposes(0)
                    t.wait_ge(s_xev, 4 * u + 4)
                    for ff in range(32):
                        fidx = 32 * u + ff
                        hidx = 2 * u + ff // 16
                        t.wait_ge(s_w1p[hidx % 2], 16 * 8 * (hidx // 2 + 1))
                        if fidx >= 2:
                            t.wait_ge(s_sil, fidx - 1)
                        for c in range(8):
                            ins = t.matmul(pH[ff % 2],
                                           w1s[hidx % 2][c][:, (ff % 16) * 128:(ff % 16) * 128 + 128],
                                           xefm[:, c * CAP:(c + 1) * CAP],
                                           start=(c == 0), stop=(c == 7))
                        ins.then_inc(s_h1, 1)
                    if e + 1 < E:
                        emit_transposes(e + 1)
                    for ch in range(2):
                        for ff in range(32):
                            t.wait_ge(s_sil, 32 * u + ff + 1)
                            if ch == 0 and ff == 0:
                                t.wait_ge(s_w2, 16 * 32 * (u + 1))
                            if ch == 1 and ff == 0:
                                t.wait_ge(s_oev, 8 * u + 4)
                            for cs in range(2):
                                for dh in range(2):
                                    ins = t.matmul(
                                        pO[2 * cs + dh],
                                        silu[ff][:, (ch * 2 + cs) * 128:(ch * 2 + cs) * 128 + 128],
                                        w2s[ff][:, dh * 512:(dh + 1) * 512],
                                        start=(ff == 0), stop=(ff == 31))
                            ins.then_inc(s_h2, 1)

            @blk.scalar
            def _(s, l=l):
                for e in range(E):
                    u = 16 * l + e
                    for ff in range(32):
                        fidx = 32 * u + ff
                        s.wait_ge(s_h1, fidx + 1)
                        if u >= 1:
                            s.wait_ge(s_h2, (u - 1) * 64 + 32 + ff + 1)
                        p.afence(s, s.activation(silu[ff][:], pH[ff % 2], AF.Silu))
                        s.nop().then_inc(s_sil, 1)

            @blk.vector
            def _(v, l=l):
                def emit_evacs(e):
                    u = 16 * l + e
                    for cs in range(4):
                        v.wait_ge(s_xtr, 4 * u + cs + 1)
                        if u >= 1 and cs == 0:
                            v.wait_ge(s_h1, u * 32)
                        p.vfence(v, v.tensor_copy(xefm3[:, :, cs * 128:cs * 128 + 128],
                                                   bb[cs % 2][:, 0:1024]
                                                   .rearrange("p (c f) -> p c f", f=128)))
                        v.engine_nop().then_inc(s_xev, 1)

                emit_evacs(0)
                for e in range(E):
                    u = 16 * l + e
                    if e + 1 < E:
                        emit_evacs(e + 1)
                    for ch in range(2):
                        v.wait_ge(s_h2, 64 * u + 32 * (ch + 1))
                        for cs in range(2):
                            sc = ch * 2 + cs
                            if u >= 1:
                                v.wait_ge(s_sctp[(u - 1) % 2], 16 * 4 * ((u - 1) // 2 + 1))
                            for dh in range(2):
                                p.vfence(v, v.tensor_scalar(
                                    oute[sc][:, dh * 512:(dh + 1) * 512],
                                    pO[2 * cs + dh], wv[e % 2][sc][:], None, ALU.mult))
                                v.engine_nop().then_inc(s_oev, 1)
    pctx.close()


def build_p5(p):
    nc, sem = p.nc, p.sem
    pctx = ExitStack()
    sb = lambda name, shape, dt: pctx.enter_context(nc.sbuf_tensor(name, shape, dt))
    ps = lambda name, shape, dt=F32: pctx.enter_context(nc.psum_tensor(name, shape, dt))

    xld = [[sb(f"p5_x{i}_{c}", [128, 512], F32) for c in range(8)] for i in range(2)]
    ald = [[sb(f"p5_a{i}_{c}", [128, 512], F32) for c in range(8)] for i in range(2)]
    fst = [sb(f"p5_f{i}", [128, C], BF16) for i in range(2)]
    ob = [sb(f"p5_o{i}", [128, 512], F32) for i in range(8)]
    pF = [h[:] for h in [ps(f"p5_p{c}", [128, 512], BF16) for c in range(8)]]

    s_ld = sem("p5_ld")
    s_fsp = [sem("p5_fsa"), sem("p5_fsb")]
    s_tr = sem("p5_tr")
    s_add = sem("p5_add")
    s_st = sem("p5_st")

    with nc.Block() as blk:
        @blk.sync
        def _(sync):
            for bq in range(8):
                b, qg = bq // 4, bq % 4
                if bq >= 2:
                    sync.wait_ge(s_add, 8 * (bq - 1))
                for c in range(8):
                    sync.dma_start(xld[bq % 2][c][:],
                                   p.x_fm[b, c * 128:(c + 1) * 128,
                                          qg * 512:(qg + 1) * 512]).then_inc(s_ld, 16)
                    sync.dma_start(ald[bq % 2][c][:],
                                   p.attnproj[b, c * 128:(c + 1) * 128,
                                              qg * 512:(qg + 1) * 512]).then_inc(s_ld, 16)

        @blk.gpsimd
        def _(gp):
            for bq in range(8):
                for sub in range(4):
                    fi = bq * 4 + sub
                    if fi >= 2:
                        gp.wait_ge(s_tr, fi - 1)
                    row0 = (bq // 4) * S + (bq % 4) * 512 + sub * 128
                    gp.dma_start(fst[fi % 2][:], p.xt_tm[NL - 1][row0:row0 + 128, :]
                                 ).then_inc(s_fsp[fi % 2], 16)

        @blk.tensor
        def _(t):
            for bq in range(8):
                if bq >= 1:
                    t.wait_ge(s_add, 8 * bq)
                for sub in range(4):
                    fi = bq * 4 + sub
                    t.wait_ge(s_fsp[fi % 2], 16 * (fi // 2 + 1))
                    for c in range(8):
                        ins = t.transpose(pF[c][:, sub * 128:(sub + 1) * 128] if False else pF[c].tensor[0:128, sub * 128:(sub + 1) * 128],
                                          fst[fi % 2][:, c * 128:(c + 1) * 128],
                                          p.ident[:])
                    ins.then_inc(s_tr, 1)

        @blk.vector
        def _(v):
            for bq in range(8):
                v.wait_ge(s_ld, 16 * 16 * (bq + 1))
                v.wait_ge(s_tr, 4 * (bq + 1))
                for c in range(8):
                    m = bq * 8 + c
                    if m >= 8:
                        v.wait_ge(s_st, 16 * (m - 7))
                    v.tensor_add(ob[m % 8][:], xld[bq % 2][c][:], ald[bq % 2][c][:])
                    p.vfence(v, v.tensor_add(ob[m % 8][:], ob[m % 8][:], pF[c]))
                    v.engine_nop().then_inc(s_add, 1)

        @blk.scalar
        def _(sc):
            for bq in range(8):
                b, qg = bq // 4, bq % 4
                for c in range(8):
                    m = bq * 8 + c
                    sc.wait_ge(s_add, m + 1)
                    sc.dma_start(p.out_fm[b, c * 128:(c + 1) * 128,
                                          qg * 512:(qg + 1) * 512],
                                 ob[m % 8][:]).then_inc(s_st, 16)
            sc.wait_ge(s_st, 16 * 64)
    pctx.close()


def kernel(**inputs):
    in_maps = _host_prep(inputs)
    nc = build()
    res = run_bass_kernel_spmd(nc, in_maps, core_ids=list(range(NCORES)))
    outs = []
    for c in range(NCORES):
        o = np.asarray(res.results[c]["out_fm"], np.float32)
        outs.append(o.transpose(0, 2, 1))
    return np.ascontiguousarray(np.concatenate(outs, 0))
